# version 12
# baseline (speedup 1.0000x reference)
import numpy as np
from contextlib import ExitStack

import concourse.bass as bass
import concourse.mybir as mybir
from concourse.bass_utils import run_bass_kernel_spmd

F32 = mybir.dt.float32
BF16 = mybir.dt.bfloat16
AF = mybir.ActivationFunctionType
ALU = mybir.AluOpType
AX = mybir.AxisListType

Bq, L, D = 4, 8192, 1024
DI = 2048
CH = DI // 2
NJ = CH // 128
N16 = 16
R = 64
RR = 96
TH = L // 2
E, F = 8, 4096
EPS_A = 1e-5
EPS_F = 1e-6

CA = 2048
NCA = L // CA
QB = 512
NCB = L // QB
TS = 256
NTS = TH // TS
NFB = 4

PAIRS = [[0, 1], [2, 3], [4, 5], [6, 7]]
ENGINES = ("sync", "gpsimd", "vector", "scalar", "tensor")


class Sem:
    def __init__(self, nc, es, name):
        self.h = es.enter_context(nc.semaphore(name))
        self.n = 0


class Phase:
    def __init__(self):
        self.ops = {e: [] for e in ENGINES}

    def add(self, eng, fn, waits=(), incs=()):
        w = []
        for x in waits:
            if isinstance(x, tuple):
                w.append((x[0].h, x[1]))
            else:
                w.append((x.h, x.n))
        ii = []
        assert len(list(incs)) <= 1, "one sync update per instruction"
        for (s, k) in incs:
            if k is None:
                s.n += 1
                ii.append((s.h, None))
            else:
                s.n += k
                ii.append((s.h, k))
        self.ops[eng].append((fn, w, ii))

    def emit(self, nc):
        with nc.Block() as blk:
            for eng in ENGINES:
                ops = self.ops[eng]
                if not ops:
                    continue

                def mk(ops):
                    def body(e):
                        for (fn, w, ii) in ops:
                            for (h, c) in w:
                                if c > 0:
                                    e.wait_ge(h, c)
                            instr = fn(e)
                            for (h, k) in ii:
                                if k is None:
                                    instr.then_inc(h)
                                else:
                                    instr.then_inc(h, k)
                    return body

                getattr(blk, eng)(mk(ops))


def build_program(debug=False):
    nc = bass.Bass("TRN2", target_bir_lowering=False)

    def din(name, shape, dt=F32):
        return nc.dram_tensor(name, shape, dt, kind="ExternalInput").ap()

    x_full = din("x_full", [L, D])
    x_half = din("x_half", [TH, D])
    eye_f = din("eye_f", [128, 128])
    constv_d = din("constv", [128, 8])
    norm_w = din("norm_w", [D])
    wxt_d = din("wxt_w", [D, CH])
    wzt_d = din("wzt_w", [D, CH])
    bx_d = din("bx", [128, NJ])
    bz_d = din("bz", [128, NJ])
    convw_d = din("convw", [128, NJ * 4])
    convb_d = din("convb", [128, NJ])
    wxpt_d = din("wxpt_w", [CH, RR])
    wdtt_d = din("wdtt_w", [R, CH])
    bdt_d = din("bdt", [128, NJ])
    alog_d = din("alog", [128, NJ * N16])
    dp_d = din("dp", [128, NJ])
    woutt_d = din("woutt_w", [CH, D])
    bout_d = din("bout", [D])
    wg_d = din("wg", [128, 8 * E])
    gamma_d = din("gamma", [128, E * 8])
    beta_d = din("beta", [128, E * 8])
    w1_d = din("w1", [E, D, F])
    b1_d = din("b1", [128, E * (F // 128)])
    w2_d = din("w2", [E, F, D])
    b2_d = din("b2", [E, D])

    out_d = nc.dram_tensor("out", [TH, D], F32, kind="ExternalOutput").ap()

    dbg_copies = []

    def scr(name, shape, dt=F32, dbg=False):
        t = nc.dram_tensor(name, shape, dt, kind="Internal").ap()
        if debug and dbg:
            o = nc.dram_tensor("dbg_" + name, shape, dt, kind="ExternalOutput").ap()
            dbg_copies.append((o, t))
        return t

    u_dram = scr("u_dram", [L, D], dbg=True)
    xb2_dram = scr("xb2_dram", [CH, L], BF16, dbg=True)
    zs_dram = scr("zs_dram", [CH, L], BF16)
    xdbl_in = scr("xdbl_in", [RR, L])
    xdbl_out = scr("xdbl_out", [RR, L], dbg=True)
    rs_in = scr("rs_in", [L, D])
    rs_out = scr("rs_out", [TH, D], dbg=True)
    x1_dram = scr("x1_dram", [TH, D], dbg=True)
    v_dram = scr("v_dram", [TH, D])
    ymoe_dram = scr("ymoe_dram", [TH, D], dbg=True)

    es = ExitStack()
    with es:
        sd = Sem(nc, es, "sd")
        gd = Sem(nc, es, "gd")
        sv = Sem(nc, es, "sv")
        ss = Sem(nc, es, "ss")
        st = Sem(nc, es, "st")
        c1 = Sem(nc, es, "c1")
        c2 = Sem(nc, es, "c2")
        ob = Sem(nc, es, "ob")
        vt = Sem(nc, es, "vt")
        sce = Sem(nc, es, "sce")

        ALLS = (sd, gd, sv, ss, st)

        def bw():
            return [sd, gd, sv, ss, st]

        def sb(stk, name, shape, dt=F32):
            return stk.enter_context(nc.sbuf_tensor(name, shape, dt))

        def ps(stk, name, shape, dt=F32):
            return stk.enter_context(nc.psum_tensor(name, shape, dt))

        eyeb = sb(es, "eyeb", [128, 128], BF16)
        eyef = sb(es, "eyef", [128, 128], F32)
        onesb = sb(es, "onesb", [33, 128], BF16)
        onesf = sb(es, "onesf", [1, 128], F32)
        nwrow = sb(es, "nwrow", [1, D], F32)
        boutrow = sb(es, "boutrow", [1, D], F32)
        nw_rep = sb(es, "nw_rep", [128, D], F32)
        bout_rep = sb(es, "bout_rep", [128, D], F32)
        bx_sb = sb(es, "bx_sb", [128, NJ], F32)
        bz_sb = sb(es, "bz_sb", [128, NJ], F32)
        convb_sb = sb(es, "convb_sb", [128, NJ], F32)
        bdt_sb = sb(es, "bdt_sb", [128, NJ], F32)
        dp_sb = sb(es, "dp_sb", [128, NJ], F32)
        convw_sb = sb(es, "convw_sb", [128, NJ * 4], F32)
        a_sb = sb(es, "a_sb", [128, NJ * N16], F32)
        hstate = sb(es, "hstate", [128, NJ * N16], F32)
        cv = sb(es, "cv", [128, 8], F32)
        stat = sb(es, "stat", [128, 16], F32)

        def emit_stats(ph, srcbuf, sq, mcol, vcol, eps_ap, rms=False):
            if not rms:
                ph.add("vector",
                       lambda e: e.reduce_sum(stat[:, mcol:mcol + 1], srcbuf[:],
                                              axis=AX.X, negate=True),
                       incs=[(vt, 1)])
            ph.add("scalar",
                   lambda e: e.activation(sq[:], srcbuf[:], AF.Square,
                                          accum_out=stat[:, vcol:vcol + 1]),
                   waits=[(vt, vt.n)] if not rms else [sv], incs=[(sce, 1)])
            if rms:
                ph.add("scalar",
                       lambda e: e.activation(stat[:, vcol:vcol + 1],
                                              stat[:, vcol:vcol + 1], AF.Ln,
                                              scale=cv[:, 5:6], bias=eps_ap),
                       waits=[(sce, sce.n)], incs=[(sce, 1)])
                ph.add("scalar",
                       lambda e: e.activation(stat[:, vcol:vcol + 1],
                                              stat[:, vcol:vcol + 1], AF.Exp,
                                              scale=cv[:, 4:5]),
                       waits=[(sce, sce.n)], incs=[(sce, 1)])
                return
            ph.add("vector",
                   lambda e: e.tensor_scalar_mul(stat[:, mcol:mcol + 1],
                                                 stat[:, mcol:mcol + 1], 1.0 / D),
                   waits=[(vt, vt.n)], incs=[(vt, 1)])
            ph.add("vector",
                   lambda e: e.tensor_scalar_mul(stat[:, 14:15],
                                                 stat[:, vcol:vcol + 1], 1.0 / D),
                   waits=[(sce, sce.n)], incs=[(vt, 1)])
            ph.add("vector",
                   lambda e: e.tensor_mul(stat[:, 15:16], stat[:, mcol:mcol + 1],
                                          stat[:, mcol:mcol + 1]),
                   waits=[(vt, vt.n)], incs=[(vt, 1)])
            ph.add("vector",
                   lambda e: e.tensor_sub(stat[:, vcol:vcol + 1], stat[:, 14:15],
                                          stat[:, 15:16]),
                   waits=[(vt, vt.n)], incs=[(vt, 1)])
            ph.add("scalar",
                   lambda e: e.activation(stat[:, vcol:vcol + 1],
                                          stat[:, vcol:vcol + 1], AF.Ln, bias=eps_ap),
                   waits=[(vt, vt.n)], incs=[(sce, 1)])
            ph.add("scalar",
                   lambda e: e.activation(stat[:, vcol:vcol + 1],
                                          stat[:, vcol:vcol + 1], AF.Exp,
                                          scale=cv[:, 4:5]),
                   waits=[(sce, sce.n)], incs=[(sce, 1)])

        ph = Phase()

        def ld(dst, src):
            ph.add("sync", lambda e, d=dst, s=src: e.dma_start(out=d, in_=s),
                   incs=[(sd, 16)])

        ld(eyef[:], eye_f)
        ld(cv[:], constv_d)
        ld(nwrow[:], norm_w.rearrange("(a d) -> a d", a=1))
        ld(boutrow[:], bout_d.rearrange("(a d) -> a d", a=1))
        ld(bx_sb[:], bx_d)
        ld(bz_sb[:], bz_d)
        ld(convb_sb[:], convb_d)
        ld(bdt_sb[:], bdt_d)
        ld(dp_sb[:], dp_d)
        ld(convw_sb[:], convw_d)
        ld(a_sb[:], alog_d)
        ph.add("gpsimd", lambda e: e.dma_start(out=eyeb[:], in_=eye_f),
               incs=[(gd, 16)])
        ph.add("gpsimd", lambda e: e.memset(onesb[:], 1.0), incs=[(gd, 1)])
        ph.add("gpsimd", lambda e: e.memset(onesf[:], 1.0), incs=[(gd, 1)])
        ph.add("scalar", lambda e: e.activation(a_sb[:], a_sb[:], AF.Exp),
               waits=[sd], incs=[(ss, 1)])
        ph.add("vector", lambda e: e.tensor_scalar_mul(a_sb[:], a_sb[:], -1.0),
               waits=[ss], incs=[(sv, 1)])
        ph.add("vector", lambda e: e.memset(hstate[:], 0.0), incs=[(sv, 1)])
        es0 = ExitStack()
        pbc = ps(es0, "pbc", [128, 1024], F32)
        for h in range(2):
            ph.add("tensor",
                   lambda e, h=h: e.matmul(pbc[:, h * 512:(h + 1) * 512], onesf[:],
                                           nwrow[0:1, h * 512:(h + 1) * 512],
                                           start=True, stop=True),
                   waits=([sd, gd] if h == 0 else []), incs=[(st, 1)])
        ph.add("scalar", lambda e: e.copy(nw_rep[:], pbc[:]), waits=[st],
               incs=[(ss, 1)])
        for h in range(2):
            ph.add("tensor",
                   lambda e, h=h: e.matmul(pbc[:, h * 512:(h + 1) * 512], onesf[:],
                                           boutrow[0:1, h * 512:(h + 1) * 512],
                                           start=True, stop=True),
                   waits=([ss] if h == 0 else []), incs=[(st, 1)])
        ph.add("scalar", lambda e: e.copy(bout_rep[:], pbc[:]), waits=[st],
               incs=[(ss, 1)])
        ph.emit(nc)
        es0.close()

        esA = ExitStack()
        with esA:
            wxt = sb(esA, "wxt", [128, 8 * CH], BF16)
            wzt = sb(esA, "wzt", [128, 8 * CH], BF16)
            wxpt = sb(esA, "wxpt", [128, NJ * RR], BF16)
            xnT = sb(esA, "xnT", [128, 8 * CA], BF16)
            xb2k = sb(esA, "xb2k", [128, NJ * CA], BF16)
            zstile = sb(esA, "zstile", [128, CA], BF16)
            xbw = sb(esA, "xbw", [128, CA + 3], F32)
            xcw = sb(esA, "xcw", [128, CA], F32)
            xbtail = sb(esA, "xbtail", [128, NJ * 3], F32)
            xdblp = sb(esA, "xdblp", [96, L], F32)
            xtile = [sb(esA, f"xtile{i}", [128, D], F32) for i in range(2)]
            sqscr = sb(esA, "sqscrA", [128, D], F32)
            utile = sb(esA, "utile", [128, D], F32)
            xntile = sb(esA, "xntile", [128, D], BF16)
            ptr = ps(esA, "ptr", [128, 1024], BF16)
            pxb = ps(esA, "pxb", [128, 512], F32)
            pz = ps(esA, "pz", [128, 512], F32)
            pxd = ps(esA, "pxd", [96, 512], F32)

            ph = Phase()
            for a in range(8):
                ph.add("gpsimd",
                       lambda e, a=a: e.dma_start(
                           out=wxt[:, a * CH:(a + 1) * CH],
                           in_=wxt_d[a * 128:(a + 1) * 128, :]),
                       incs=[(gd, 16)])
                ph.add("gpsimd",
                       lambda e, a=a: e.dma_start(
                           out=wzt[:, a * CH:(a + 1) * CH],
                           in_=wzt_d[a * 128:(a + 1) * 128, :]),
                       incs=[(gd, 16)])
            for a in range(NJ):
                ph.add("gpsimd",
                       lambda e, a=a: e.dma_start(
                           out=wxpt[:, a * RR:(a + 1) * RR],
                           in_=wxpt_d[a * 128:(a + 1) * 128, :]),
                       incs=[(gd, 16)])
            gd_wA = gd.n

            for ch in range(NCA):
                t0 = ch * CA
                for tt in range(16):
                    xt = xtile[tt % 2]
                    row = t0 + tt * 128
                    ph.add("sync",
                           lambda e, xt=xt, row=row: e.dma_start(
                               out=xt[:], in_=x_full[row:row + 128, :]),
                           waits=[sv, ss], incs=[(sd, 16)])
                    ph.add("vector", lambda e: e.engine_nop(), waits=[sd])
                    emit_stats(ph, xt, sqscr, 0, 4, cv[:, 0:1])
                    ph.add("vector",
                           lambda e, xt=xt: e.tensor_scalar(utile[:], xt[:], stat[:, 0:1],
                                                            stat[:, 4:5], ALU.add, ALU.mult),
                           waits=[sd, (vt, vt.n), (sce, sce.n)], incs=[(sv, 1)])
                    ph.add("sync",
                           lambda e, row=row: e.dma_start(out=u_dram[row:row + 128, :],
                                                          in_=utile[:]),
                           waits=[sv], incs=[(sd, 16)])
                    emit_stats(ph, utile, sqscr, 0, 6, cv[:, 0:1], rms=True)
                    ph.add("vector",
                           lambda e: e.scalar_tensor_tensor(xntile[:], utile[:], stat[:, 6:7],
                                                            nw_rep[:], ALU.mult, ALU.mult),
                           waits=[st, (sce, sce.n)], incs=[(sv, 1)])
                    for a in range(8):
                        ph.add("tensor",
                               lambda e, a=a: e.transpose(ptr[:, a * 128:(a + 1) * 128],
                                                          xntile[:, a * 128:(a + 1) * 128],
                                                          eyeb[:]),
                               waits=([sv, ss] if a == 0 else []),
                               incs=([(st, 1)] if a == 7 else []))
                    ph.add("scalar",
                           lambda e, tt=tt: e.copy(
                               xnT[:].rearrange("p (a t) -> p a t", a=8)[:, :, tt * 128:(tt + 1) * 128],
                               ptr[:].rearrange("p (a t) -> p a t", a=8)),
                           waits=[st], incs=[(ss, 1)])

                for j in range(NJ):
                    if ch == 0:
                        ph.add("vector", lambda e: e.memset(xbw[:, 0:3], 0.0),
                               waits=[ss, st], incs=[(vt, 1)])
                    else:
                        ph.add("vector",
                               lambda e, j=j: e.tensor_copy(xbw[:, 0:3],
                                                            xbtail[:, j * 3:(j + 1) * 3]),
                               waits=[ss, st], incs=[(vt, 1)])
                    for ts4 in range(4):
                        for a in range(8):
                            ph.add("tensor",
                                   lambda e, a=a, j=j, ts4=ts4: e.matmul(
                                       pxb[:],
                                       wxt[:, a * CH + j * 128: a * CH + (j + 1) * 128],
                                       xnT[:, a * CA + ts4 * 512: a * CA + (ts4 + 1) * 512],
                                       start=(a == 0), stop=(a == 7)),
                                   waits=([ss, sv, (gd, gd_wA)] if a == 0 else []),
                                   incs=([(st, 1)] if a == 7 else []))
                        ph.add("scalar",
                               lambda e, j=j, ts4=ts4: e.activation(
                                   xbw[:, 3 + ts4 * 512: 3 + (ts4 + 1) * 512], pxb[:],
                                   AF.Identity, bias=bx_sb[:, j:j + 1]),
                               waits=[st, sv], incs=[(ss, 1)])
                        for a in range(8):
                            ph.add("tensor",
                                   lambda e, a=a, j=j, ts4=ts4: e.matmul(
                                       pz[:],
                                       wzt[:, a * CH + j * 128: a * CH + (j + 1) * 128],
                                       xnT[:, a * CA + ts4 * 512: a * CA + (ts4 + 1) * 512],
                                       start=(a == 0), stop=(a == 7)),
                                   waits=([ss] if a == 0 else []),
                                   incs=([(st, 1)] if a == 7 else []))
                        ph.add("scalar",
                               lambda e, ts4=ts4, j=j: e.activation(
                                   zstile[:, ts4 * 512:(ts4 + 1) * 512], pz[:], AF.Silu,
                                   bias=bz_sb[:, j:j + 1]),
                               waits=[st, sd], incs=[(ss, 1)])
                    ph.add("sync",
                           lambda e, j=j, t0=t0: e.dma_start(
                               out=zs_dram[j * 128:(j + 1) * 128, t0:t0 + CA],
                               in_=zstile[:]),
                           waits=[ss], incs=[(sd, 16)])
                    ph.add("vector",
                           lambda e, j=j: e.tensor_copy(xbtail[:, j * 3:(j + 1) * 3],
                                                        xbw[:, CA:CA + 3]),
                           waits=[ss], incs=[(sv, 1)])
                    ph.add("vector",
                           lambda e, j=j: e.tensor_scalar_mul(
                               xcw[:], xbw[:, 0:CA],
                               convw_sb[:, j * 4:j * 4 + 1]),
                           waits=[(vt, vt.n)], incs=[(sv, 1)])
                    for k in range(1, 4):
                        ph.add("vector",
                               lambda e, j=j, k=k: e.scalar_tensor_tensor(
                                   xcw[:], xbw[:, k:k + CA],
                                   convw_sb[:, j * 4 + k:j * 4 + k + 1],
                                   xcw[:], ALU.mult, ALU.add),
                               incs=([(sv, 1)] if k == 3 else []))
                    ph.add("scalar",
                           lambda e, j=j: e.activation(
                               xb2k[:, j * CA:(j + 1) * CA], xcw[:], AF.Silu,
                               bias=convb_sb[:, j:j + 1]),
                           waits=[sv, st, sd], incs=[(ss, 1)])
                    ph.add("sync",
                           lambda e, j=j, t0=t0: e.dma_start(
                               out=xb2_dram[j * 128:(j + 1) * 128, t0:t0 + CA],
                               in_=xb2k[:, j * CA:(j + 1) * CA]),
                           waits=[ss], incs=[(sd, 16)])

                for ts4 in range(4):
                    for j in range(NJ):
                        ph.add("tensor",
                               lambda e, j=j, ts4=ts4: e.matmul(
                                   pxd[:],
                                   wxpt[:, j * RR:(j + 1) * RR],
                                   xb2k[:, j * CA + ts4 * 512: j * CA + (ts4 + 1) * 512],
                                   start=(j == 0), stop=(j == NJ - 1)),
                               waits=([ss, sv] if j == 0 else []),
                               incs=([(st, 1)] if j == NJ - 1 else []))
                    ph.add("scalar",
                           lambda e, ts4=ts4, t0=t0: e.copy(
                               xdblp[0:96, t0 + ts4 * 512: t0 + (ts4 + 1) * 512], pxd[:]),
                           waits=[st], incs=[(ss, 1)])

            ph.add("sync", lambda e: e.dma_start(out=xdbl_in, in_=xdblp[0:96, :]),
                   waits=[ss], incs=[(sd, 16)])
            ph.emit(nc)

        ph = Phase()
        ph.add("gpsimd",
               lambda e: e.collective_compute(
                   "AllReduce", ALU.add, replica_groups=PAIRS,
                   ins=[xdbl_in.opt()], outs=[xdbl_out.opt()]),
               waits=bw(), incs=[(c1, None)])
        ph.emit(nc)

        esB = ExitStack()
        with esB:
            wdtt = sb(esB, "wdtt", [64, CH], BF16)
            woutt = sb(esB, "woutt", [128, NJ * D], BF16)
            dlowb = sb(esB, "dlowb", [64, QB], BF16)
            bc1 = sb(esB, "bc1", [33, 16 * QB], BF16)
            dt_sb = sb(esB, "dt_sb", [128, NJ * QB], F32)
            u_sb = sb(esB, "u_sb", [128, NJ * QB], F32)
            xb2c = sb(esB, "xb2c", [128, NJ * QB], BF16)
            zsc = sb(esB, "zsc", [128, NJ * QB], BF16)
            dA = sb(esB, "dA", [128, NJ * QB], F32)
            dA_b = sb(esB, "dA_b", [128, NJ * QB], F32)
            dBx = sb(esB, "dBx", [128, NJ * QB], F32)
            h_n = sb(esB, "h_n", [128, NJ * QB], F32)
            y_sb = sb(esB, "y_sb", [128, NJ * QB], F32)
            yz_bf = sb(esB, "yz_bf", [128, NJ * QB], BF16)
            uev0 = sb(esB, "uev0", [128, D], F32)
            uev = [uev0, uev0]
            outp = [sb(esB, f"outp{i}", [128, D], F32) for i in range(2)]
            tabs = h_n[:, 0:QB]
            tln = h_n[:, QB:2 * QB]
            pdt = ps(esB, "pdt", [128, 512], F32)
            pb_b = ps(esB, "pb_b", [128, 512], F32)
            pb_c = ps(esB, "pb_c", [128, 512], F32)
            pb_b2 = ps(esB, "pb_b2", [128, 512], F32)
            pb_c2 = ps(esB, "pb_c2", [128, 512], F32)
            po = ps(esB, "po", [128, 1024], F32)

            ph = Phase()
            for a in range(NJ):
                ph.add("gpsimd",
                       lambda e, a=a: e.dma_start(
                           out=woutt[:, a * D:(a + 1) * D],
                           in_=woutt_d[a * 128:(a + 1) * 128, :]),
                       incs=[(gd, 16)])
            ph.add("gpsimd", lambda e: e.dma_start(out=wdtt[:], in_=wdtt_d),
                   incs=[(gd, 16)])

            for cb in range(NCB):
                t0 = cb * QB
                ph.add("gpsimd",
                       lambda e, t0=t0: e.dma_start(out=dlowb[:],
                                                    in_=xdbl_out[0:64, t0:t0 + QB]),
                       waits=[(c1, 1), st, ss], incs=[(gd, 16)])
                ph.add("gpsimd",
                       lambda e, t0=t0: e.dma_start(
                           out=bc1[0:1, :].rearrange("q (n t) -> q n t", n=16),
                           in_=xdbl_out[64:80, t0:t0 + QB]),
                       waits=[(c1, 1), st], incs=[(gd, 16)])
                ph.add("gpsimd",
                       lambda e, t0=t0: e.dma_start(
                           out=bc1[32:33, :].rearrange("q (n t) -> q n t", n=16),
                           in_=xdbl_out[80:96, t0:t0 + QB]),
                       waits=[(c1, 1), st], incs=[(gd, 16)])
                for j in range(NJ):
                    ph.add("sync",
                           lambda e, t0=t0, j=j: e.dma_start(
                               out=xb2c[:, j * QB:(j + 1) * QB],
                               in_=xb2_dram[j * 128:(j + 1) * 128, t0:t0 + QB]),
                           waits=([sv] if j == 0 else []), incs=[(sd, 16)])
                    ph.add("sync",
                           lambda e, t0=t0, j=j: e.dma_start(
                               out=zsc[:, j * QB:(j + 1) * QB],
                               in_=zs_dram[j * 128:(j + 1) * 128, t0:t0 + QB]),
                           incs=[(sd, 16)])

                gd_cb = gd.n
                for j in range(NJ):
                    ph.add("tensor",
                           lambda e, j=j: e.matmul(pdt[:, 0:QB],
                                                   wdtt[:, j * 128:(j + 1) * 128],
                                                   dlowb[:], start=True, stop=True),
                           waits=([(gd, gd_cb), ss, sv] if j == 0 else [ss]),
                           incs=[(st, 1)])
                    ph.add("scalar",
                           lambda e, j=j: e.activation(tabs, pdt[:, 0:QB], AF.Abs,
                                                       bias=bdt_sb[:, j:j + 1]),
                           waits=[st])
                    ph.add("scalar",
                           lambda e: e.activation(tabs, tabs, AF.Exp, scale=cv[:, 3:4]))
                    ph.add("scalar",
                           lambda e: e.activation(tln, tabs, AF.Ln, bias=cv[:, 2:3]))
                    ph.add("scalar",
                           lambda e, j=j: e.activation(tabs, pdt[:, 0:QB], AF.Relu,
                                                       bias=bdt_sb[:, j:j + 1]),
                           incs=[(ss, 1)])
                    ph.add("vector",
                           lambda e, j=j: e.tensor_add(dt_sb[:, j * QB:(j + 1) * QB],
                                                       tabs, tln),
                           waits=[ss], incs=[(sv, 1)])
                    ph.add("vector",
                           lambda e, j=j: e.tensor_mul(u_sb[:, j * QB:(j + 1) * QB],
                                                       dt_sb[:, j * QB:(j + 1) * QB],
                                                       xb2c[:, j * QB:(j + 1) * QB]),
                           waits=[sd], incs=[(sv, 1)])
                sv_u = sv.n

                dA2 = [dA, dA_b]
                pbb = [pb_b, pb_b2]
                pbcc = [pb_c, pb_c2]
                yadd_done = {-2: sv_u, -1: sv_u}
                for n in range(N16):
                    par = n % 2
                    ph.add("tensor",
                           lambda e, n=n, par=par: e.matmul(
                               pbb[par][:, 0:QB], onesb[0:1, :],
                               bc1[0:1, n * QB:(n + 1) * QB],
                               start=True, stop=True),
                           waits=[(sv, yadd_done[n - 2])], incs=[(st, 1)])
                    ph.add("tensor",
                           lambda e, n=n, par=par: e.matmul(
                               pbcc[par][:, 0:QB], onesb[32:33, :],
                               bc1[32:33, n * QB:(n + 1) * QB],
                               start=True, stop=True),
                           incs=[(st, 1)])
                    bc_ok = st.n
                    for j in range(NJ):
                        ph.add("scalar",
                               lambda e, j=j, n=n, par=par: e.activation(
                                   dA2[par][:, j * QB:(j + 1) * QB],
                                   dt_sb[:, j * QB:(j + 1) * QB], AF.Exp,
                                   scale=a_sb[:, j * N16 + n: j * N16 + n + 1]),
                               waits=([(sv, yadd_done[n - 2])] if j == 0 else []),
                               incs=([(ss, 1)] if j == NJ - 1 else []))
                    ss_dA = ss.n
                    for j in range(NJ):
                        ph.add("vector",
                               lambda e, j=j, par=par: e.tensor_mul(
                                   dBx[:, j * QB:(j + 1) * QB],
                                   u_sb[:, j * QB:(j + 1) * QB],
                                   pbb[par][:, 0:QB]),
                               waits=([(st, bc_ok), (ss, ss_dA)] if j == 0 else []))
                    for j in range(NJ):
                        ph.add("vector",
                               lambda e, j=j, n=n, par=par: e.scalar_tensor_tensor(
                                   dBx[:, j * QB:j * QB + 1],
                                   dA2[par][:, j * QB:j * QB + 1],
                                   hstate[:, j * N16 + n:j * N16 + n + 1],
                                   dBx[:, j * QB:j * QB + 1], ALU.mult, ALU.add),
                               incs=([(vt, 1)] if j == NJ - 1 else []))
                    for j in range(NJ):
                        ph.add("vector",
                               lambda e, j=j, par=par: e.tensor_tensor_scan(
                                   h_n[:, j * QB:(j + 1) * QB],
                                   dA2[par][:, j * QB:(j + 1) * QB],
                                   dBx[:, j * QB:(j + 1) * QB],
                                   0.0, ALU.mult, ALU.add),
                               waits=([(vt, vt.n)] if j == 0 else []),
                               incs=([(vt, 1)] if j == NJ - 1 else []))
                    for j in range(NJ):
                        ph.add("vector",
                               lambda e, j=j, n=n: e.tensor_copy(
                                   hstate[:, j * N16 + n:j * N16 + n + 1],
                                   h_n[:, (j + 1) * QB - 1:(j + 1) * QB]),
                               waits=([(vt, vt.n)] if j == 0 else []))
                    for j in range(NJ):
                        ph.add("vector",
                               lambda e, j=j, par=par: e.tensor_mul(
                                   dA2[par][:, j * QB:(j + 1) * QB],
                                   h_n[:, j * QB:(j + 1) * QB],
                                   pbcc[par][:, 0:QB]))
                        if n == 0:
                            ph.add("vector",
                                   lambda e, j=j, par=par: e.tensor_copy(
                                       y_sb[:, j * QB:(j + 1) * QB],
                                       dA2[par][:, j * QB:(j + 1) * QB]),
                                   incs=([(sv, 1)] if j == NJ - 1 else []))
                        else:
                            ph.add("vector",
                                   lambda e, j=j, par=par: e.tensor_add(
                                       y_sb[:, j * QB:(j + 1) * QB],
                                       y_sb[:, j * QB:(j + 1) * QB],
                                       dA2[par][:, j * QB:(j + 1) * QB]),
                                   incs=([(sv, 1)] if j == NJ - 1 else []))
                    yadd_done[n] = sv.n

                for j in range(NJ):
                    ph.add("vector",
                           lambda e, j=j: e.scalar_tensor_tensor(
                               y_sb[:, j * QB:(j + 1) * QB],
                               xb2c[:, j * QB:(j + 1) * QB],
                               dp_sb[:, j:j + 1],
                               y_sb[:, j * QB:(j + 1) * QB], ALU.mult, ALU.add))
                    ph.add("vector",
                           lambda e, j=j: e.tensor_mul(yz_bf[:, j * QB:(j + 1) * QB],
                                                       y_sb[:, j * QB:(j + 1) * QB],
                                                       zsc[:, j * QB:(j + 1) * QB]),
                           incs=[(sv, 1)])

                for tsub in range(QB // 128):
                    row = t0 + tsub * 128
                    bfi = tsub % 2
                    ph.add("sync",
                           lambda e, row=row, bfi=bfi: e.dma_start(
                               out=uev[bfi][:], in_=u_dram[row:row + 128, :]),
                           waits=[sv], incs=[(sd, 16)])
                    for dh in range(2):
                        for j in range(NJ):
                            ph.add("tensor",
                                   lambda e, j=j, dh=dh, tsub=tsub: e.matmul(
                                       po[:, dh * 512:(dh + 1) * 512],
                                       yz_bf[:, j * QB + tsub * 128: j * QB + (tsub + 1) * 128],
                                       woutt[:, j * D + dh * 512: j * D + (dh + 1) * 512],
                                       start=(j == 0), stop=(j == NJ - 1)),
                                   waits=([sv] if (dh == 0 and j == 0) else []),
                                   incs=([(st, 1)] if (dh == 1 and j == NJ - 1) else []))
                    ph.add("vector",
                           lambda e, bfi=bfi: e.scalar_tensor_tensor(
                               outp[bfi][:, 0:512], uev[bfi][:, 0:512], 0.5,
                               po[:, 0:512], ALU.mult, ALU.add),
                           waits=[st, sd], incs=[(sv, 1)])
                    ph.add("vector",
                           lambda e, bfi=bfi: e.scalar_tensor_tensor(
                               outp[bfi][:, 512:1024], uev[bfi][:, 512:1024], 0.5,
                               po[:, 512:1024], ALU.mult, ALU.add),
                           incs=[(sv, 1)])
                    ph.add("sync",
                           lambda e, row=row, bfi=bfi: e.dma_start(
                               out=rs_in[row:row + 128, :], in_=outp[bfi][:]),
                           waits=[sv], incs=[(sd, 16)])
            ph.emit(nc)

        ph = Phase()
        ph.add("gpsimd",
               lambda e: e.collective_compute(
                   "ReduceScatter", ALU.add, replica_groups=PAIRS,
                   ins=[rs_in.opt()], outs=[rs_out.opt()]),
               waits=bw(), incs=[(c2, None)])
        ph.emit(nc)

        esCD = ExitStack()
        with esCD:
            xhatT = sb(esCD, "xhatT", [128, 8 * TH], BF16)
            wg_sb = sb(esCD, "wg_sb", [128, 8 * E], F32)
            gamma_sb = sb(esCD, "gamma_sb", [128, E * 8], F32)
            beta_sb = sb(esCD, "beta_sb", [128, E * 8], F32)
            b1_sb = sb(esCD, "b1_sb", [128, E * 32], F32)
            b2_sb = sb(esCD, "b2_sb", [8, D], BF16)
            cw_sb = sb(esCD, "cw_sb", [128, (TH // 128) * E], F32)
            cwT_sb = sb(esCD, "cwT_sb", [8, TH], BF16)
            sqscr = sb(esCD, "sqscrC", [128, D], F32)

            esC = ExitStack()
            with esC:
                rst = [sb(esC, f"rst{i}", [128, D], F32) for i in range(2)]
                xht = [sb(esC, f"xht{i}", [128, D], F32) for i in range(2)]
                aow = sb(esC, "aow", [128, D], F32)
                x1w = sb(esC, "x1w", [128, D], F32)
                vw = sb(esC, "vw", [128, D], F32)
                xhbf = sb(esC, "xhbf", [128, D], BF16)
                vT = sb(esC, "vT", [128, 8 * 128], F32)
                sct = sb(esC, "sct", [128, 8], F32)
                cwt = sb(esC, "cwt", [128, 8], F32)
                b2ev = sb(esC, "b2ev", [128, D], F32)
                ptrv = ps(esC, "ptrv", [128, 1024], F32)
                ptrb = ps(esC, "ptrb", [128, 1024], BF16)
                psc = ps(esC, "psc", [128, 512], F32)
                pb2 = ps(esC, "pb2", [128, 1024], F32)

                ph = Phase()
                for dst, srcd in ((wg_sb, wg_d), (gamma_sb, gamma_d),
                                  (beta_sb, beta_d), (b1_sb, b1_d)):
                    ph.add("sync",
                           lambda e, d=dst, s=srcd: e.dma_start(out=d[:], in_=s),
                           incs=[(sd, 16)])
                ph.add("gpsimd", lambda e: e.dma_start(out=b2_sb[:], in_=b2_d),
                       incs=[(gd, 16)])
                gd_b2 = gd.n

                for tt in range(TH // 128):
                    row = tt * 128
                    bfi = tt % 2
                    ph.add("sync",
                           lambda e, row=row, bfi=bfi: e.dma_start(
                               out=rst[bfi][:], in_=rs_out[row:row + 128, :]),
                           waits=[(c2, 1), sv, ss], incs=[(sd, 16)])
                    ph.add("sync",
                           lambda e, row=row, bfi=bfi: e.dma_start(
                               out=xht[bfi][:], in_=x_half[row:row + 128, :]),
                           incs=[(sd, 16)])
                    ph.add("vector",
                           lambda e, bfi=bfi: e.tensor_add(aow[:], rst[bfi][:], bout_rep[:]),
                           waits=[sd, ss], incs=[(sv, 1)])
                    emit_stats(ph, aow, sqscr, 0, 1, cv[:, 0:1])
                    ph.add("vector",
                           lambda e: e.tensor_scalar(aow[:], aow[:], stat[:, 0:1],
                                                     stat[:, 1:2], ALU.add, ALU.mult),
                           waits=[(vt, vt.n), (sce, sce.n)], incs=[(sv, 1)])
                    ph.add("vector",
                           lambda e, bfi=bfi: e.tensor_add(x1w[:], xht[bfi][:], aow[:]),
                           incs=[(sv, 1)])
                    ph.add("sync",
                           lambda e, row=row: e.dma_start(out=x1_dram[row:row + 128, :],
                                                          in_=x1w[:]),
                           waits=[sv], incs=[(sd, 16)])
                    emit_stats(ph, x1w, sqscr, 2, 3, cv[:, 0:1])
                    ph.add("vector",
                           lambda e: e.tensor_scalar(vw[:], x1w[:], stat[:, 2:3],
                                                     stat[:, 3:4], ALU.add, ALU.mult),
                           waits=[ss, st, sd, (vt, vt.n), (sce, sce.n)], incs=[(sv, 1)])
                    ph.add("sync",
                           lambda e, row=row: e.dma_start(out=v_dram[row:row + 128, :],
                                                          in_=vw[:]),
                           waits=[sv], incs=[(sd, 16)])
                    for a in range(8):
                        ph.add("tensor",
                               lambda e, a=a: e.transpose(ptrv[:, a * 128:(a + 1) * 128],
                                                          vw[:, a * 128:(a + 1) * 128],
                                                          eyef[:]),
                               waits=([sv, ss] if a == 0 else []),
                               incs=([(st, 1)] if a == 7 else []))
                    ph.add("scalar", lambda e: e.copy(vT[:], ptrv[:]), waits=[st],
                           incs=[(ss, 1)])
                    for a in range(8):
                        ph.add("tensor",
                               lambda e, a=a: e.matmul(psc[:, 0:8],
                                                       vT[:, a * 128:(a + 1) * 128],
                                                       wg_sb[:, a * E:(a + 1) * E],
                                                       start=(a == 0), stop=(a == 7)),
                               waits=([ss] if a == 0 else []),
                               incs=([(st, 1)] if a == 7 else []))
                    ph.add("scalar", lambda e: e.copy(sct[:], psc[:, 0:8]), waits=[st],
                           incs=[(ss, 1)])
                    def tv(fn, extra=()):
                        ph.add("vector", fn,
                               waits=list(extra) + [(vt, vt.n)],
                               incs=[(vt, 1)])
                    tv(lambda e: e.reduce_max(stat[:, 8:9], sct[:], axis=AX.X,
                                              negate=True), extra=[ss])
                    tv(lambda e: e.tensor_single_scalar(cwt[:], sct[:], stat[:, 8:9],
                                                        ALU.add))
                    tv(lambda e: e.tensor_single_scalar(cwt[:], cwt[:], 0.0, ALU.is_ge))
                    tv(lambda e: e.scalar_tensor_tensor(cwt[:], cwt[:], -1e30, sct[:],
                                                        ALU.mult, ALU.add))
                    tv(lambda e: e.reduce_max(stat[:, 9:10], cwt[:], axis=AX.X))
                    tv(lambda e: e.tensor_scalar_mul(stat[:, 10:11], stat[:, 9:10], -1.0))
                    tv(lambda e: e.tensor_single_scalar(cwt[:], sct[:], stat[:, 10:11],
                                                        ALU.add))
                    tv(lambda e: e.tensor_single_scalar(cwt[:], cwt[:], 0.0, ALU.is_ge))
                    ph.add("scalar",
                           lambda e: e.activation(sct[:], sct[:], AF.Exp, bias=stat[:, 8:9]),
                           waits=[(vt, vt.n), (sce, sce.n)], incs=[(sce, 1)])
                    tv(lambda e: e.tensor_mul(cwt[:], cwt[:], sct[:]),
                       extra=[(sce, sce.n)])
                    tv(lambda e: e.reduce_sum(stat[:, 11:12], cwt[:], axis=AX.X))
                    tv(lambda e: e.reciprocal(stat[:, 12:13], stat[:, 11:12]))
                    tv(lambda e, tt=tt: e.tensor_scalar_mul(
                           cw_sb[:, tt * E:(tt + 1) * E], cwt[:], stat[:, 12:13]))
                    ph.add("tensor",
                           lambda e, tt=tt: e.transpose(psc[0:8, 128:256],
                                                        cw_sb[:, tt * E:(tt + 1) * E],
                                                        eyef[:]),
                           waits=[(vt, vt.n)], incs=[(st, 1)])
                    ph.add("scalar",
                           lambda e, tt=tt: e.copy(cwT_sb[0:8, tt * 128:(tt + 1) * 128],
                                                   psc[0:8, 128:256]),
                           waits=[st], incs=[(ss, 1)])
                    emit_stats(ph, vw, sqscr, 4, 5, cv[:, 1:2])
                    ph.add("vector",
                           lambda e: e.tensor_scalar(xhbf[:], vw[:], stat[:, 4:5],
                                                     stat[:, 5:6], ALU.add, ALU.mult),
                           waits=[ss, st, (vt, vt.n), (sce, sce.n)], incs=[(sv, 1)])
                    for a in range(8):
                        ph.add("tensor",
                               lambda e, a=a: e.transpose(ptrb[:, a * 128:(a + 1) * 128],
                                                          xhbf[:, a * 128:(a + 1) * 128],
                                                          eyeb[:]),
                               waits=([sv, ss] if a == 0 else []),
                               incs=([(st, 1)] if a == 7 else []))
                    ph.add("scalar",
                           lambda e, tt=tt: e.copy(
                               xhatT[:].rearrange("p (a t) -> p a t", a=8)[:, :, tt * 128:(tt + 1) * 128],
                               ptrb[:].rearrange("p (a t) -> p a t", a=8)),
                           waits=[st], incs=[(ss, 1)])
                    for dh in range(2):
                        ph.add("tensor",
                               lambda e, tt=tt, dh=dh: e.matmul(
                                   pb2[:, dh * 512:(dh + 1) * 512],
                                   cwT_sb[0:8, tt * 128:(tt + 1) * 128],
                                   b2_sb[0:8, dh * 512:(dh + 1) * 512],
                                   start=True, stop=True),
                               waits=([ss, (gd, gd_b2)] if dh == 0 else []),
                               incs=([(st, 1)] if dh == 1 else []))
                    ph.add("scalar", lambda e: e.copy(b2ev[:], pb2[:]), waits=[st, sd],
                           incs=[(ss, 1)])
                    ph.add("sync",
                           lambda e, row=row: e.dma_start(
                               out=ymoe_dram[row:row + 128, :], in_=b2ev[:]),
                           waits=[ss], incs=[(sd, 16)])
                ph.emit(nc)

            esD = ExitStack()
            with esD:
                w1h = sb(esD, "w1h", [128, 8 * 2048], BF16)
                w2h = sb(esD, "w2h", [128, 16 * 1024], BF16)
                nT = sb(esD, "nT", [128, 8 * TS], BF16)
                h_sb = sb(esD, "h_sb", [128, 4 * TS], BF16)
                yin = sb(esD, "yin", [128, 2 * D], F32)
                yout = sb(esD, "yout", [128, 2 * D], F32)
                ph_ps = ps(esD, "ph_ps", [128, 1024], F32)
                py = [ps(esD, f"py{i}", [128, 1024], F32) for i in range(2)]

                ph = Phase()
                for e_i in range(E):
                    for fh in range(2):
                        for kb in range(8):
                            ph.add("gpsimd",
                                   lambda e, e_i=e_i, fh=fh, kb=kb: e.dma_start(
                                       out=w1h[:, kb * 2048:(kb + 1) * 2048],
                                       in_=w1_d[e_i, kb * 128:(kb + 1) * 128,
                                                fh * 2048:(fh + 1) * 2048]),
                                   waits=([st] if kb == 0 else []),
                                   incs=[(gd, 16)])
                        for a in range(16):
                            ph.add("gpsimd",
                                   lambda e, e_i=e_i, fh=fh, a=a: e.dma_start(
                                       out=w2h[:, a * 1024:(a + 1) * 1024],
                                       in_=w2_d[e_i, fh * 2048 + a * 128:
                                                fh * 2048 + (a + 1) * 128, :]),
                                   incs=[(gd, 16)])
                        gd_ef = gd.n
                        for ts in range(NTS):
                            trow = ts * TS
                            bfi = ts % 2
                            for kb in range(8):
                                ph.add("vector",
                                       lambda e, e_i=e_i, kb=kb, ts=ts: e.tensor_scalar(
                                           nT[:, kb * TS:(kb + 1) * TS],
                                           xhatT[:, kb * TH + ts * TS: kb * TH + (ts + 1) * TS],
                                           gamma_sb[:, e_i * 8 + kb:e_i * 8 + kb + 1],
                                           beta_sb[:, e_i * 8 + kb:e_i * 8 + kb + 1],
                                           ALU.mult, ALU.add),
                                       waits=([st] if kb == 0 else []),
                                       incs=([(sv, 1)] if kb == 7 else []))
                            ph.add("sync",
                                   lambda e, trow=trow: e.dma_start(
                                       out=yin[:].rearrange("p (a d) -> p a d", a=2),
                                       in_=ymoe_dram[trow:trow + TS, :]
                                           .rearrange("(a p) d -> p a d", p=128)),
                                   waits=[sv, sd], incs=[(sd, 16)])
                            for fb in range(NFB):
                                for ft in range(4):
                                    fcol = fb * 512 + ft * 128
                                    for kb in range(8):
                                        ph.add("tensor",
                                               lambda e, kb=kb, ft=ft, fcol=fcol: e.matmul(
                                                   ph_ps[:, ft * TS:(ft + 1) * TS],
                                                   w1h[:, kb * 2048 + fcol: kb * 2048 + fcol + 128],
                                                   nT[:, kb * TS:(kb + 1) * TS],
                                                   start=(kb == 0), stop=(kb == 7)),
                                               waits=([sv, ss, (gd, gd_ef)]
                                                      if (ft == 0 and kb == 0) else []),
                                               incs=([(st, 1)] if kb == 7 else []))
                                for ft in range(4):
                                    fi = fh * 16 + fb * 4 + ft
                                    ph.add("scalar",
                                           lambda e, ft=ft, fi=fi, e_i=e_i: e.activation(
                                               h_sb[:, ft * TS:(ft + 1) * TS],
                                               ph_ps[:, ft * TS:(ft + 1) * TS], AF.Relu,
                                               bias=b1_sb[:, e_i * 32 + fi:e_i * 32 + fi + 1]),
                                           waits=([st] if ft == 0 else []),
                                           incs=([(ss, 1)] if ft == 3 else []))
                                for tt2 in range(2):
                                    for dh in range(2):
                                        for ft in range(4):
                                            a = fb * 4 + ft
                                            first = (fb == 0 and ft == 0)
                                            last = (fb == NFB - 1 and ft == 3)
                                            ph.add("tensor",
                                                   lambda e, tt2=tt2, dh=dh, ft=ft, a=a,
                                                          first=first, last=last: e.matmul(
                                                       py[tt2][:, dh * 512:(dh + 1) * 512],
                                                       h_sb[:, ft * TS + tt2 * 128:
                                                            ft * TS + tt2 * 128 + 128],
                                                       w2h[:, a * 1024 + dh * 512:
                                                           a * 1024 + (dh + 1) * 512],
                                                       start=first, stop=last),
                                                   waits=([ss, sv] if (first and tt2 == 0 and dh == 0) else
                                                          ([ss] if (ft == 0 and dh == 0) else [])),
                                                   incs=([(st, 1)] if (last and dh == 1 and tt2 == 1) else []))
                            for tt2 in range(2):
                                ph.add("vector",
                                       lambda e, tt2=tt2, e_i=e_i, ts=ts: e.scalar_tensor_tensor(
                                           yout[:, tt2 * D:(tt2 + 1) * D],
                                           py[tt2][:, 0:1024],
                                           cw_sb[:, (ts * 2 + tt2) * E + e_i:
                                                 (ts * 2 + tt2) * E + e_i + 1],
                                           yin[:, tt2 * D:(tt2 + 1) * D],
                                           ALU.mult, ALU.add),
                                       waits=[st, sd], incs=[(sv, 1)])
                            ph.add("sync",
                                   lambda e, trow=trow: e.dma_start(
                                       out=ymoe_dram[trow:trow + TS, :]
                                           .rearrange("(a p) d -> p a d", p=128),
                                       in_=yout[:].rearrange("p (a d) -> p a d", a=2)),
                                   waits=[sv], incs=[(sd, 16)])
                ph.emit(nc)

        esE = ExitStack()
        with esE:
            ytl = [sb(esE, f"ytl{i}", [128, D], F32) for i in range(2)]
            vtl = [sb(esE, f"vtl{i}", [128, D], F32) for i in range(2)]
            x1t = [sb(esE, f"x1t{i}", [128, D], F32) for i in range(2)]
            fyw = sb(esE, "fyw", [128, D], F32)
            sqscr = sb(esE, "sqscrE", [128, D], F32)
            otl = sb(esE, "otl", [128, D], F32)

            ph = Phase()
            for tt in range(TH // 128):
                row = tt * 128
                bfi = tt % 2
                ph.add("sync",
                       lambda e, row=row, bfi=bfi: e.dma_start(
                           out=ytl[bfi][:], in_=ymoe_dram[row:row + 128, :]),
                       waits=bw(), incs=[(sd, 16)])
                ph.add("sync",
                       lambda e, row=row, bfi=bfi: e.dma_start(
                           out=vtl[bfi][:], in_=v_dram[row:row + 128, :]),
                       incs=[(sd, 16)])
                ph.add("sync",
                       lambda e, row=row, bfi=bfi: e.dma_start(
                           out=x1t[bfi][:], in_=x1_dram[row:row + 128, :]),
                       incs=[(sd, 16)])
                ph.add("vector",
                       lambda e, bfi=bfi: e.tensor_add(fyw[:], vtl[bfi][:], ytl[bfi][:]),
                       waits=[sd, ss, ob], incs=[(sv, 1)])
                emit_stats(ph, fyw, sqscr, 0, 4, cv[:, 0:1])
                ph.add("vector",
                       lambda e: e.tensor_scalar(fyw[:], fyw[:], stat[:, 0:1],
                                                 stat[:, 4:5], ALU.add, ALU.mult),
                       waits=[(vt, vt.n), (sce, sce.n)], incs=[(sv, 1)])
                ph.add("vector",
                       lambda e, bfi=bfi: e.tensor_add(otl[:], x1t[bfi][:], fyw[:]),
                       incs=[(sv, 1)])
                ph.add("sync",
                       lambda e, row=row: e.dma_start(out=out_d[row:row + 128, :],
                                                      in_=otl[:]),
                       waits=[sv], incs=[(ob, 16)])
            for (o, t) in dbg_copies:
                ph.add("sync", lambda e, o=o, t=t: e.dma_start(out=o, in_=t),
                       waits=bw(), incs=[(ob, 16)])
            ph.add("sync", lambda e: e.nop(), waits=[(ob, ob.n)])
            ph.emit(nc)

    return nc


_NC_CACHE = {}


def _get_nc(debug=False):
    key = bool(debug)
    if key not in _NC_CACHE:
        _NC_CACHE[key] = build_program(debug=key)
    return _NC_CACHE[key]


def _stage(inputs):
    f32 = np.float32
    x = np.asarray(inputs["L0_attn_x"], f32)
    nw = np.asarray(inputs["L0_attn_norm_w"], f32)
    Wx = np.asarray(inputs["L0_attn_Wx"], f32)
    bx = np.asarray(inputs["L0_attn_bx"], f32)
    Wz = np.asarray(inputs["L0_attn_Wz"], f32)
    bz = np.asarray(inputs["L0_attn_bz"], f32)
    convw = np.asarray(inputs["L0_attn_conv_w"], f32).reshape(DI, 4)
    convb = np.asarray(inputs["L0_attn_conv_b"], f32)
    Wxp = np.asarray(inputs["L0_attn_Wxp"], f32)
    Wdt = np.asarray(inputs["L0_attn_Wdt"], f32)
    bdt = np.asarray(inputs["L0_attn_bdt"], f32)
    Alog = np.asarray(inputs["L0_attn_A_log"], f32)
    Dp = np.asarray(inputs["L0_attn_D"], f32)
    Wout = np.asarray(inputs["L0_attn_Wout"], f32)
    bout = np.asarray(inputs["L0_attn_bout"], f32)
    Wg = np.asarray(inputs["L0_ffn_Wg"], f32)
    gamma = np.asarray(inputs["L0_ffn_gamma"], f32)
    beta = np.asarray(inputs["L0_ffn_beta"], f32)
    W1 = np.ascontiguousarray(np.asarray(inputs["L0_ffn_W1"], f32))
    b1 = np.asarray(inputs["L0_ffn_b1"], f32)
    W2 = np.ascontiguousarray(np.asarray(inputs["L0_ffn_W2"], f32))
    b2 = np.ascontiguousarray(np.asarray(inputs["L0_ffn_b2"], f32))

    eye = np.eye(128, dtype=f32)
    constv = np.ascontiguousarray(
        np.tile(np.array([EPS_A, EPS_F, 1.0, -1.0, -0.5, 1.0 / D, 0.0, 0.0],
                         dtype=f32), (128, 1)))
    wg_s = np.ascontiguousarray(Wg.reshape(8, 128, E).transpose(1, 0, 2).reshape(128, 8 * E))
    gamma_s = np.ascontiguousarray(gamma.reshape(E, 8, 128).transpose(2, 0, 1).reshape(128, E * 8))
    beta_s = np.ascontiguousarray(beta.reshape(E, 8, 128).transpose(2, 0, 1).reshape(128, E * 8))
    b1_s = np.ascontiguousarray(b1.reshape(E, 32, 128).transpose(2, 0, 1).reshape(128, E * 32))

    def chparam(vec, p):
        hh = vec[p * CH:(p + 1) * CH]
        return np.ascontiguousarray(hh.reshape(NJ, 128).T)

    per_par = {}
    for p in range(2):
        chs = slice(p * CH, (p + 1) * CH)
        per_par[p] = {
            "wxt_w": np.ascontiguousarray(Wx[chs, :].T),
            "wzt_w": np.ascontiguousarray(Wz[chs, :].T),
            "bx": chparam(bx, p),
            "bz": chparam(bz, p),
            "convw": np.ascontiguousarray(
                convw[chs].reshape(NJ, 128, 4).transpose(1, 0, 2).reshape(128, NJ * 4)),
            "convb": chparam(convb, p),
            "wxpt_w": np.ascontiguousarray(Wxp[:, chs].T),
            "wdtt_w": np.ascontiguousarray(Wdt[chs, :].T),
            "bdt": chparam(bdt, p),
            "alog": np.ascontiguousarray(
                Alog[chs].reshape(NJ, 128, N16).transpose(1, 0, 2).reshape(128, NJ * N16)),
            "dp": chparam(Dp, p),
            "woutt_w": np.ascontiguousarray(Wout[:, chs].T),
        }

    in_maps = []
    for c in range(8):
        b, p = c // 2, c % 2
        im = {
            "x_full": np.ascontiguousarray(x[b]),
            "x_half": np.ascontiguousarray(x[b, p * TH:(p + 1) * TH]),
            "eye_f": eye,
            "constv": constv,
            "norm_w": nw,
            "bout": bout,
            "wg": wg_s,
            "gamma": gamma_s,
            "beta": beta_s,
            "w1": W1,
            "b1": b1_s,
            "w2": W2,
            "b2": b2,
        }
        im.update(per_par[p])
        in_maps.append(im)
    return in_maps


def kernel(**inputs):
    nc = _get_nc(debug=False)
    in_maps = _stage(inputs)
    res = run_bass_kernel_spmd(nc, in_maps, core_ids=list(range(8)))
    out = np.empty((Bq, L, D), np.float32)
    for c in range(8):
        b, p = c // 2, c % 2
        out[b, p * TH:(p + 1) * TH] = res.results[c]["out"]
    return out


# version 13
# speedup vs baseline: 1.0142x; 1.0142x over previous
import numpy as np
from contextlib import ExitStack

import concourse.bass as bass
import concourse.mybir as mybir
from concourse.bass_utils import run_bass_kernel_spmd

F32 = mybir.dt.float32
BF16 = mybir.dt.bfloat16
AF = mybir.ActivationFunctionType
ALU = mybir.AluOpType
AX = mybir.AxisListType

Bq, L, D = 4, 8192, 1024
DI = 2048
CH = DI // 2
NJ = CH // 128
N16 = 16
R = 64
RR = 96
TH = L // 2
E, F = 8, 4096
EPS_A = 1e-5
EPS_F = 1e-6

CA = 2048
NCA = L // CA
QB = 512
NCB = L // QB
TS = 256
NTS = TH // TS
NFB = 4

PAIRS = [[0, 1], [2, 3], [4, 5], [6, 7]]
ENGINES = ("sync", "gpsimd", "vector", "scalar", "tensor")


class Sem:
    def __init__(self, nc, es, name):
        self.h = es.enter_context(nc.semaphore(name))
        self.n = 0


class Phase:
    def __init__(self):
        self.ops = {e: [] for e in ENGINES}

    def add(self, eng, fn, waits=(), incs=()):
        w = []
        for x in waits:
            if isinstance(x, tuple):
                w.append((x[0].h, x[1]))
            else:
                w.append((x.h, x.n))
        ii = []
        assert len(list(incs)) <= 1, "one sync update per instruction"
        for (s, k) in incs:
            if k is None:
                s.n += 1
                ii.append((s.h, None))
            else:
                s.n += k
                ii.append((s.h, k))
        self.ops[eng].append((fn, w, ii))

    def emit(self, nc):
        with nc.Block() as blk:
            for eng in ENGINES:
                ops = self.ops[eng]
                if not ops:
                    continue

                def mk(ops):
                    def body(e):
                        for (fn, w, ii) in ops:
                            for (h, c) in w:
                                if c > 0:
                                    e.wait_ge(h, c)
                            instr = fn(e)
                            for (h, k) in ii:
                                if k is None:
                                    instr.then_inc(h)
                                else:
                                    instr.then_inc(h, k)
                    return body

                getattr(blk, eng)(mk(ops))


def build_program(debug=False):
    nc = bass.Bass("TRN2", target_bir_lowering=False)

    def din(name, shape, dt=F32):
        return nc.dram_tensor(name, shape, dt, kind="ExternalInput").ap()

    x_full = din("x_full", [L, D])
    x_half = din("x_half", [TH, D])
    eye_f = din("eye_f", [128, 128])
    constv_d = din("constv", [128, 8])
    norm_w = din("norm_w", [D])
    wxt_d = din("wxt_w", [D, CH])
    wzt_d = din("wzt_w", [D, CH])
    bx_d = din("bx", [128, NJ])
    bz_d = din("bz", [128, NJ])
    convw_d = din("convw", [128, NJ * 4])
    convb_d = din("convb", [128, NJ])
    wxpt_d = din("wxpt_w", [CH, RR])
    wdtt_d = din("wdtt_w", [R, CH])
    bdt_d = din("bdt", [128, NJ])
    alog_d = din("alog", [128, NJ * N16])
    dp_d = din("dp", [128, NJ])
    woutt_d = din("woutt_w", [CH, D])
    bout_d = din("bout", [D])
    wg_d = din("wg", [128, 8 * E])
    gamma_d = din("gamma", [128, E * 8])
    beta_d = din("beta", [128, E * 8])
    w1_d = din("w1", [E, D, F])
    b1_d = din("b1", [128, E * (F // 128)])
    w2_d = din("w2", [E, F, D])
    b2_d = din("b2", [E, D])

    out_d = nc.dram_tensor("out", [TH, D], F32, kind="ExternalOutput").ap()

    dbg_copies = []

    def scr(name, shape, dt=F32, dbg=False):
        t = nc.dram_tensor(name, shape, dt, kind="Internal").ap()
        if debug and dbg:
            o = nc.dram_tensor("dbg_" + name, shape, dt, kind="ExternalOutput").ap()
            dbg_copies.append((o, t))
        return t

    u_dram = scr("u_dram", [L, D], dbg=True)
    xb2_dram = scr("xb2_dram", [CH, L], BF16, dbg=True)
    zs_dram = scr("zs_dram", [CH, L], BF16)
    xdbl_in = scr("xdbl_in", [RR, L])
    xdbl_out = scr("xdbl_out", [RR, L], dbg=True)
    rs_in = scr("rs_in", [L, D])
    rs_out = scr("rs_out", [TH, D], dbg=True)
    x1_dram = scr("x1_dram", [TH, D], dbg=True)
    v_dram = scr("v_dram", [TH, D])
    ymoe_dram = scr("ymoe_dram", [TH, D], dbg=True)
    nwrep_dram = scr("nwrep_dram", [128, D])
    boutrep_dram = scr("boutrep_dram", [128, D])

    es = ExitStack()
    with es:
        sd = Sem(nc, es, "sd")
        gd = Sem(nc, es, "gd")
        sv = Sem(nc, es, "sv")
        ss = Sem(nc, es, "ss")
        st = Sem(nc, es, "st")
        c1 = Sem(nc, es, "c1")
        c2 = Sem(nc, es, "c2")
        ob = Sem(nc, es, "ob")
        vt = Sem(nc, es, "vt")
        sce = Sem(nc, es, "sce")

        ALLS = (sd, gd, sv, ss, st)

        def bw():
            return [sd, gd, sv, ss, st]

        def sb(stk, name, shape, dt=F32):
            return stk.enter_context(nc.sbuf_tensor(name, shape, dt))

        def ps(stk, name, shape, dt=F32):
            return stk.enter_context(nc.psum_tensor(name, shape, dt))

        eyeb = sb(es, "eyeb", [128, 128], BF16)
        eyef = sb(es, "eyef", [128, 128], F32)
        onesb = sb(es, "onesb", [33, 128], BF16)
        onesf = sb(es, "onesf", [1, 128], F32)
        bx_sb = sb(es, "bx_sb", [128, NJ], F32)
        bz_sb = sb(es, "bz_sb", [128, NJ], F32)
        convb_sb = sb(es, "convb_sb", [128, NJ], F32)
        bdt_sb = sb(es, "bdt_sb", [128, NJ], F32)
        dp_sb = sb(es, "dp_sb", [128, NJ], F32)
        convw_sb = sb(es, "convw_sb", [128, NJ * 4], F32)
        a_sb = sb(es, "a_sb", [128, NJ * N16], F32)
        hstate = sb(es, "hstate", [128, NJ * N16], F32)
        cv = sb(es, "cv", [128, 8], F32)
        stat = sb(es, "stat", [128, 16], F32)

        def emit_stats(ph, srcbuf, sq, mcol, vcol, eps_ap, rms=False):
            if not rms:
                ph.add("vector",
                       lambda e: e.reduce_sum(stat[:, mcol:mcol + 1], srcbuf[:],
                                              axis=AX.X, negate=True),
                       incs=[(vt, 1)])
            ph.add("scalar",
                   lambda e: e.activation(sq[:], srcbuf[:], AF.Square,
                                          accum_out=stat[:, vcol:vcol + 1]),
                   waits=[(vt, vt.n)] if not rms else [sv], incs=[(sce, 1)])
            if rms:
                ph.add("scalar",
                       lambda e: e.activation(stat[:, vcol:vcol + 1],
                                              stat[:, vcol:vcol + 1], AF.Ln,
                                              scale=cv[:, 5:6], bias=eps_ap),
                       waits=[(sce, sce.n)], incs=[(sce, 1)])
                ph.add("scalar",
                       lambda e: e.activation(stat[:, vcol:vcol + 1],
                                              stat[:, vcol:vcol + 1], AF.Exp,
                                              scale=cv[:, 4:5]),
                       waits=[(sce, sce.n)], incs=[(sce, 1)])
                return
            ph.add("vector",
                   lambda e: e.tensor_scalar_mul(stat[:, mcol:mcol + 1],
                                                 stat[:, mcol:mcol + 1], 1.0 / D),
                   waits=[(vt, vt.n)], incs=[(vt, 1)])
            ph.add("vector",
                   lambda e: e.tensor_scalar_mul(stat[:, 14:15],
                                                 stat[:, vcol:vcol + 1], 1.0 / D),
                   waits=[(sce, sce.n)], incs=[(vt, 1)])
            ph.add("vector",
                   lambda e: e.tensor_mul(stat[:, 15:16], stat[:, mcol:mcol + 1],
                                          stat[:, mcol:mcol + 1]),
                   waits=[(vt, vt.n)], incs=[(vt, 1)])
            ph.add("vector",
                   lambda e: e.tensor_sub(stat[:, vcol:vcol + 1], stat[:, 14:15],
                                          stat[:, 15:16]),
                   waits=[(vt, vt.n)], incs=[(vt, 1)])
            ph.add("scalar",
                   lambda e: e.activation(stat[:, vcol:vcol + 1],
                                          stat[:, vcol:vcol + 1], AF.Ln, bias=eps_ap),
                   waits=[(vt, vt.n)], incs=[(sce, 1)])
            ph.add("scalar",
                   lambda e: e.activation(stat[:, vcol:vcol + 1],
                                          stat[:, vcol:vcol + 1], AF.Exp,
                                          scale=cv[:, 4:5]),
                   waits=[(sce, sce.n)], incs=[(sce, 1)])

        es0 = ExitStack()
        nwrow = sb(es0, "nwrow", [1, D], F32)
        boutrow = sb(es0, "boutrow", [1, D], F32)
        nw_rep = sb(es0, "nw_rep", [128, D], F32)
        bout_rep = sb(es0, "bout_rep", [128, D], F32)
        pbc = ps(es0, "pbc", [128, 1024], F32)
        ph = Phase()

        def ld(dst, src):
            ph.add("sync", lambda e, d=dst, s=src: e.dma_start(out=d, in_=s),
                   incs=[(sd, 16)])

        ld(eyef[:], eye_f)
        ld(cv[:], constv_d)
        ld(nwrow[:], norm_w.rearrange("(a d) -> a d", a=1))
        ld(boutrow[:], bout_d.rearrange("(a d) -> a d", a=1))
        ld(bx_sb[:], bx_d)
        ld(bz_sb[:], bz_d)
        ld(convb_sb[:], convb_d)
        ld(bdt_sb[:], bdt_d)
        ld(dp_sb[:], dp_d)
        ld(convw_sb[:], convw_d)
        ld(a_sb[:], alog_d)
        ph.add("gpsimd", lambda e: e.dma_start(out=eyeb[:], in_=eye_f),
               incs=[(gd, 16)])
        ph.add("gpsimd", lambda e: e.memset(onesb[:], 1.0), incs=[(gd, 1)])
        ph.add("gpsimd", lambda e: e.memset(onesf[:], 1.0), incs=[(gd, 1)])
        ph.add("scalar", lambda e: e.activation(a_sb[:], a_sb[:], AF.Exp),
               waits=[sd], incs=[(ss, 1)])
        ph.add("vector", lambda e: e.tensor_scalar_mul(a_sb[:], a_sb[:], -1.0),
               waits=[ss], incs=[(sv, 1)])
        ph.add("vector", lambda e: e.memset(hstate[:], 0.0), incs=[(sv, 1)])
        for h in range(2):
            ph.add("tensor",
                   lambda e, h=h: e.matmul(pbc[:, h * 512:(h + 1) * 512], onesf[:],
                                           nwrow[0:1, h * 512:(h + 1) * 512],
                                           start=True, stop=True),
                   waits=([sd, gd] if h == 0 else []), incs=[(st, 1)])
        ph.add("scalar", lambda e: e.copy(nw_rep[:], pbc[:]), waits=[st],
               incs=[(ss, 1)])
        for h in range(2):
            ph.add("tensor",
                   lambda e, h=h: e.matmul(pbc[:, h * 512:(h + 1) * 512], onesf[:],
                                           boutrow[0:1, h * 512:(h + 1) * 512],
                                           start=True, stop=True),
                   waits=([ss] if h == 0 else []), incs=[(st, 1)])
        ph.add("scalar", lambda e: e.copy(bout_rep[:], pbc[:]), waits=[st],
               incs=[(ss, 1)])
        ph.add("sync", lambda e: e.dma_start(out=nwrep_dram, in_=nw_rep[:]),
               waits=[ss], incs=[(sd, 16)])
        ph.add("sync", lambda e: e.dma_start(out=boutrep_dram, in_=bout_rep[:]),
               waits=[ss], incs=[(sd, 16)])
        ph.emit(nc)
        es0.close()

        esA = ExitStack()
        with esA:
            wxt = sb(esA, "wxt", [128, 8 * CH], BF16)
            wzt = sb(esA, "wzt", [128, 8 * CH], BF16)
            wxpt = sb(esA, "wxpt", [128, NJ * RR], BF16)
            xnT = sb(esA, "xnT", [128, 8 * CA], BF16)
            xb2k = sb(esA, "xb2k", [128, NJ * CA], BF16)
            zstile = sb(esA, "zstile", [128, CA], BF16)
            xbw = sb(esA, "xbw", [128, CA + 3], F32)
            xcw = sb(esA, "xcw", [128, CA], F32)
            xbtail = sb(esA, "xbtail", [128, NJ * 3], F32)
            xdblp = sb(esA, "xdblp", [96, L], F32)
            xtile = [sb(esA, f"xtile{i}", [128, D], F32) for i in range(2)]
            sqscr = sb(esA, "sqscrA", [128, D], F32)
            nw_rep = sb(esA, "nw_repA", [128, D], F32)
            utile = sb(esA, "utile", [128, D], F32)
            xntile = sb(esA, "xntile", [128, D], BF16)
            ptr = ps(esA, "ptr", [128, 1024], BF16)
            pxb = ps(esA, "pxb", [128, 512], F32)
            pz = ps(esA, "pz", [128, 512], F32)
            pxd = ps(esA, "pxd", [96, 512], F32)

            ph = Phase()
            ph.add("sync", lambda e: e.dma_start(out=nw_rep[:], in_=nwrep_dram),
                   waits=[sd], incs=[(sd, 16)])
            for a in range(8):
                ph.add("gpsimd",
                       lambda e, a=a: e.dma_start(
                           out=wxt[:, a * CH:(a + 1) * CH],
                           in_=wxt_d[a * 128:(a + 1) * 128, :]),
                       incs=[(gd, 16)])
                ph.add("gpsimd",
                       lambda e, a=a: e.dma_start(
                           out=wzt[:, a * CH:(a + 1) * CH],
                           in_=wzt_d[a * 128:(a + 1) * 128, :]),
                       incs=[(gd, 16)])
            for a in range(NJ):
                ph.add("gpsimd",
                       lambda e, a=a: e.dma_start(
                           out=wxpt[:, a * RR:(a + 1) * RR],
                           in_=wxpt_d[a * 128:(a + 1) * 128, :]),
                       incs=[(gd, 16)])
            gd_wA = gd.n

            for ch in range(NCA):
                t0 = ch * CA
                for tt in range(16):
                    xt = xtile[tt % 2]
                    row = t0 + tt * 128
                    ph.add("sync",
                           lambda e, xt=xt, row=row: e.dma_start(
                               out=xt[:], in_=x_full[row:row + 128, :]),
                           waits=[sv, ss], incs=[(sd, 16)])
                    ph.add("vector", lambda e: e.engine_nop(), waits=[sd])
                    emit_stats(ph, xt, sqscr, 0, 4, cv[:, 0:1])
                    ph.add("vector",
                           lambda e, xt=xt: e.tensor_scalar(utile[:], xt[:], stat[:, 0:1],
                                                            stat[:, 4:5], ALU.add, ALU.mult),
                           waits=[sd, (vt, vt.n), (sce, sce.n)], incs=[(sv, 1)])
                    ph.add("sync",
                           lambda e, row=row: e.dma_start(out=u_dram[row:row + 128, :],
                                                          in_=utile[:]),
                           waits=[sv], incs=[(sd, 16)])
                    emit_stats(ph, utile, sqscr, 0, 6, cv[:, 0:1], rms=True)
                    ph.add("vector",
                           lambda e: e.scalar_tensor_tensor(xntile[:], utile[:], stat[:, 6:7],
                                                            nw_rep[:], ALU.mult, ALU.mult),
                           waits=[st, (sce, sce.n)], incs=[(sv, 1)])
                    for a in range(8):
                        ph.add("tensor",
                               lambda e, a=a: e.transpose(ptr[:, a * 128:(a + 1) * 128],
                                                          xntile[:, a * 128:(a + 1) * 128],
                                                          eyeb[:]),
                               waits=([sv, ss] if a == 0 else []),
                               incs=([(st, 1)] if a == 7 else []))
                    ph.add("scalar",
                           lambda e, tt=tt: e.copy(
                               xnT[:].rearrange("p (a t) -> p a t", a=8)[:, :, tt * 128:(tt + 1) * 128],
                               ptr[:].rearrange("p (a t) -> p a t", a=8)),
                           waits=[st], incs=[(ss, 1)])

                for j in range(NJ):
                    if ch == 0:
                        ph.add("vector", lambda e: e.memset(xbw[:, 0:3], 0.0),
                               waits=[ss, st], incs=[(vt, 1)])
                    else:
                        ph.add("vector",
                               lambda e, j=j: e.tensor_copy(xbw[:, 0:3],
                                                            xbtail[:, j * 3:(j + 1) * 3]),
                               waits=[ss, st], incs=[(vt, 1)])
                    for ts4 in range(4):
                        for a in range(8):
                            ph.add("tensor",
                                   lambda e, a=a, j=j, ts4=ts4: e.matmul(
                                       pxb[:],
                                       wxt[:, a * CH + j * 128: a * CH + (j + 1) * 128],
                                       xnT[:, a * CA + ts4 * 512: a * CA + (ts4 + 1) * 512],
                                       start=(a == 0), stop=(a == 7)),
                                   waits=([ss, sv, (gd, gd_wA)] if a == 0 else []),
                                   incs=([(st, 1)] if a == 7 else []))
                        ph.add("scalar",
                               lambda e, j=j, ts4=ts4: e.activation(
                                   xbw[:, 3 + ts4 * 512: 3 + (ts4 + 1) * 512], pxb[:],
                                   AF.Identity, bias=bx_sb[:, j:j + 1]),
                               waits=[st, sv], incs=[(ss, 1)])
                        for a in range(8):
                            ph.add("tensor",
                                   lambda e, a=a, j=j, ts4=ts4: e.matmul(
                                       pz[:],
                                       wzt[:, a * CH + j * 128: a * CH + (j + 1) * 128],
                                       xnT[:, a * CA + ts4 * 512: a * CA + (ts4 + 1) * 512],
                                       start=(a == 0), stop=(a == 7)),
                                   waits=([ss] if a == 0 else []),
                                   incs=([(st, 1)] if a == 7 else []))
                        ph.add("scalar",
                               lambda e, ts4=ts4, j=j: e.activation(
                                   zstile[:, ts4 * 512:(ts4 + 1) * 512], pz[:], AF.Silu,
                                   bias=bz_sb[:, j:j + 1]),
                               waits=[st, sd], incs=[(ss, 1)])
                    ph.add("sync",
                           lambda e, j=j, t0=t0: e.dma_start(
                               out=zs_dram[j * 128:(j + 1) * 128, t0:t0 + CA],
                               in_=zstile[:]),
                           waits=[ss], incs=[(sd, 16)])
                    ph.add("vector",
                           lambda e, j=j: e.tensor_copy(xbtail[:, j * 3:(j + 1) * 3],
                                                        xbw[:, CA:CA + 3]),
                           waits=[ss], incs=[(sv, 1)])
                    ph.add("vector",
                           lambda e, j=j: e.tensor_scalar_mul(
                               xcw[:], xbw[:, 0:CA],
                               convw_sb[:, j * 4:j * 4 + 1]),
                           waits=[(vt, vt.n)], incs=[(sv, 1)])
                    for k in range(1, 4):
                        ph.add("vector",
                               lambda e, j=j, k=k: e.scalar_tensor_tensor(
                                   xcw[:], xbw[:, k:k + CA],
                                   convw_sb[:, j * 4 + k:j * 4 + k + 1],
                                   xcw[:], ALU.mult, ALU.add),
                               incs=([(sv, 1)] if k == 3 else []))
                    ph.add("scalar",
                           lambda e, j=j: e.activation(
                               xb2k[:, j * CA:(j + 1) * CA], xcw[:], AF.Silu,
                               bias=convb_sb[:, j:j + 1]),
                           waits=[sv, st, sd], incs=[(ss, 1)])
                    ph.add("sync",
                           lambda e, j=j, t0=t0: e.dma_start(
                               out=xb2_dram[j * 128:(j + 1) * 128, t0:t0 + CA],
                               in_=xb2k[:, j * CA:(j + 1) * CA]),
                           waits=[ss], incs=[(sd, 16)])

                for ts4 in range(4):
                    for j in range(NJ):
                        ph.add("tensor",
                               lambda e, j=j, ts4=ts4: e.matmul(
                                   pxd[:],
                                   wxpt[:, j * RR:(j + 1) * RR],
                                   xb2k[:, j * CA + ts4 * 512: j * CA + (ts4 + 1) * 512],
                                   start=(j == 0), stop=(j == NJ - 1)),
                               waits=([ss, sv] if j == 0 else []),
                               incs=([(st, 1)] if j == NJ - 1 else []))
                    ph.add("scalar",
                           lambda e, ts4=ts4, t0=t0: e.copy(
                               xdblp[0:96, t0 + ts4 * 512: t0 + (ts4 + 1) * 512], pxd[:]),
                           waits=[st], incs=[(ss, 1)])

            ph.add("sync", lambda e: e.dma_start(out=xdbl_in, in_=xdblp[0:96, :]),
                   waits=[ss], incs=[(sd, 16)])
            ph.emit(nc)

        ph = Phase()
        ph.add("gpsimd",
               lambda e: e.collective_compute(
                   "AllReduce", ALU.add, replica_groups=PAIRS,
                   ins=[xdbl_in.opt()], outs=[xdbl_out.opt()]),
               waits=bw(), incs=[(c1, None)])
        ph.emit(nc)

        esB = ExitStack()
        with esB:
            wdtt = sb(esB, "wdtt", [64, CH], BF16)
            woutt = sb(esB, "woutt", [128, NJ * D], BF16)
            dlowb = sb(esB, "dlowb", [64, QB], BF16)
            bc1 = sb(esB, "bc1", [33, 16 * QB], BF16)
            dt_sb = sb(esB, "dt_sb", [128, NJ * QB], F32)
            u_sb = sb(esB, "u_sb", [128, NJ * QB], F32)
            xb2c = sb(esB, "xb2c", [128, NJ * QB], BF16)
            zsc = sb(esB, "zsc", [128, NJ * QB], BF16)
            dA = sb(esB, "dA", [128, NJ * QB], F32)
            dA_b = sb(esB, "dA_b", [128, NJ * QB], F32)
            dBx = sb(esB, "dBx", [128, NJ * QB], F32)
            h_n = sb(esB, "h_n", [128, NJ * QB], F32)
            y_sb = sb(esB, "y_sb", [128, NJ * QB], F32)
            yz_bf = sb(esB, "yz_bf", [128, NJ * QB], BF16)
            uev = [sb(esB, f"uev{i}", [128, D], F32) for i in range(2)]
            outp = [sb(esB, f"outp{i}", [128, D], F32) for i in range(2)]
            tabs = sb(esB, "tabs", [128, QB], F32)
            tln = sb(esB, "tln", [128, QB], F32)
            pdt = ps(esB, "pdt", [128, 512], F32)
            pb_b = ps(esB, "pb_b", [128, 512], F32)
            pb_c = ps(esB, "pb_c", [128, 512], F32)
            pb_b2 = ps(esB, "pb_b2", [128, 512], F32)
            pb_c2 = ps(esB, "pb_c2", [128, 512], F32)
            po = ps(esB, "po", [128, 1024], F32)

            ph = Phase()
            for a in range(NJ):
                ph.add("gpsimd",
                       lambda e, a=a: e.dma_start(
                           out=woutt[:, a * D:(a + 1) * D],
                           in_=woutt_d[a * 128:(a + 1) * 128, :]),
                       incs=[(gd, 16)])
            ph.add("gpsimd", lambda e: e.dma_start(out=wdtt[:], in_=wdtt_d),
                   incs=[(gd, 16)])

            for cb in range(NCB):
                t0 = cb * QB
                ph.add("gpsimd",
                       lambda e, t0=t0: e.dma_start(out=dlowb[:],
                                                    in_=xdbl_out[0:64, t0:t0 + QB]),
                       waits=[(c1, 1), st, ss], incs=[(gd, 16)])
                ph.add("gpsimd",
                       lambda e, t0=t0: e.dma_start(
                           out=bc1[0:1, :].rearrange("q (n t) -> q n t", n=16),
                           in_=xdbl_out[64:80, t0:t0 + QB]),
                       waits=[(c1, 1), st], incs=[(gd, 16)])
                ph.add("gpsimd",
                       lambda e, t0=t0: e.dma_start(
                           out=bc1[32:33, :].rearrange("q (n t) -> q n t", n=16),
                           in_=xdbl_out[80:96, t0:t0 + QB]),
                       waits=[(c1, 1), st], incs=[(gd, 16)])
                for j in range(NJ):
                    ph.add("sync",
                           lambda e, t0=t0, j=j: e.dma_start(
                               out=xb2c[:, j * QB:(j + 1) * QB],
                               in_=xb2_dram[j * 128:(j + 1) * 128, t0:t0 + QB]),
                           waits=([sv] if j == 0 else []), incs=[(sd, 16)])
                    ph.add("sync",
                           lambda e, t0=t0, j=j: e.dma_start(
                               out=zsc[:, j * QB:(j + 1) * QB],
                               in_=zs_dram[j * 128:(j + 1) * 128, t0:t0 + QB]),
                           incs=[(sd, 16)])

                gd_cb = gd.n
                for j in range(NJ):
                    ph.add("tensor",
                           lambda e, j=j: e.matmul(pdt[:, 0:QB],
                                                   wdtt[:, j * 128:(j + 1) * 128],
                                                   dlowb[:], start=True, stop=True),
                           waits=([(gd, gd_cb), ss, sv] if j == 0 else [ss]),
                           incs=[(st, 1)])
                    ph.add("scalar",
                           lambda e, j=j: e.activation(tabs[:], pdt[:, 0:QB], AF.Abs,
                                                       bias=bdt_sb[:, j:j + 1]),
                           waits=[st])
                    ph.add("scalar",
                           lambda e: e.activation(tabs[:], tabs[:], AF.Exp, scale=cv[:, 3:4]))
                    ph.add("scalar",
                           lambda e: e.activation(tln[:], tabs[:], AF.Ln, bias=cv[:, 2:3]))
                    ph.add("scalar",
                           lambda e, j=j: e.activation(tabs[:], pdt[:, 0:QB], AF.Relu,
                                                       bias=bdt_sb[:, j:j + 1]),
                           incs=[(ss, 1)])
                    ph.add("vector",
                           lambda e, j=j: e.tensor_add(dt_sb[:, j * QB:(j + 1) * QB],
                                                       tabs[:], tln[:]),
                           waits=[ss], incs=[(sv, 1)])
                    ph.add("vector",
                           lambda e, j=j: e.tensor_mul(u_sb[:, j * QB:(j + 1) * QB],
                                                       dt_sb[:, j * QB:(j + 1) * QB],
                                                       xb2c[:, j * QB:(j + 1) * QB]),
                           waits=[sd], incs=[(sv, 1)])
                sv_u = sv.n

                dA2 = [dA, dA_b]
                pbb = [pb_b, pb_b2]
                pbcc = [pb_c, pb_c2]
                yadd_done = {-2: sv_u, -1: sv_u}
                bcast_ok = {}
                for n in range(N16):
                    par = n % 2
                    ph.add("tensor",
                           lambda e, n=n, par=par: e.matmul(
                               pbb[par][:, 0:QB], onesb[0:1, :],
                               bc1[0:1, n * QB:(n + 1) * QB],
                               start=True, stop=True),
                           waits=[(sv, yadd_done[n - 2])], incs=[(st, 1)])
                    ph.add("tensor",
                           lambda e, n=n, par=par: e.matmul(
                               pbcc[par][:, 0:QB], onesb[32:33, :],
                               bc1[32:33, n * QB:(n + 1) * QB],
                               start=True, stop=True),
                           incs=[(st, 1)])
                    bcast_ok[n] = st.n
                    for j in range(NJ):
                        ph.add("scalar",
                               lambda e, j=j, n=n, par=par: e.activation(
                                   dA2[par][:, j * QB:(j + 1) * QB],
                                   dt_sb[:, j * QB:(j + 1) * QB], AF.Exp,
                                   scale=a_sb[:, j * N16 + n: j * N16 + n + 1]),
                               waits=([(sv, yadd_done[n - 2]), sv] if j == 0 and cb == 0 and n < 2 else
                                      ([(sv, yadd_done[n - 2])] if j == 0 else [])),
                               incs=([(ss, 1)] if j == NJ - 1 else []))
                    ss_dA = ss.n
                    for j in range(NJ):
                        ph.add("vector",
                               lambda e, j=j, par=par: e.tensor_mul(
                                   dBx[:, j * QB:(j + 1) * QB],
                                   u_sb[:, j * QB:(j + 1) * QB],
                                   pbb[par][:, 0:QB]),
                               waits=([(st, bcast_ok[n]), (ss, ss_dA)] if j == 0 else []))
                    for j in range(NJ):
                        ph.add("vector",
                               lambda e, j=j, n=n, par=par: e.scalar_tensor_tensor(
                                   dBx[:, j * QB:j * QB + 1],
                                   dA2[par][:, j * QB:j * QB + 1],
                                   hstate[:, j * N16 + n:j * N16 + n + 1],
                                   dBx[:, j * QB:j * QB + 1], ALU.mult, ALU.add),
                               incs=([(vt, 1)] if j == NJ - 1 else []))
                    for j in range(NJ):
                        ph.add("vector",
                               lambda e, j=j, par=par: e.tensor_tensor_scan(
                                   h_n[:, j * QB:(j + 1) * QB],
                                   dA2[par][:, j * QB:(j + 1) * QB],
                                   dBx[:, j * QB:(j + 1) * QB],
                                   0.0, ALU.mult, ALU.add),
                               waits=([(vt, vt.n)] if j == 0 else []),
                               incs=([(vt, 1)] if j == NJ - 1 else []))
                    for j in range(NJ):
                        ph.add("vector",
                               lambda e, j=j, n=n: e.tensor_copy(
                                   hstate[:, j * N16 + n:j * N16 + n + 1],
                                   h_n[:, (j + 1) * QB - 1:(j + 1) * QB]),
                               waits=([(vt, vt.n)] if j == 0 else []))
                    for j in range(NJ):
                        ph.add("vector",
                               lambda e, j=j, par=par: e.tensor_mul(
                                   dA2[par][:, j * QB:(j + 1) * QB],
                                   h_n[:, j * QB:(j + 1) * QB],
                                   pbcc[par][:, 0:QB]))
                        if n == 0:
                            ph.add("vector",
                                   lambda e, j=j, par=par: e.tensor_copy(
                                       y_sb[:, j * QB:(j + 1) * QB],
                                       dA2[par][:, j * QB:(j + 1) * QB]),
                                   incs=([(sv, 1)] if j == NJ - 1 else []))
                        else:
                            ph.add("vector",
                                   lambda e, j=j, par=par: e.tensor_add(
                                       y_sb[:, j * QB:(j + 1) * QB],
                                       y_sb[:, j * QB:(j + 1) * QB],
                                       dA2[par][:, j * QB:(j + 1) * QB]),
                                   incs=([(sv, 1)] if j == NJ - 1 else []))
                    yadd_done[n] = sv.n

                for j in range(NJ):
                    ph.add("vector",
                           lambda e, j=j: e.scalar_tensor_tensor(
                               y_sb[:, j * QB:(j + 1) * QB],
                               xb2c[:, j * QB:(j + 1) * QB],
                               dp_sb[:, j:j + 1],
                               y_sb[:, j * QB:(j + 1) * QB], ALU.mult, ALU.add))
                    ph.add("vector",
                           lambda e, j=j: e.tensor_mul(yz_bf[:, j * QB:(j + 1) * QB],
                                                       y_sb[:, j * QB:(j + 1) * QB],
                                                       zsc[:, j * QB:(j + 1) * QB]),
                           incs=[(sv, 1)])

                for tsub in range(QB // 128):
                    row = t0 + tsub * 128
                    bfi = tsub % 2
                    ph.add("sync",
                           lambda e, row=row, bfi=bfi: e.dma_start(
                               out=uev[bfi][:], in_=u_dram[row:row + 128, :]),
                           waits=[sv], incs=[(sd, 16)])
                    for dh in range(2):
                        for j in range(NJ):
                            ph.add("tensor",
                                   lambda e, j=j, dh=dh, tsub=tsub: e.matmul(
                                       po[:, dh * 512:(dh + 1) * 512],
                                       yz_bf[:, j * QB + tsub * 128: j * QB + (tsub + 1) * 128],
                                       woutt[:, j * D + dh * 512: j * D + (dh + 1) * 512],
                                       start=(j == 0), stop=(j == NJ - 1)),
                                   waits=([sv] if (dh == 0 and j == 0) else []),
                                   incs=([(st, 1)] if (dh == 1 and j == NJ - 1) else []))
                    ph.add("vector",
                           lambda e, bfi=bfi: e.scalar_tensor_tensor(
                               outp[bfi][:, 0:512], uev[bfi][:, 0:512], 0.5,
                               po[:, 0:512], ALU.mult, ALU.add),
                           waits=[st, sd], incs=[(sv, 1)])
                    ph.add("vector",
                           lambda e, bfi=bfi: e.scalar_tensor_tensor(
                               outp[bfi][:, 512:1024], uev[bfi][:, 512:1024], 0.5,
                               po[:, 512:1024], ALU.mult, ALU.add),
                           incs=[(sv, 1)])
                    ph.add("sync",
                           lambda e, row=row, bfi=bfi: e.dma_start(
                               out=rs_in[row:row + 128, :], in_=outp[bfi][:]),
                           waits=[sv], incs=[(sd, 16)])
            ph.emit(nc)

        ph = Phase()
        ph.add("gpsimd",
               lambda e: e.collective_compute(
                   "ReduceScatter", ALU.add, replica_groups=PAIRS,
                   ins=[rs_in.opt()], outs=[rs_out.opt()]),
               waits=bw(), incs=[(c2, None)])
        ph.emit(nc)

        esCD = ExitStack()
        with esCD:
            xhatT = sb(esCD, "xhatT", [128, 8 * TH], BF16)
            wg_sb = sb(esCD, "wg_sb", [128, 8 * E], F32)
            gamma_sb = sb(esCD, "gamma_sb", [128, E * 8], F32)
            beta_sb = sb(esCD, "beta_sb", [128, E * 8], F32)
            b1_sb = sb(esCD, "b1_sb", [128, E * 32], F32)
            b2_sb = sb(esCD, "b2_sb", [8, D], BF16)
            cw_sb = sb(esCD, "cw_sb", [128, (TH // 128) * E], F32)
            cwT_sb = sb(esCD, "cwT_sb", [8, TH], BF16)
            sqscr = sb(esCD, "sqscrC", [128, D], F32)
            bout_rep = sb(esCD, "bout_repC", [128, D], F32)

            esC = ExitStack()
            with esC:
                rst = [sb(esC, f"rst{i}", [128, D], F32) for i in range(2)]
                xht = [sb(esC, f"xht{i}", [128, D], F32) for i in range(2)]
                aow = sb(esC, "aow", [128, D], F32)
                x1w = sb(esC, "x1w", [128, D], F32)
                vw = sb(esC, "vw", [128, D], F32)
                xhbf = sb(esC, "xhbf", [128, D], BF16)
                vT = sb(esC, "vT", [128, 8 * 128], F32)
                sct = sb(esC, "sct", [128, 8], F32)
                cwt = sb(esC, "cwt", [128, 8], F32)
                b2ev = sb(esC, "b2ev", [128, D], F32)
                ptrv = ps(esC, "ptrv", [128, 1024], F32)
                ptrb = ps(esC, "ptrb", [128, 1024], BF16)
                psc = ps(esC, "psc", [128, 512], F32)
                pb2 = ps(esC, "pb2", [128, 1024], F32)

                ph = Phase()
                for dst, srcd in ((wg_sb, wg_d), (gamma_sb, gamma_d),
                                  (beta_sb, beta_d), (b1_sb, b1_d)):
                    ph.add("sync",
                           lambda e, d=dst, s=srcd: e.dma_start(out=d[:], in_=s),
                           incs=[(sd, 16)])
                ph.add("gpsimd", lambda e: e.dma_start(out=b2_sb[:], in_=b2_d),
                       incs=[(gd, 16)])
                ph.add("sync", lambda e: e.dma_start(out=bout_rep[:], in_=boutrep_dram),
                       incs=[(sd, 16)])
                gd_b2 = gd.n

                for tt in range(TH // 128):
                    row = tt * 128
                    bfi = tt % 2
                    ph.add("sync",
                           lambda e, row=row, bfi=bfi: e.dma_start(
                               out=rst[bfi][:], in_=rs_out[row:row + 128, :]),
                           waits=[(c2, 1), sv, ss], incs=[(sd, 16)])
                    ph.add("sync",
                           lambda e, row=row, bfi=bfi: e.dma_start(
                               out=xht[bfi][:], in_=x_half[row:row + 128, :]),
                           incs=[(sd, 16)])
                    ph.add("vector",
                           lambda e, bfi=bfi: e.tensor_add(aow[:], rst[bfi][:], bout_rep[:]),
                           waits=[sd, ss], incs=[(sv, 1)])
                    emit_stats(ph, aow, sqscr, 0, 1, cv[:, 0:1])
                    ph.add("vector",
                           lambda e: e.tensor_scalar(aow[:], aow[:], stat[:, 0:1],
                                                     stat[:, 1:2], ALU.add, ALU.mult),
                           waits=[(vt, vt.n), (sce, sce.n)], incs=[(sv, 1)])
                    ph.add("vector",
                           lambda e, bfi=bfi: e.tensor_add(x1w[:], xht[bfi][:], aow[:]),
                           incs=[(sv, 1)])
                    ph.add("sync",
                           lambda e, row=row: e.dma_start(out=x1_dram[row:row + 128, :],
                                                          in_=x1w[:]),
                           waits=[sv], incs=[(sd, 16)])
                    emit_stats(ph, x1w, sqscr, 2, 3, cv[:, 0:1])
                    ph.add("vector",
                           lambda e: e.tensor_scalar(vw[:], x1w[:], stat[:, 2:3],
                                                     stat[:, 3:4], ALU.add, ALU.mult),
                           waits=[ss, st, sd, (vt, vt.n), (sce, sce.n)], incs=[(sv, 1)])
                    ph.add("sync",
                           lambda e, row=row: e.dma_start(out=v_dram[row:row + 128, :],
                                                          in_=vw[:]),
                           waits=[sv], incs=[(sd, 16)])
                    for a in range(8):
                        ph.add("tensor",
                               lambda e, a=a: e.transpose(ptrv[:, a * 128:(a + 1) * 128],
                                                          vw[:, a * 128:(a + 1) * 128],
                                                          eyef[:]),
                               waits=([sv, ss] if a == 0 else []),
                               incs=([(st, 1)] if a == 7 else []))
                    ph.add("scalar", lambda e: e.copy(vT[:], ptrv[:]), waits=[st],
                           incs=[(ss, 1)])
                    for a in range(8):
                        ph.add("tensor",
                               lambda e, a=a: e.matmul(psc[:, 0:8],
                                                       vT[:, a * 128:(a + 1) * 128],
                                                       wg_sb[:, a * E:(a + 1) * E],
                                                       start=(a == 0), stop=(a == 7)),
                               waits=([ss] if a == 0 else []),
                               incs=([(st, 1)] if a == 7 else []))
                    ph.add("scalar", lambda e: e.copy(sct[:], psc[:, 0:8]), waits=[st],
                           incs=[(ss, 1)])
                    def tv(fn, extra=()):
                        ph.add("vector", fn,
                               waits=list(extra) + [(vt, vt.n)],
                               incs=[(vt, 1)])
                    tv(lambda e: e.reduce_max(stat[:, 8:9], sct[:], axis=AX.X,
                                              negate=True), extra=[ss])
                    tv(lambda e: e.tensor_single_scalar(cwt[:], sct[:], stat[:, 8:9],
                                                        ALU.add))
                    tv(lambda e: e.tensor_single_scalar(cwt[:], cwt[:], 0.0, ALU.is_ge))
                    tv(lambda e: e.scalar_tensor_tensor(cwt[:], cwt[:], -1e30, sct[:],
                                                        ALU.mult, ALU.add))
                    tv(lambda e: e.reduce_max(stat[:, 9:10], cwt[:], axis=AX.X))
                    tv(lambda e: e.tensor_scalar_mul(stat[:, 10:11], stat[:, 9:10], -1.0))
                    tv(lambda e: e.tensor_single_scalar(cwt[:], sct[:], stat[:, 10:11],
                                                        ALU.add))
                    tv(lambda e: e.tensor_single_scalar(cwt[:], cwt[:], 0.0, ALU.is_ge))
                    ph.add("scalar",
                           lambda e: e.activation(sct[:], sct[:], AF.Exp, bias=stat[:, 8:9]),
                           waits=[(vt, vt.n), (sce, sce.n)], incs=[(sce, 1)])
                    tv(lambda e: e.tensor_mul(cwt[:], cwt[:], sct[:]),
                       extra=[(sce, sce.n)])
                    tv(lambda e: e.reduce_sum(stat[:, 11:12], cwt[:], axis=AX.X))
                    tv(lambda e: e.reciprocal(stat[:, 12:13], stat[:, 11:12]))
                    tv(lambda e, tt=tt: e.tensor_scalar_mul(
                           cw_sb[:, tt * E:(tt + 1) * E], cwt[:], stat[:, 12:13]))
                    ph.add("tensor",
                           lambda e, tt=tt: e.transpose(psc[0:8, 128:256],
                                                        cw_sb[:, tt * E:(tt + 1) * E],
                                                        eyef[:]),
                           waits=[(vt, vt.n)], incs=[(st, 1)])
                    ph.add("scalar",
                           lambda e, tt=tt: e.copy(cwT_sb[0:8, tt * 128:(tt + 1) * 128],
                                                   psc[0:8, 128:256]),
                           waits=[st], incs=[(ss, 1)])
                    emit_stats(ph, vw, sqscr, 4, 5, cv[:, 1:2])
                    ph.add("vector",
                           lambda e: e.tensor_scalar(xhbf[:], vw[:], stat[:, 4:5],
                                                     stat[:, 5:6], ALU.add, ALU.mult),
                           waits=[ss, st, (vt, vt.n), (sce, sce.n)], incs=[(sv, 1)])
                    for a in range(8):
                        ph.add("tensor",
                               lambda e, a=a: e.transpose(ptrb[:, a * 128:(a + 1) * 128],
                                                          xhbf[:, a * 128:(a + 1) * 128],
                                                          eyeb[:]),
                               waits=([sv, ss] if a == 0 else []),
                               incs=([(st, 1)] if a == 7 else []))
                    ph.add("scalar",
                           lambda e, tt=tt: e.copy(
                               xhatT[:].rearrange("p (a t) -> p a t", a=8)[:, :, tt * 128:(tt + 1) * 128],
                               ptrb[:].rearrange("p (a t) -> p a t", a=8)),
                           waits=[st], incs=[(ss, 1)])
                    for dh in range(2):
                        ph.add("tensor",
                               lambda e, tt=tt, dh=dh: e.matmul(
                                   pb2[:, dh * 512:(dh + 1) * 512],
                                   cwT_sb[0:8, tt * 128:(tt + 1) * 128],
                                   b2_sb[0:8, dh * 512:(dh + 1) * 512],
                                   start=True, stop=True),
                               waits=([ss, (gd, gd_b2)] if dh == 0 else []),
                               incs=([(st, 1)] if dh == 1 else []))
                    ph.add("scalar", lambda e: e.copy(b2ev[:], pb2[:]), waits=[st, sd],
                           incs=[(ss, 1)])
                    ph.add("sync",
                           lambda e, row=row: e.dma_start(
                               out=ymoe_dram[row:row + 128, :], in_=b2ev[:]),
                           waits=[ss], incs=[(sd, 16)])
                ph.emit(nc)

            esD = ExitStack()
            with esD:
                w1h = sb(esD, "w1h", [128, 8 * 2048], BF16)
                w2h = sb(esD, "w2h", [128, 16 * 1024], BF16)
                nT = sb(esD, "nT", [128, 8 * TS], BF16)
                h_sb = sb(esD, "h_sb", [128, 4 * TS], BF16)
                yin = sb(esD, "yin", [128, 2 * D], F32)
                yout = sb(esD, "yout", [128, 2 * D], F32)
                ph_ps = ps(esD, "ph_ps", [128, 1024], F32)
                py = [ps(esD, f"py{i}", [128, 1024], F32) for i in range(2)]

                ph = Phase()
                for e_i in range(E):
                    for fh in range(2):
                        for kb in range(8):
                            ph.add("gpsimd",
                                   lambda e, e_i=e_i, fh=fh, kb=kb: e.dma_start(
                                       out=w1h[:, kb * 2048:(kb + 1) * 2048],
                                       in_=w1_d[e_i, kb * 128:(kb + 1) * 128,
                                                fh * 2048:(fh + 1) * 2048]),
                                   waits=([st] if kb == 0 else []),
                                   incs=[(gd, 16)])
                        for a in range(16):
                            ph.add("gpsimd",
                                   lambda e, e_i=e_i, fh=fh, a=a: e.dma_start(
                                       out=w2h[:, a * 1024:(a + 1) * 1024],
                                       in_=w2_d[e_i, fh * 2048 + a * 128:
                                                fh * 2048 + (a + 1) * 128, :]),
                                   incs=[(gd, 16)])
                        gd_ef = gd.n
                        for ts in range(NTS):
                            trow = ts * TS
                            bfi = ts % 2
                            for kb in range(8):
                                ph.add("vector",
                                       lambda e, e_i=e_i, kb=kb, ts=ts: e.tensor_scalar(
                                           nT[:, kb * TS:(kb + 1) * TS],
                                           xhatT[:, kb * TH + ts * TS: kb * TH + (ts + 1) * TS],
                                           gamma_sb[:, e_i * 8 + kb:e_i * 8 + kb + 1],
                                           beta_sb[:, e_i * 8 + kb:e_i * 8 + kb + 1],
                                           ALU.mult, ALU.add),
                                       waits=([st] if kb == 0 else []),
                                       incs=([(sv, 1)] if kb == 7 else []))
                            ph.add("sync",
                                   lambda e, trow=trow: e.dma_start(
                                       out=yin[:].rearrange("p (a d) -> p a d", a=2),
                                       in_=ymoe_dram[trow:trow + TS, :]
                                           .rearrange("(a p) d -> p a d", p=128)),
                                   waits=[sv, sd], incs=[(sd, 16)])
                            for fb in range(NFB):
                                for ft in range(4):
                                    fcol = fb * 512 + ft * 128
                                    for kb in range(8):
                                        ph.add("tensor",
                                               lambda e, kb=kb, ft=ft, fcol=fcol: e.matmul(
                                                   ph_ps[:, ft * TS:(ft + 1) * TS],
                                                   w1h[:, kb * 2048 + fcol: kb * 2048 + fcol + 128],
                                                   nT[:, kb * TS:(kb + 1) * TS],
                                                   start=(kb == 0), stop=(kb == 7)),
                                               waits=([sv, ss, (gd, gd_ef)]
                                                      if (ft == 0 and kb == 0) else []),
                                               incs=([(st, 1)] if kb == 7 else []))
                                for ft in range(4):
                                    fi = fh * 16 + fb * 4 + ft
                                    ph.add("scalar",
                                           lambda e, ft=ft, fi=fi, e_i=e_i: e.activation(
                                               h_sb[:, ft * TS:(ft + 1) * TS],
                                               ph_ps[:, ft * TS:(ft + 1) * TS], AF.Relu,
                                               bias=b1_sb[:, e_i * 32 + fi:e_i * 32 + fi + 1]),
                                           waits=([st] if ft == 0 else []),
                                           incs=([(ss, 1)] if ft == 3 else []))
                                for tt2 in range(2):
                                    for dh in range(2):
                                        for ft in range(4):
                                            a = fb * 4 + ft
                                            first = (fb == 0 and ft == 0)
                                            last = (fb == NFB - 1 and ft == 3)
                                            ph.add("tensor",
                                                   lambda e, tt2=tt2, dh=dh, ft=ft, a=a,
                                                          first=first, last=last: e.matmul(
                                                       py[tt2][:, dh * 512:(dh + 1) * 512],
                                                       h_sb[:, ft * TS + tt2 * 128:
                                                            ft * TS + tt2 * 128 + 128],
                                                       w2h[:, a * 1024 + dh * 512:
                                                           a * 1024 + (dh + 1) * 512],
                                                       start=first, stop=last),
                                                   waits=([ss, sv] if (first and tt2 == 0 and dh == 0) else
                                                          ([ss] if (ft == 0 and dh == 0) else [])),
                                                   incs=([(st, 1)] if (last and dh == 1 and tt2 == 1) else []))
                            for tt2 in range(2):
                                ph.add("vector",
                                       lambda e, tt2=tt2, e_i=e_i, ts=ts: e.scalar_tensor_tensor(
                                           yout[:, tt2 * D:(tt2 + 1) * D],
                                           py[tt2][:, 0:1024],
                                           cw_sb[:, (ts * 2 + tt2) * E + e_i:
                                                 (ts * 2 + tt2) * E + e_i + 1],
                                           yin[:, tt2 * D:(tt2 + 1) * D],
                                           ALU.mult, ALU.add),
                                       waits=[st, sd], incs=[(sv, 1)])
                            ph.add("sync",
                                   lambda e, trow=trow: e.dma_start(
                                       out=ymoe_dram[trow:trow + TS, :]
                                           .rearrange("(a p) d -> p a d", p=128),
                                       in_=yout[:].rearrange("p (a d) -> p a d", a=2)),
                                   waits=[sv], incs=[(sd, 16)])
                ph.emit(nc)

        esE = ExitStack()
        with esE:
            ytl = [sb(esE, f"ytl{i}", [128, D], F32) for i in range(2)]
            vtl = [sb(esE, f"vtl{i}", [128, D], F32) for i in range(2)]
            x1t = [sb(esE, f"x1t{i}", [128, D], F32) for i in range(2)]
            fyw = sb(esE, "fyw", [128, D], F32)
            sqscr = sb(esE, "sqscrE", [128, D], F32)
            otl = sb(esE, "otl", [128, D], F32)

            ph = Phase()
            for tt in range(TH // 128):
                row = tt * 128
                bfi = tt % 2
                ph.add("sync",
                       lambda e, row=row, bfi=bfi: e.dma_start(
                           out=ytl[bfi][:], in_=ymoe_dram[row:row + 128, :]),
                       waits=bw(), incs=[(sd, 16)])
                ph.add("sync",
                       lambda e, row=row, bfi=bfi: e.dma_start(
                           out=vtl[bfi][:], in_=v_dram[row:row + 128, :]),
                       incs=[(sd, 16)])
                ph.add("sync",
                       lambda e, row=row, bfi=bfi: e.dma_start(
                           out=x1t[bfi][:], in_=x1_dram[row:row + 128, :]),
                       incs=[(sd, 16)])
                ph.add("vector",
                       lambda e, bfi=bfi: e.tensor_add(fyw[:], vtl[bfi][:], ytl[bfi][:]),
                       waits=[sd, ss, ob], incs=[(sv, 1)])
                emit_stats(ph, fyw, sqscr, 0, 4, cv[:, 0:1])
                ph.add("vector",
                       lambda e: e.tensor_scalar(fyw[:], fyw[:], stat[:, 0:1],
                                                 stat[:, 4:5], ALU.add, ALU.mult),
                       waits=[(vt, vt.n), (sce, sce.n)], incs=[(sv, 1)])
                ph.add("vector",
                       lambda e, bfi=bfi: e.tensor_add(otl[:], x1t[bfi][:], fyw[:]),
                       incs=[(sv, 1)])
                ph.add("sync",
                       lambda e, row=row: e.dma_start(out=out_d[row:row + 128, :],
                                                      in_=otl[:]),
                       waits=[sv], incs=[(ob, 16)])
            for (o, t) in dbg_copies:
                ph.add("sync", lambda e, o=o, t=t: e.dma_start(out=o, in_=t),
                       waits=bw(), incs=[(ob, 16)])
            ph.add("sync", lambda e: e.nop(), waits=[(ob, ob.n)])
            ph.emit(nc)

    return nc


_NC_CACHE = {}


def _get_nc(debug=False):
    key = bool(debug)
    if key not in _NC_CACHE:
        _NC_CACHE[key] = build_program(debug=key)
    return _NC_CACHE[key]


def _stage(inputs):
    f32 = np.float32
    x = np.asarray(inputs["L0_attn_x"], f32)
    nw = np.asarray(inputs["L0_attn_norm_w"], f32)
    Wx = np.asarray(inputs["L0_attn_Wx"], f32)
    bx = np.asarray(inputs["L0_attn_bx"], f32)
    Wz = np.asarray(inputs["L0_attn_Wz"], f32)
    bz = np.asarray(inputs["L0_attn_bz"], f32)
    convw = np.asarray(inputs["L0_attn_conv_w"], f32).reshape(DI, 4)
    convb = np.asarray(inputs["L0_attn_conv_b"], f32)
    Wxp = np.asarray(inputs["L0_attn_Wxp"], f32)
    Wdt = np.asarray(inputs["L0_attn_Wdt"], f32)
    bdt = np.asarray(inputs["L0_attn_bdt"], f32)
    Alog = np.asarray(inputs["L0_attn_A_log"], f32)
    Dp = np.asarray(inputs["L0_attn_D"], f32)
    Wout = np.asarray(inputs["L0_attn_Wout"], f32)
    bout = np.asarray(inputs["L0_attn_bout"], f32)
    Wg = np.asarray(inputs["L0_ffn_Wg"], f32)
    gamma = np.asarray(inputs["L0_ffn_gamma"], f32)
    beta = np.asarray(inputs["L0_ffn_beta"], f32)
    W1 = np.ascontiguousarray(np.asarray(inputs["L0_ffn_W1"], f32))
    b1 = np.asarray(inputs["L0_ffn_b1"], f32)
    W2 = np.ascontiguousarray(np.asarray(inputs["L0_ffn_W2"], f32))
    b2 = np.ascontiguousarray(np.asarray(inputs["L0_ffn_b2"], f32))

    eye = np.eye(128, dtype=f32)
    constv = np.ascontiguousarray(
        np.tile(np.array([EPS_A, EPS_F, 1.0, -1.0, -0.5, 1.0 / D, 0.0, 0.0],
                         dtype=f32), (128, 1)))
    wg_s = np.ascontiguousarray(Wg.reshape(8, 128, E).transpose(1, 0, 2).reshape(128, 8 * E))
    gamma_s = np.ascontiguousarray(gamma.reshape(E, 8, 128).transpose(2, 0, 1).reshape(128, E * 8))
    beta_s = np.ascontiguousarray(beta.reshape(E, 8, 128).transpose(2, 0, 1).reshape(128, E * 8))
    b1_s = np.ascontiguousarray(b1.reshape(E, 32, 128).transpose(2, 0, 1).reshape(128, E * 32))

    def chparam(vec, p):
        hh = vec[p * CH:(p + 1) * CH]
        return np.ascontiguousarray(hh.reshape(NJ, 128).T)

    per_par = {}
    for p in range(2):
        chs = slice(p * CH, (p + 1) * CH)
        per_par[p] = {
            "wxt_w": np.ascontiguousarray(Wx[chs, :].T),
            "wzt_w": np.ascontiguousarray(Wz[chs, :].T),
            "bx": chparam(bx, p),
            "bz": chparam(bz, p),
            "convw": np.ascontiguousarray(
                convw[chs].reshape(NJ, 128, 4).transpose(1, 0, 2).reshape(128, NJ * 4)),
            "convb": chparam(convb, p),
            "wxpt_w": np.ascontiguousarray(Wxp[:, chs].T),
            "wdtt_w": np.ascontiguousarray(Wdt[chs, :].T),
            "bdt": chparam(bdt, p),
            "alog": np.ascontiguousarray(
                Alog[chs].reshape(NJ, 128, N16).transpose(1, 0, 2).reshape(128, NJ * N16)),
            "dp": chparam(Dp, p),
            "woutt_w": np.ascontiguousarray(Wout[:, chs].T),
        }

    in_maps = []
    for c in range(8):
        b, p = c // 2, c % 2
        im = {
            "x_full": np.ascontiguousarray(x[b]),
            "x_half": np.ascontiguousarray(x[b, p * TH:(p + 1) * TH]),
            "eye_f": eye,
            "constv": constv,
            "norm_w": nw,
            "bout": bout,
            "wg": wg_s,
            "gamma": gamma_s,
            "beta": beta_s,
            "w1": W1,
            "b1": b1_s,
            "w2": W2,
            "b2": b2,
        }
        im.update(per_par[p])
        in_maps.append(im)
    return in_maps


def kernel(**inputs):
    nc = _get_nc(debug=False)
    in_maps = _stage(inputs)
    res = run_bass_kernel_spmd(nc, in_maps, core_ids=list(range(8)))
    out = np.empty((Bq, L, D), np.float32)
    for c in range(8):
        b, p = c // 2, c % 2
        out[b, p * TH:(p + 1) * TH] = res.results[c]["out"]
    return out


# version 14
# speedup vs baseline: 1.0781x; 1.0630x over previous
import numpy as np
from contextlib import ExitStack

import concourse.bass_utils as _bu

if not getattr(_bu, "_ldw_opt_patched", False):
    _orig_run_command = _bu.run_command

    def _run_command_ldw(argv, **kwargs):
        argv = ["--enable-ldw-opt=true" if a == "--enable-ldw-opt=false" else a
                for a in argv]
        return _orig_run_command(argv, **kwargs)

    _bu.run_command = _run_command_ldw
    _bu._ldw_opt_patched = True

import concourse.bass as bass
import concourse.mybir as mybir
from concourse.bass_utils import run_bass_kernel_spmd

F32 = mybir.dt.float32
BF16 = mybir.dt.bfloat16
AF = mybir.ActivationFunctionType
ALU = mybir.AluOpType
AX = mybir.AxisListType

Bq, L, D = 4, 8192, 1024
DI = 2048
CH = DI // 2
NJ = CH // 128
N16 = 16
R = 64
RR = 96
TH = L // 2
E, F = 8, 4096
EPS_A = 1e-5
EPS_F = 1e-6

CA = 2048
NCA = L // CA
QB = 512
NCB = L // QB
TS = 256
NTS = TH // TS
NFB = 4

PAIRS = [[0, 1], [2, 3], [4, 5], [6, 7]]
ENGINES = ("sync", "gpsimd", "vector", "scalar", "tensor")


class Sem:
    def __init__(self, nc, es, name):
        self.h = es.enter_context(nc.semaphore(name))
        self.n = 0


class Phase:
    def __init__(self):
        self.ops = {e: [] for e in ENGINES}

    def add(self, eng, fn, waits=(), incs=()):
        w = []
        for x in waits:
            if isinstance(x, tuple):
                w.append((x[0].h, x[1]))
            else:
                w.append((x.h, x.n))
        ii = []
        assert len(list(incs)) <= 1, "one sync update per instruction"
        for (s, k) in incs:
            if k is None:
                s.n += 1
                ii.append((s.h, None))
            else:
                s.n += k
                ii.append((s.h, k))
        self.ops[eng].append((fn, w, ii))

    def emit(self, nc):
        with nc.Block() as blk:
            for eng in ENGINES:
                ops = self.ops[eng]
                if not ops:
                    continue

                def mk(ops):
                    def body(e):
                        for (fn, w, ii) in ops:
                            for (h, c) in w:
                                if c > 0:
                                    e.wait_ge(h, c)
                            instr = fn(e)
                            for (h, k) in ii:
                                if k is None:
                                    instr.then_inc(h)
                                else:
                                    instr.then_inc(h, k)
                    return body

                getattr(blk, eng)(mk(ops))


def build_program(debug=False):
    nc = bass.Bass("TRN2", target_bir_lowering=False)

    def din(name, shape, dt=F32):
        return nc.dram_tensor(name, shape, dt, kind="ExternalInput").ap()

    x_full = din("x_full", [L, D])
    x_half = din("x_half", [TH, D])
    eye_f = din("eye_f", [128, 128])
    constv_d = din("constv", [128, 8])
    norm_w = din("norm_w", [D])
    wxt_d = din("wxt_w", [D, CH])
    wzt_d = din("wzt_w", [D, CH])
    bx_d = din("bx", [128, NJ])
    bz_d = din("bz", [128, NJ])
    convw_d = din("convw", [128, NJ * 4])
    convb_d = din("convb", [128, NJ])
    wxpt_d = din("wxpt_w", [CH, RR])
    wdtt_d = din("wdtt_w", [R, CH])
    bdt_d = din("bdt", [128, NJ])
    alog_d = din("alog", [128, NJ * N16])
    dp_d = din("dp", [128, NJ])
    woutt_d = din("woutt_w", [CH, D])
    bout_d = din("bout", [D])
    wg_d = din("wg", [128, 8 * E])
    gamma_d = din("gamma", [128, E * 8])
    beta_d = din("beta", [128, E * 8])
    w1_d = din("w1", [E, D, F])
    b1_d = din("b1", [128, E * (F // 128)])
    w2_d = din("w2", [E, F, D])
    b2_d = din("b2", [E, D])

    out_d = nc.dram_tensor("out", [TH, D], F32, kind="ExternalOutput").ap()

    dbg_copies = []

    def scr(name, shape, dt=F32, dbg=False):
        t = nc.dram_tensor(name, shape, dt, kind="Internal").ap()
        if debug and dbg:
            o = nc.dram_tensor("dbg_" + name, shape, dt, kind="ExternalOutput").ap()
            dbg_copies.append((o, t))
        return t

    u_dram = scr("u_dram", [L, D], dbg=True)
    xb2_dram = scr("xb2_dram", [CH, L], BF16, dbg=True)
    zs_dram = scr("zs_dram", [CH, L], BF16)
    xdbl_in = scr("xdbl_in", [RR, L])
    xdbl_out = scr("xdbl_out", [RR, L], dbg=True)
    rs_in = scr("rs_in", [L, D])
    rs_out = scr("rs_out", [TH, D], dbg=True)
    x1_dram = scr("x1_dram", [TH, D], dbg=True)
    v_dram = scr("v_dram", [TH, D])
    ymoe_dram = scr("ymoe_dram", [D, TH], dbg=True)
    xhatT_dram = scr("xhatT_dram", [128, 8 * TH], BF16)
    nwrep_dram = scr("nwrep_dram", [128, D])
    boutrep_dram = scr("boutrep_dram", [128, D])
    w1b_dram = scr("w1b_dram", [E, D, F], BF16)
    w2b_dram = scr("w2b_dram", [E, F, D], BF16)

    es = ExitStack()
    with es:
        sd = Sem(nc, es, "sd")
        gd = Sem(nc, es, "gd")
        sv = Sem(nc, es, "sv")
        ss = Sem(nc, es, "ss")
        st = Sem(nc, es, "st")
        c1 = Sem(nc, es, "c1")
        c2 = Sem(nc, es, "c2")
        ob = Sem(nc, es, "ob")
        vt = Sem(nc, es, "vt")
        gc = Sem(nc, es, "gc")
        s_pc = Sem(nc, es, "s_pc")
        g_pc = Sem(nc, es, "g_pc")
        s_usp = Sem(nc, es, "s_usp")
        s_zsp = Sem(nc, es, "s_zsp")
        s_xsp = Sem(nc, es, "s_xsp")
        s_xt = Sem(nc, es, "s_xt")
        g_xb = Sem(nc, es, "g_xb")
        s_bl = Sem(nc, es, "s_bl")
        s_nt = Sem(nc, es, "s_nt")
        s_cw = Sem(nc, es, "s_cw")
        s_yio = Sem(nc, es, "s_yio")
        w_w1 = Sem(nc, es, "w_w1")
        w_w2 = Sem(nc, es, "w_w2")
        s_ue = Sem(nc, es, "s_ue")
        sce = Sem(nc, es, "sce")

        ALLS = (sd, gd, sv, ss, st)

        def bw():
            return [sd, gd, sv, ss, st, s_pc, g_pc, s_usp, s_zsp, s_xsp, s_xt,
                    g_xb, s_bl, s_nt, s_cw, s_yio, w_w1, w_w2, s_ue]

        def sb(stk, name, shape, dt=F32):
            return stk.enter_context(nc.sbuf_tensor(name, shape, dt))

        def ps(stk, name, shape, dt=F32):
            return stk.enter_context(nc.psum_tensor(name, shape, dt))

        eyeb = sb(es, "eyeb", [128, 128], BF16)
        eyef = sb(es, "eyef", [128, 128], F32)
        onesb = sb(es, "onesb", [33, 128], BF16)
        onesf = sb(es, "onesf", [1, 128], F32)
        bx_sb = sb(es, "bx_sb", [128, NJ], F32)
        bz_sb = sb(es, "bz_sb", [128, NJ], F32)
        convb_sb = sb(es, "convb_sb", [128, NJ], F32)
        bdt_sb = sb(es, "bdt_sb", [128, NJ], F32)
        dp_sb = sb(es, "dp_sb", [128, NJ], F32)
        convw_sb = sb(es, "convw_sb", [128, NJ * 4], F32)
        a_sb = sb(es, "a_sb", [128, NJ * N16], F32)
        hstate = sb(es, "hstate", [128, NJ * N16], F32)
        cv = sb(es, "cv", [128, 8], F32)
        stat = sb(es, "stat", [128, 16], F32)

        def emit_stats(ph, srcbuf, sq, mcol, vcol, eps_ap, rms=False):
            if not rms:
                ph.add("vector",
                       lambda e: e.reduce_sum(stat[:, mcol:mcol + 1], srcbuf[:],
                                              axis=AX.X, negate=True),
                       incs=[(vt, 1)])
            ph.add("scalar",
                   lambda e: e.activation(sq[:], srcbuf[:], AF.Square,
                                          accum_out=stat[:, vcol:vcol + 1]),
                   waits=[(vt, vt.n)] if not rms else [sv], incs=[(sce, 1)])
            if rms:
                ph.add("scalar",
                       lambda e: e.activation(stat[:, vcol:vcol + 1],
                                              stat[:, vcol:vcol + 1], AF.Ln,
                                              scale=cv[:, 5:6], bias=eps_ap),
                       waits=[(sce, sce.n)], incs=[(sce, 1)])
                ph.add("scalar",
                       lambda e: e.activation(stat[:, vcol:vcol + 1],
                                              stat[:, vcol:vcol + 1], AF.Exp,
                                              scale=cv[:, 4:5]),
                       waits=[(sce, sce.n)], incs=[(sce, 1)])
                return
            ph.add("vector",
                   lambda e: e.tensor_scalar_mul(stat[:, mcol:mcol + 1],
                                                 stat[:, mcol:mcol + 1], 1.0 / D),
                   waits=[(vt, vt.n)], incs=[(vt, 1)])
            ph.add("vector",
                   lambda e: e.tensor_scalar_mul(stat[:, 14:15],
                                                 stat[:, vcol:vcol + 1], 1.0 / D),
                   waits=[(sce, sce.n)], incs=[(vt, 1)])
            ph.add("vector",
                   lambda e: e.tensor_mul(stat[:, 15:16], stat[:, mcol:mcol + 1],
                                          stat[:, mcol:mcol + 1]),
                   waits=[(vt, vt.n)], incs=[(vt, 1)])
            ph.add("vector",
                   lambda e: e.tensor_sub(stat[:, vcol:vcol + 1], stat[:, 14:15],
                                          stat[:, 15:16]),
                   waits=[(vt, vt.n)], incs=[(vt, 1)])
            ph.add("scalar",
                   lambda e: e.activation(stat[:, vcol:vcol + 1],
                                          stat[:, vcol:vcol + 1], AF.Ln, bias=eps_ap),
                   waits=[(vt, vt.n)], incs=[(sce, 1)])
            ph.add("scalar",
                   lambda e: e.activation(stat[:, vcol:vcol + 1],
                                          stat[:, vcol:vcol + 1], AF.Exp,
                                          scale=cv[:, 4:5]),
                   waits=[(sce, sce.n)], incs=[(sce, 1)])

        es0 = ExitStack()
        nwrow = sb(es0, "nwrow", [1, D], F32)
        boutrow = sb(es0, "boutrow", [1, D], F32)
        pbc = ps(es0, "pbc", [128, 1024], F32)
        ph = Phase()

        def ld(dst, src):
            ph.add("sync", lambda e, d=dst, s=src: e.dma_start(out=d, in_=s),
                   incs=[(sd, 16)])

        ld(eyef[:], eye_f)
        ld(cv[:], constv_d)
        ld(nwrow[:], norm_w.rearrange("(a d) -> a d", a=1))
        ld(boutrow[:], bout_d.rearrange("(a d) -> a d", a=1))
        nw_rep = sb(es0, "nw_rep", [128, D], F32)
        bout_rep_s = sb(es0, "bout_rep_s", [128, D], F32)
        ld(bx_sb[:], bx_d)
        ld(bz_sb[:], bz_d)
        ld(convb_sb[:], convb_d)
        ld(bdt_sb[:], bdt_d)
        ld(dp_sb[:], dp_d)
        ld(convw_sb[:], convw_d)
        ld(a_sb[:], alog_d)
        ph.add("gpsimd", lambda e: e.dma_start(out=eyeb[:], in_=eye_f),
               incs=[(gd, 16)])
        ph.add("gpsimd", lambda e: e.memset(onesb[:], 1.0), incs=[(gd, 1)])
        ph.add("gpsimd", lambda e: e.memset(onesf[:], 1.0), incs=[(gd, 1)])
        ph.add("scalar", lambda e: e.activation(a_sb[:], a_sb[:], AF.Exp),
               waits=[sd], incs=[(ss, 1)])
        ph.add("vector", lambda e: e.tensor_scalar_mul(a_sb[:], a_sb[:], -1.0),
               waits=[ss], incs=[(sv, 1)])
        ph.add("vector", lambda e: e.memset(hstate[:], 0.0), incs=[(sv, 1)])
        for h in range(2):
            ph.add("tensor",
                   lambda e, h=h: e.matmul(pbc[:, h * 512:(h + 1) * 512], onesf[:],
                                           nwrow[0:1, h * 512:(h + 1) * 512],
                                           start=True, stop=True),
                   waits=([sd, gd] if h == 0 else []), incs=[(st, 1)])
        ph.add("scalar", lambda e: e.copy(nw_rep[:], pbc[:]), waits=[st],
               incs=[(ss, 1)])
        for h in range(2):
            ph.add("tensor",
                   lambda e, h=h: e.matmul(pbc[:, h * 512:(h + 1) * 512], onesf[:],
                                           boutrow[0:1, h * 512:(h + 1) * 512],
                                           start=True, stop=True),
                   waits=([ss] if h == 0 else []), incs=[(st, 1)])
        ph.add("scalar", lambda e: e.copy(bout_rep_s[:], pbc[:]), waits=[st],
               incs=[(ss, 1)])
        ph.add("sync", lambda e: e.dma_start(out=nwrep_dram, in_=nw_rep[:]),
               waits=[ss], incs=[(sd, 16)])
        ph.add("sync", lambda e: e.dma_start(out=boutrep_dram, in_=bout_rep_s[:]),
               waits=[ss], incs=[(sd, 16)])
        ph.emit(nc)
        es0.close()

        esA = ExitStack()
        with esA:
            wxt = sb(esA, "wxt", [128, 8 * CH], BF16)
            wzt = sb(esA, "wzt", [128, 8 * CH], BF16)
            wxpt = sb(esA, "wxpt", [128, NJ * RR], BF16)
            xnT = sb(esA, "xnT", [128, 8 * CA], BF16)
            xb2k = sb(esA, "xb2k", [128, NJ * CA], BF16)
            zstile = sb(esA, "zstile", [128, CA], BF16)
            xbw = sb(esA, "xbw", [128, CA + 3], F32)
            xcw = sb(esA, "xcw", [128, CA], F32)
            xbtail = sb(esA, "xbtail", [128, NJ * 3], F32)
            xdblp = sb(esA, "xdblp", [96, L], F32)
            xtile = [sb(esA, f"xtile{i}", [128, D], F32) for i in range(2)]
            sqscr = sb(esA, "sqscrA", [128, D], F32)
            nw_rep = sb(esA, "nw_repA", [128, D], F32)
            utile = sb(esA, "utile", [128, D], F32)
            xntile = sb(esA, "xntile", [128, D], BF16)
            ptr = ps(esA, "ptr", [128, 1024], BF16)
            pxb2 = [ps(esA, f"pxb{i}", [128, 512], F32) for i in range(2)]
            pz2 = [ps(esA, f"pz{i}", [128, 512], F32) for i in range(2)]
            pxd = ps(esA, "pxd", [96, 512], F32)

            wcast = [sb(esA, f"wcast{i}", [128, 4096], BF16) for i in range(2)]
            gd_pc = {}
            sd_pc = {}
            pc_steps = []
            for e_i in range(E):
                for s8 in range(8):
                    pc_steps.append(("w1", e_i, s8 * 128))
                for s8 in range(8):
                    pc_steps.append(("w2", e_i, s8 * 512))

            ph = Phase()

            def emit_precast(step_i):
                tag, e_i, r0 = pc_steps[step_i]
                wc = wcast[step_i % 2]
                if tag == "w1":
                    src_ap = w1_d[e_i, r0:r0 + 128, :]
                    dst_ap = w1b_dram[e_i, r0:r0 + 128, :]
                    wview_i = wc[:]
                    wview_o = wc[:]
                else:
                    src_ap = w2_d[e_i, r0:r0 + 512, :].rearrange(
                        "(a p) q -> p a q", p=128)
                    dst_ap = w2b_dram[e_i, r0:r0 + 512, :].rearrange(
                        "(a p) q -> p a q", p=128)
                    wview_i = wc[:].rearrange("p (a q) -> p a q", a=4)
                    wview_o = wc[:].rearrange("p (a q) -> p a q", a=4)
                ph.add("gpsimd",
                       lambda e, s=src_ap, w=wview_i: e.dma_start(out=w, in_=s),
                       waits=[(s_pc, sd_pc.get(step_i - 2, 0))], incs=[(g_pc, 16)])
                gd_pc[step_i] = g_pc.n
                ph.add("sync",
                       lambda e, d=dst_ap, w=wview_o: e.dma_start(out=d, in_=w),
                       waits=[(g_pc, gd_pc[step_i])], incs=[(s_pc, 16)])
                sd_pc[step_i] = s_pc.n

            ph.add("sync", lambda e: e.dma_start(out=nw_rep[:], in_=nwrep_dram),
                   waits=[sd], incs=[(sd, 16)])
            for a in range(8):
                ph.add("gpsimd",
                       lambda e, a=a: e.dma_start(
                           out=wxt[:, a * CH:(a + 1) * CH],
                           in_=wxt_d[a * 128:(a + 1) * 128, :]),
                       incs=[(gd, 16)])
                ph.add("gpsimd",
                       lambda e, a=a: e.dma_start(
                           out=wzt[:, a * CH:(a + 1) * CH],
                           in_=wzt_d[a * 128:(a + 1) * 128, :]),
                       incs=[(gd, 16)])
            for a in range(NJ):
                ph.add("gpsimd",
                       lambda e, a=a: e.dma_start(
                           out=wxpt[:, a * RR:(a + 1) * RR],
                           in_=wxpt_d[a * 128:(a + 1) * 128, :]),
                       incs=[(gd, 16)])
            gd_wA = gd.n

            usp = {}
            zsp = {}
            xsp = {}
            for ch in range(NCA):
                t0 = ch * CA
                for _pc in range(32):
                    si = ch * 32 + _pc
                    if si < len(pc_steps):
                        emit_precast(si)
                for tt in range(16):
                    xt = xtile[tt % 2]
                    row = t0 + tt * 128
                    ph.add("sync",
                           lambda e, xt=xt, row=row: e.dma_start(
                               out=xt[:], in_=x_full[row:row + 128, :]),
                           waits=[sv, ss], incs=[(s_xt, 16)])
                    sd_xt = s_xt.n
                    ph.add("vector", lambda e: e.engine_nop(), waits=[(s_xt, sd_xt)])
                    emit_stats(ph, xt, sqscr, 0, 4, cv[:, 0:1])
                    ph.add("vector",
                           lambda e, xt=xt: e.tensor_scalar(utile[:], xt[:], stat[:, 0:1],
                                                            stat[:, 4:5], ALU.add, ALU.mult),
                           waits=[(s_usp, usp.get(0, 0)), (vt, vt.n), (sce, sce.n)],
                           incs=[(sv, 1)])
                    ph.add("sync",
                           lambda e, row=row: e.dma_start(out=u_dram[row:row + 128, :],
                                                          in_=utile[:]),
                           waits=[sv], incs=[(s_usp, 16)])
                    usp[0] = s_usp.n
                    emit_stats(ph, utile, sqscr, 0, 6, cv[:, 0:1], rms=True)
                    ph.add("vector",
                           lambda e: e.scalar_tensor_tensor(xntile[:], utile[:], stat[:, 6:7],
                                                            nw_rep[:], ALU.mult, ALU.mult),
                           waits=[st, (sce, sce.n)], incs=[(sv, 1)])
                    for a in range(8):
                        ph.add("tensor",
                               lambda e, a=a: e.transpose(ptr[:, a * 128:(a + 1) * 128],
                                                          xntile[:, a * 128:(a + 1) * 128],
                                                          eyeb[:]),
                               waits=([sv, ss] if a == 0 else []),
                               incs=([(st, 1)] if a == 7 else []))
                    ph.add("scalar",
                           lambda e, tt=tt: e.copy(
                               xnT[:].rearrange("p (a t) -> p a t", a=8)[:, :, tt * 128:(tt + 1) * 128],
                               ptr[:].rearrange("p (a t) -> p a t", a=8)),
                           waits=[st], incs=[(ss, 1)])

                sv_xnt = sv.n
                ss_xnt = ss.n
                xbev = {0: ss_xnt, 1: ss_xnt}
                zev = {0: ss_xnt, 1: ss_xnt}
                tail_vt = {}
                for j in range(NJ):
                    if ch == 0:
                        ph.add("vector", lambda e: e.memset(xbw[:, 0:3], 0.0),
                               waits=[ss, st], incs=[(vt, 1)])
                        tail_vt[j] = vt.n
                    else:
                        ph.add("vector",
                               lambda e, j=j: e.tensor_copy(xbw[:, 0:3],
                                                            xbtail[:, j * 3:(j + 1) * 3]),
                               waits=[ss, st], incs=[(vt, 1)])
                    tail_vt[j] = vt.n
                    for ts4 in range(4):
                        par = ts4 % 2
                        for a in range(8):
                            ph.add("tensor",
                                   lambda e, a=a, j=j, ts4=ts4, par=par: e.matmul(
                                       pxb2[par][:],
                                       wxt[:, a * CH + j * 128: a * CH + (j + 1) * 128],
                                       xnT[:, a * CA + ts4 * 512: a * CA + (ts4 + 1) * 512],
                                       start=(a == 0), stop=(a == 7)),
                                   waits=([(ss, xbev.get(par, 0)), (gd, gd_wA),
                                           (sv, sv_xnt)] if a == 0 else []),
                                   incs=([(st, 1)] if a == 7 else []))
                        xbmm = st.n
                        for a in range(8):
                            ph.add("tensor",
                                   lambda e, a=a, j=j, ts4=ts4, par=par: e.matmul(
                                       pz2[par][:],
                                       wzt[:, a * CH + j * 128: a * CH + (j + 1) * 128],
                                       xnT[:, a * CA + ts4 * 512: a * CA + (ts4 + 1) * 512],
                                       start=(a == 0), stop=(a == 7)),
                                   waits=([(ss, zev.get(par, 0))] if a == 0 else []),
                                   incs=([(st, 1)] if a == 7 else []))
                        zmm = st.n
                        ph.add("scalar",
                               lambda e, j=j, ts4=ts4, par=par: e.activation(
                                   xbw[:, 3 + ts4 * 512: 3 + (ts4 + 1) * 512],
                                   pxb2[par][:],
                                   AF.Identity, bias=bx_sb[:, j:j + 1]),
                               waits=[(st, xbmm), (vt, tail_vt.get(j, 0))],
                               incs=[(ss, 1)])
                        xbev[par] = ss.n
                        ph.add("scalar",
                               lambda e, ts4=ts4, j=j, par=par: e.activation(
                                   zstile[:, ts4 * 512:(ts4 + 1) * 512], pz2[par][:],
                                   AF.Silu,
                                   bias=bz_sb[:, j:j + 1]),
                               waits=[(st, zmm), (s_zsp, zsp.get(0, 0))],
                               incs=[(ss, 1)])
                        zev[par] = ss.n
                    ph.add("sync",
                           lambda e, j=j, t0=t0: e.dma_start(
                               out=zs_dram[j * 128:(j + 1) * 128, t0:t0 + CA],
                               in_=zstile[:]),
                           waits=[ss], incs=[(s_zsp, 16)])
                    zsp[0] = s_zsp.n
                    ph.add("vector",
                           lambda e, j=j: e.tensor_copy(xbtail[:, j * 3:(j + 1) * 3],
                                                        xbw[:, CA:CA + 3]),
                           waits=[ss], incs=[(sv, 1)])
                    ph.add("vector",
                           lambda e, j=j: e.tensor_scalar_mul(
                               xcw[:], xbw[:, 0:CA],
                               convw_sb[:, j * 4:j * 4 + 1]),
                           waits=[(vt, vt.n)], incs=[(sv, 1)])
                    for k in range(1, 4):
                        ph.add("vector",
                               lambda e, j=j, k=k: e.scalar_tensor_tensor(
                                   xcw[:], xbw[:, k:k + CA],
                                   convw_sb[:, j * 4 + k:j * 4 + k + 1],
                                   xcw[:], ALU.mult, ALU.add),
                               incs=([(sv, 1)] if k == 3 else []))
                    ph.add("scalar",
                           lambda e, j=j: e.activation(
                               xb2k[:, j * CA:(j + 1) * CA], xcw[:], AF.Silu,
                               bias=convb_sb[:, j:j + 1]),
                           waits=[sv, st, (s_xsp, xsp.get(j, 0))], incs=[(ss, 1)])
                    ph.add("sync",
                           lambda e, j=j, t0=t0: e.dma_start(
                               out=xb2_dram[j * 128:(j + 1) * 128, t0:t0 + CA],
                               in_=xb2k[:, j * CA:(j + 1) * CA]),
                           waits=[ss], incs=[(s_xsp, 16)])
                    xsp[j] = s_xsp.n

                for ts4 in range(4):
                    for j in range(NJ):
                        ph.add("tensor",
                               lambda e, j=j, ts4=ts4: e.matmul(
                                   pxd[:],
                                   wxpt[:, j * RR:(j + 1) * RR],
                                   xb2k[:, j * CA + ts4 * 512: j * CA + (ts4 + 1) * 512],
                                   start=(j == 0), stop=(j == NJ - 1)),
                               waits=([ss, sv] if j == 0 else []),
                               incs=([(st, 1)] if j == NJ - 1 else []))
                    ph.add("scalar",
                           lambda e, ts4=ts4, t0=t0: e.copy(
                               xdblp[0:96, t0 + ts4 * 512: t0 + (ts4 + 1) * 512], pxd[:]),
                           waits=[st], incs=[(ss, 1)])

            ph.add("sync", lambda e: e.dma_start(out=xdbl_in, in_=xdblp[0:96, :]),
                   waits=[ss], incs=[(sd, 16)])
            ph.emit(nc)

        ph = Phase()
        ph.add("gpsimd",
               lambda e: e.collective_compute(
                   "AllReduce", ALU.add, replica_groups=PAIRS,
                   ins=[xdbl_in.opt()], outs=[xdbl_out.opt()]),
               waits=bw(), incs=[(c1, None)])
        ph.emit(nc)

        esB = ExitStack()
        with esB:
            wdtt = sb(esB, "wdtt", [64, CH], BF16)
            woutt = sb(esB, "woutt", [128, NJ * D], BF16)
            dlowb = sb(esB, "dlowb", [64, QB], BF16)
            bc1 = sb(esB, "bc1", [33, 16 * QB], BF16)
            dt_sb = sb(esB, "dt_sb", [128, NJ * QB], F32)
            u_sb = sb(esB, "u_sb", [128, NJ * QB], F32)
            xb2c = sb(esB, "xb2c", [128, NJ * QB], BF16)
            zsc = sb(esB, "zsc", [128, NJ * QB], BF16)
            dA = sb(esB, "dA", [128, NJ * QB], F32)
            dA_b = sb(esB, "dA_b", [128, NJ * QB], F32)
            dBx = sb(esB, "dBx", [128, NJ * QB], F32)
            dBx_b = sb(esB, "dBx_b", [128, NJ * QB], F32)
            bbsb = sb(esB, "bbsb", [128, 2 * QB], F32)
            h_n = sb(esB, "h_n", [128, NJ * QB], F32)
            y_sb = sb(esB, "y_sb", [128, NJ * QB], F32)
            yz_bf = sb(esB, "yz_bf", [128, NJ * QB], BF16)
            uev0 = sb(esB, "uev0", [128, D], F32)
            uev = [uev0, uev0]
            outp = [sb(esB, f"outp{i}", [128, D], F32) for i in range(2)]
            tabs = h_n[:, 0:QB]
            tln = h_n[:, QB:2 * QB]
            pdt = ps(esB, "pdt", [128, 512], F32)
            pb_b = ps(esB, "pb_b", [128, 512], F32)
            pb_c = ps(esB, "pb_c", [128, 512], F32)
            pb_b2 = ps(esB, "pb_b2", [128, 512], F32)
            pb_c2 = ps(esB, "pb_c2", [128, 512], F32)
            po = ps(esB, "po", [128, 1024], F32)

            ph = Phase()
            for a in range(NJ):
                ph.add("gpsimd",
                       lambda e, a=a: e.dma_start(
                           out=woutt[:, a * D:(a + 1) * D],
                           in_=woutt_d[a * 128:(a + 1) * 128, :]),
                       incs=[(gd, 16)])
            ph.add("gpsimd", lambda e: e.dma_start(out=wdtt[:], in_=wdtt_d),
                   incs=[(gd, 16)])

            dA2 = [dA, dA_b]
            pbb = [pb_b, pb_b2]
            pbcc = [pb_c, pb_c2]
            yadd_done = {-2: 0, -1: 0}
            bcast_ok = {-2: 0, -1: 0}
            for cb in range(NCB):
                t0 = cb * QB
                ph.add("gpsimd",
                       lambda e, t0=t0: e.dma_start(out=dlowb[:],
                                                    in_=xdbl_out[0:64, t0:t0 + QB]),
                       waits=[(c1, 1), st, ss], incs=[(g_xb, 16)])
                ph.add("gpsimd",
                       lambda e, t0=t0: e.dma_start(
                           out=bc1[0:1, :].rearrange("q (n t) -> q n t", n=16),
                           in_=xdbl_out[64:80, t0:t0 + QB]),
                       waits=[(c1, 1), st], incs=[(g_xb, 16)])
                ph.add("gpsimd",
                       lambda e, t0=t0: e.dma_start(
                           out=bc1[32:33, :].rearrange("q (n t) -> q n t", n=16),
                           in_=xdbl_out[80:96, t0:t0 + QB]),
                       waits=[(c1, 1), st], incs=[(g_xb, 16)])
                for j in range(NJ):
                    ph.add("sync",
                           lambda e, t0=t0, j=j: e.dma_start(
                               out=xb2c[:, j * QB:(j + 1) * QB],
                               in_=xb2_dram[j * 128:(j + 1) * 128, t0:t0 + QB]),
                           waits=([sv, (s_xsp, s_xsp.n), (s_zsp, s_zsp.n)]
                                  if j == 0 else []),
                           incs=[(s_bl, 16)])
                    ph.add("sync",
                           lambda e, t0=t0, j=j: e.dma_start(
                               out=zsc[:, j * QB:(j + 1) * QB],
                               in_=zs_dram[j * 128:(j + 1) * 128, t0:t0 + QB]),
                           incs=[(s_bl, 16)])

                gd_cb = g_xb.n
                for j in range(NJ):
                    ph.add("tensor",
                           lambda e, j=j: e.matmul(pdt[:, 0:QB],
                                                   wdtt[:, j * 128:(j + 1) * 128],
                                                   dlowb[:], start=True, stop=True),
                           waits=([(g_xb, gd_cb), ss, sv] if j == 0 else [ss]),
                           incs=[(st, 1)])
                    ph.add("scalar",
                           lambda e, j=j: e.activation(tabs, pdt[:, 0:QB], AF.Abs,
                                                       bias=bdt_sb[:, j:j + 1]),
                           waits=[st])
                    ph.add("scalar",
                           lambda e: e.activation(tabs, tabs, AF.Exp, scale=cv[:, 3:4]))
                    ph.add("scalar",
                           lambda e: e.activation(tln, tabs, AF.Ln, bias=cv[:, 2:3]))
                    ph.add("scalar",
                           lambda e, j=j: e.activation(tabs, pdt[:, 0:QB], AF.Relu,
                                                       bias=bdt_sb[:, j:j + 1]),
                           incs=[(ss, 1)])
                    ph.add("vector",
                           lambda e, j=j: e.tensor_add(dt_sb[:, j * QB:(j + 1) * QB],
                                                       tabs, tln),
                           waits=[ss], incs=[(sv, 1)])
                    ph.add("vector",
                           lambda e, j=j: e.tensor_mul(u_sb[:, j * QB:(j + 1) * QB],
                                                       dt_sb[:, j * QB:(j + 1) * QB],
                                                       xb2c[:, j * QB:(j + 1) * QB]),
                           waits=[(s_bl, s_bl.n)], incs=[(sv, 1)])
                sv_u = sv.n
                sv_u_vt = vt.n

                dBx2 = [dBx, dBx_b]
                scan_done = {-2: sv_u_vt, -1: sv_u_vt}
                cm_done = {-2: sv_u, -1: sv_u}
                yadd_done = {-2: sv_u, -1: sv_u}
                for n in range(N16):
                    par = n % 2
                    ph.add("tensor",
                           lambda e, n=n, par=par: e.matmul(
                               pbb[par][:, 0:QB], onesb[0:1, :],
                               bc1[0:1, n * QB:(n + 1) * QB],
                               start=True, stop=True),
                           waits=[(sv, yadd_done[n - 2])], incs=[(st, 1)])
                    ph.add("tensor",
                           lambda e, n=n, par=par: e.matmul(
                               pbcc[par][:, 0:QB], onesb[32:33, :],
                               bc1[32:33, n * QB:(n + 1) * QB],
                               start=True, stop=True),
                           incs=[(st, 1)])
                    bcast_ok[n] = st.n
                    for j in range(NJ):
                        ph.add("scalar",
                               lambda e, j=j, n=n, par=par: e.activation(
                                   dA2[par][:, j * QB:(j + 1) * QB],
                                   dt_sb[:, j * QB:(j + 1) * QB], AF.Exp,
                                   scale=a_sb[:, j * N16 + n: j * N16 + n + 1]),
                               waits=([(sv, yadd_done[n - 2])] if j == 0 else []),
                               incs=([(ss, 1)] if j == NJ - 1 else []))
                    ss_dA = ss.n
                    for j in range(NJ):
                        ph.add("vector",
                               lambda e, j=j, par=par: e.tensor_mul(
                                   dBx2[par][:, j * QB:(j + 1) * QB],
                                   u_sb[:, j * QB:(j + 1) * QB],
                                   pbb[par][:, 0:QB]),
                               waits=([(st, bcast_ok[n]), (ss, ss_dA)] if j == 0 else []))
                    for j in range(NJ):
                        ph.add("vector",
                               lambda e, j=j, n=n, par=par: e.scalar_tensor_tensor(
                                   dBx2[par][:, j * QB:j * QB + 1],
                                   dA2[par][:, j * QB:j * QB + 1],
                                   hstate[:, j * N16 + n:j * N16 + n + 1],
                                   dBx2[par][:, j * QB:j * QB + 1], ALU.mult, ALU.add),
                               incs=([(vt, 1)] if j == NJ - 1 else []))
                    for j in range(NJ):
                        ph.add("vector",
                               lambda e, j=j, par=par: e.tensor_tensor_scan(
                                   h_n[:, j * QB:(j + 1) * QB],
                                   dA2[par][:, j * QB:(j + 1) * QB],
                                   dBx2[par][:, j * QB:(j + 1) * QB],
                                   0.0, ALU.mult, ALU.add),
                               waits=([(vt, vt.n)] if j == 0 else []),
                               incs=([(vt, 1)] if j == NJ - 1 else []))
                    scan_done[n] = vt.n
                    for j in range(NJ):
                        ph.add("vector",
                               lambda e, j=j, n=n: e.tensor_copy(
                                   hstate[:, j * N16 + n:j * N16 + n + 1],
                                   h_n[:, (j + 1) * QB - 1:(j + 1) * QB]),
                               waits=([(vt, vt.n)] if j == 0 else []))
                    for j in range(NJ):
                        ph.add("vector",
                               lambda e, j=j, par=par: e.tensor_mul(
                                   dA2[par][:, j * QB:(j + 1) * QB],
                                   h_n[:, j * QB:(j + 1) * QB],
                                   pbcc[par][:, 0:QB]))
                        if n == 0:
                            ph.add("vector",
                                   lambda e, j=j, par=par: e.tensor_copy(
                                       y_sb[:, j * QB:(j + 1) * QB],
                                       dA2[par][:, j * QB:(j + 1) * QB]),
                                   incs=([(sv, 1)] if j == NJ - 1 else []))
                        else:
                            ph.add("vector",
                                   lambda e, j=j, par=par: e.tensor_add(
                                       y_sb[:, j * QB:(j + 1) * QB],
                                       y_sb[:, j * QB:(j + 1) * QB],
                                       dA2[par][:, j * QB:(j + 1) * QB]),
                                   incs=([(sv, 1)] if j == NJ - 1 else []))
                    yadd_done[n] = sv.n
                    cm_done[n] = sv.n

                for j in range(NJ):
                    ph.add("vector",
                           lambda e, j=j: e.scalar_tensor_tensor(
                               y_sb[:, j * QB:(j + 1) * QB],
                               xb2c[:, j * QB:(j + 1) * QB],
                               dp_sb[:, j:j + 1],
                               y_sb[:, j * QB:(j + 1) * QB], ALU.mult, ALU.add))
                    ph.add("vector",
                           lambda e, j=j: e.tensor_mul(yz_bf[:, j * QB:(j + 1) * QB],
                                                       y_sb[:, j * QB:(j + 1) * QB],
                                                       zsc[:, j * QB:(j + 1) * QB]),
                           incs=[(sv, 1)])

                for tsub in range(QB // 128):
                    row = t0 + tsub * 128
                    bfi = tsub % 2
                    ph.add("sync",
                           lambda e, row=row, bfi=bfi: e.dma_start(
                               out=uev[bfi][:], in_=u_dram[row:row + 128, :]),
                           waits=[sv, (s_usp, s_usp.n)], incs=[(s_ue, 16)])
                    for j in range(NJ):
                        for dh in range(2):
                            ph.add("tensor",
                                   lambda e, j=j, dh=dh, tsub=tsub: e.matmul(
                                       po[:, dh * 512:(dh + 1) * 512],
                                       yz_bf[:, j * QB + tsub * 128: j * QB + (tsub + 1) * 128],
                                       woutt[:, j * D + dh * 512: j * D + (dh + 1) * 512],
                                       start=(j == 0), stop=(j == NJ - 1)),
                                   waits=([sv] if (dh == 0 and j == 0) else []),
                                   incs=([(st, 1)] if (dh == 1 and j == NJ - 1) else []))
                    ph.add("vector",
                           lambda e, bfi=bfi: e.scalar_tensor_tensor(
                               outp[bfi][:, 0:512], uev[bfi][:, 0:512], 0.5,
                               po[:, 0:512], ALU.mult, ALU.add),
                           waits=[st, (s_ue, s_ue.n)], incs=[(sv, 1)])
                    ph.add("vector",
                           lambda e, bfi=bfi: e.scalar_tensor_tensor(
                               outp[bfi][:, 512:1024], uev[bfi][:, 512:1024], 0.5,
                               po[:, 512:1024], ALU.mult, ALU.add),
                           incs=[(sv, 1)])
                    ph.add("sync",
                           lambda e, row=row, bfi=bfi: e.dma_start(
                               out=rs_in[row:row + 128, :], in_=outp[bfi][:]),
                           waits=[sv], incs=[(sd, 16)])
            ph.emit(nc)

        ph = Phase()
        ph.add("gpsimd",
               lambda e: e.collective_compute(
                   "ReduceScatter", ALU.add, replica_groups=PAIRS,
                   ins=[rs_in.opt()], outs=[rs_out.opt()]),
               waits=bw(), incs=[(c2, None)])
        ph.emit(nc)

        esFD = ExitStack()
        with esFD:
            gamma_sb = sb(esFD, "gamma_sb", [128, E * 8], F32)
            beta_sb = sb(esFD, "beta_sb", [128, E * 8], F32)
            b1_sb = sb(esFD, "b1_sb", [128, E * 32], F32)
            cwT_sb = sb(esFD, "cwT_sb", [8, TH], BF16)

            esC = ExitStack()
            with esC:
                xhatT = sb(esC, "xhatT", [128, 8 * TH], BF16)
                wg_sb = sb(esC, "wg_sb", [128, 8 * E], F32)
                b2_sb = sb(esC, "b2_sb", [8, D], BF16)
                cw_sb = sb(esC, "cw_sb", [128, (TH // 128) * E], F32)
                sqscr = sb(esC, "sqscrC", [128, D], F32)
                rst = [sb(esC, f"rst{i}", [128, D], F32) for i in range(2)]
                xht = [sb(esC, f"xht{i}", [128, D], F32) for i in range(2)]
                aow = sb(esC, "aow", [128, D], F32)
                x1w = sb(esC, "x1w", [128, D], F32)
                vw = sb(esC, "vw", [128, D], F32)
                xhbf = sb(esC, "xhbf", [128, D], BF16)
                vT = sb(esC, "vT", [128, 8 * 128], F32)
                sct = sb(esC, "sct", [128, 8], F32)
                cwt = sb(esC, "cwt", [128, 8], F32)
                seedw = sb(esC, "seedw", [128, TH], F32)
                bout_rep = sb(esC, "bout_repC", [128, D], F32)
                ptrv = ps(esC, "ptrv", [128, 1024], F32)
                ptrb = ps(esC, "ptrb", [128, 1024], BF16)
                psc = ps(esC, "psc", [128, 512], F32)
                pb2 = ps(esC, "pb2", [128, 1024], F32)

                ph = Phase()
                for dst, srcd in ((wg_sb, wg_d), (gamma_sb, gamma_d),
                                  (beta_sb, beta_d), (b1_sb, b1_d)):
                    ph.add("sync",
                           lambda e, d=dst, s=srcd: e.dma_start(out=d[:], in_=s),
                           incs=[(sd, 16)])
                ph.add("gpsimd", lambda e: e.dma_start(out=b2_sb[:], in_=b2_d),
                       incs=[(gd, 16)])
                gd_b2 = gd.n
                ph.add("sync", lambda e: e.dma_start(out=bout_rep[:], in_=boutrep_dram),
                       incs=[(sd, 16)])

                for tt in range(TH // 128):
                    row = tt * 128
                    bfi = tt % 2
                    ph.add("sync",
                           lambda e, row=row, bfi=bfi: e.dma_start(
                               out=rst[bfi][:], in_=rs_out[row:row + 128, :]),
                           waits=[(c2, 1), sv, ss], incs=[(sd, 16)])
                    ph.add("sync",
                           lambda e, row=row, bfi=bfi: e.dma_start(
                               out=xht[bfi][:], in_=x_half[row:row + 128, :]),
                           incs=[(sd, 16)])
                    ph.add("vector",
                           lambda e, bfi=bfi: e.tensor_add(aow[:], rst[bfi][:], bout_rep[:]),
                           waits=[sd, ss], incs=[(sv, 1)])
                    emit_stats(ph, aow, sqscr, 0, 1, cv[:, 0:1])
                    ph.add("vector",
                           lambda e: e.tensor_scalar(aow[:], aow[:], stat[:, 0:1],
                                                     stat[:, 1:2], ALU.add, ALU.mult),
                           waits=[(vt, vt.n), (sce, sce.n)], incs=[(sv, 1)])
                    ph.add("vector",
                           lambda e, bfi=bfi: e.tensor_add(x1w[:], xht[bfi][:], aow[:]),
                           incs=[(sv, 1)])
                    ph.add("sync",
                           lambda e, row=row: e.dma_start(out=x1_dram[row:row + 128, :],
                                                          in_=x1w[:]),
                           waits=[sv], incs=[(sd, 16)])
                    emit_stats(ph, x1w, sqscr, 2, 3, cv[:, 0:1])
                    ph.add("vector",
                           lambda e: e.tensor_scalar(vw[:], x1w[:], stat[:, 2:3],
                                                     stat[:, 3:4], ALU.add, ALU.mult),
                           waits=[ss, st, sd, (vt, vt.n), (sce, sce.n)], incs=[(sv, 1)])
                    ph.add("sync",
                           lambda e, row=row: e.dma_start(out=v_dram[row:row + 128, :],
                                                          in_=vw[:]),
                           waits=[sv], incs=[(sd, 16)])
                    for a in range(8):
                        ph.add("tensor",
                               lambda e, a=a: e.transpose(ptrv[:, a * 128:(a + 1) * 128],
                                                          vw[:, a * 128:(a + 1) * 128],
                                                          eyef[:]),
                               waits=([sv, ss] if a == 0 else []),
                               incs=([(st, 1)] if a == 7 else []))
                    ph.add("scalar", lambda e: e.copy(vT[:], ptrv[:]), waits=[st],
                           incs=[(ss, 1)])
                    for a in range(8):
                        ph.add("tensor",
                               lambda e, a=a: e.matmul(psc[:, 0:8],
                                                       vT[:, a * 128:(a + 1) * 128],
                                                       wg_sb[:, a * E:(a + 1) * E],
                                                       start=(a == 0), stop=(a == 7)),
                               waits=([ss] if a == 0 else []),
                               incs=([(st, 1)] if a == 7 else []))
                    ph.add("scalar", lambda e: e.copy(sct[:], psc[:, 0:8]), waits=[st],
                           incs=[(ss, 1)])

                    def tv(fn, extra=()):
                        ph.add("vector", fn,
                               waits=list(extra) + [(vt, vt.n)],
                               incs=[(vt, 1)])
                    tv(lambda e: e.reduce_max(stat[:, 8:9], sct[:], axis=AX.X,
                                              negate=True), extra=[ss])
                    tv(lambda e: e.tensor_single_scalar(cwt[:], sct[:], stat[:, 8:9],
                                                        ALU.add))
                    tv(lambda e: e.tensor_single_scalar(cwt[:], cwt[:], 0.0, ALU.is_ge))
                    tv(lambda e: e.scalar_tensor_tensor(cwt[:], cwt[:], -1e30, sct[:],
                                                        ALU.mult, ALU.add))
                    tv(lambda e: e.reduce_max(stat[:, 9:10], cwt[:], axis=AX.X))
                    tv(lambda e: e.tensor_scalar_mul(stat[:, 10:11], stat[:, 9:10], -1.0))
                    tv(lambda e: e.tensor_single_scalar(cwt[:], sct[:], stat[:, 10:11],
                                                        ALU.add))
                    tv(lambda e: e.tensor_single_scalar(cwt[:], cwt[:], 0.0, ALU.is_ge))
                    ph.add("scalar",
                           lambda e: e.activation(sct[:], sct[:], AF.Exp, bias=stat[:, 8:9]),
                           waits=[(vt, vt.n), (sce, sce.n)], incs=[(sce, 1)])
                    tv(lambda e: e.tensor_mul(cwt[:], cwt[:], sct[:]),
                       extra=[(sce, sce.n)])
                    tv(lambda e: e.reduce_sum(stat[:, 11:12], cwt[:], axis=AX.X))
                    tv(lambda e: e.reciprocal(stat[:, 12:13], stat[:, 11:12]))
                    tv(lambda e, tt=tt: e.tensor_scalar_mul(
                           cw_sb[:, tt * E:(tt + 1) * E], cwt[:], stat[:, 12:13]))
                    ph.add("tensor",
                           lambda e, tt=tt: e.transpose(psc[0:8, 128:256],
                                                        cw_sb[:, tt * E:(tt + 1) * E],
                                                        eyef[:]),
                           waits=[(vt, vt.n)], incs=[(st, 1)])
                    ph.add("scalar",
                           lambda e, tt=tt: e.copy(cwT_sb[0:8, tt * 128:(tt + 1) * 128],
                                                   psc[0:8, 128:256]),
                           waits=[st], incs=[(ss, 1)])
                    emit_stats(ph, vw, sqscr, 4, 5, cv[:, 1:2])
                    ph.add("vector",
                           lambda e: e.tensor_scalar(xhbf[:], vw[:], stat[:, 4:5],
                                                     stat[:, 5:6], ALU.add, ALU.mult),
                           waits=[ss, st, (vt, vt.n), (sce, sce.n)], incs=[(sv, 1)])
                    for a in range(8):
                        ph.add("tensor",
                               lambda e, a=a: e.transpose(ptrb[:, a * 128:(a + 1) * 128],
                                                          xhbf[:, a * 128:(a + 1) * 128],
                                                          eyeb[:]),
                               waits=([sv, ss] if a == 0 else []),
                               incs=([(st, 1)] if a == 7 else []))
                    ph.add("scalar",
                           lambda e, tt=tt: e.copy(
                               xhatT[:].rearrange("p (a t) -> p a t", a=8)[:, :, tt * 128:(tt + 1) * 128],
                               ptrb[:].rearrange("p (a t) -> p a t", a=8)),
                           waits=[st], incs=[(ss, 1)])

                for a in range(8):
                    ph.add("sync",
                           lambda e, a=a: e.dma_start(
                               out=xhatT_dram[:, a * TH:(a + 1) * TH],
                               in_=xhatT[:, a * TH:(a + 1) * TH]),
                           waits=([ss, sv] if a == 0 else []), incs=[(sd, 16)])
                for dtile in range(8):
                    for tc in range(8):
                        half = tc % 2
                        ph.add("tensor",
                               lambda e, dtile=dtile, tc=tc, half=half: e.matmul(
                                   pb2[:, half * 512:(half + 1) * 512],
                                   b2_sb[0:8, dtile * 128:(dtile + 1) * 128],
                                   cwT_sb[0:8, tc * 512:(tc + 1) * 512],
                                   start=True, stop=True),
                               waits=[ss, (gd, gd_b2)], incs=[(st, 1)])
                        ph.add("scalar",
                               lambda e, tc=tc, half=half: e.copy(
                                   seedw[:, tc * 512:(tc + 1) * 512],
                                   pb2[:, half * 512:(half + 1) * 512]),
                               waits=[st], incs=[(ss, 1)])
                    ph.add("sync",
                           lambda e, dtile=dtile: e.dma_start(
                               out=ymoe_dram[dtile * 128:(dtile + 1) * 128, :],
                               in_=seedw[:]),
                           waits=[ss], incs=[(s_yio, 16)])
                ph.emit(nc)

            TSG = 2048
            NTSG = TH // TSG
            esD = ExitStack()
            with esD:
                w1h = sb(esD, "w1h", [128, 8 * 2048], BF16)
                w2h = sb(esD, "w2h", [128, 16 * 1024], BF16)
                ntb = sb(esD, "ntb", [128, 8 * TSG], BF16)
                hbig = sb(esD, "hbig", [128, 16 * TSG], BF16)
                cwrow = sb(esD, "cwrow", [1, TSG], BF16)
                cwrep = sb(esD, "cwrep", [128, TSG], BF16)
                ydt = [sb(esD, f"ydt{i}", [128, TSG], F32) for i in range(2)]
                ytmp1 = sb(esD, "ytmp1", [128, TSG], F32)
                pA = ps(esD, "pA", [128, 2048], F32)
                pB = ps(esD, "pB", [128, 2048], F32)

                ph = Phase()
                units = [(e_i, fh) for e_i in range(E) for fh in range(2)]

                def load_w1(e_i, fh, wts):
                    for kb in range(8):
                        ph.add("scalar",
                               lambda e, e_i=e_i, fh=fh, kb=kb: e.dma_start(
                                   out=w1h[:, kb * 2048:(kb + 1) * 2048],
                                   in_=w1b_dram[e_i, kb * 128:(kb + 1) * 128,
                                                fh * 2048:(fh + 1) * 2048]),
                               waits=(wts if kb == 0 else []), incs=[(w_w1, 16)])
                    return w_w1.n

                def load_w2(e_i, fh, wts):
                    for a in range(16):
                        ph.add("scalar",
                               lambda e, e_i=e_i, fh=fh, a=a: e.dma_start(
                                   out=w2h[:, a * 1024:(a + 1) * 1024],
                                   in_=w2b_dram[e_i, fh * 2048 + a * 128:
                                                fh * 2048 + (a + 1) * 128, :]),
                               waits=(wts if a == 0 else []), incs=[(w_w2, 16)])
                    return w_w2.n

                seed_yio = s_yio.n
                yout_sd = {}
                gd_w1r = {0: load_w1(*units[0], wts=[(s_pc, sd_pc[len(pc_steps) - 1])])}
                gd_w2r = {0: load_w2(*units[0], wts=[])}
                for ui, (e_i, fh) in enumerate(units):
                    for tsg in range(NTSG):
                        t0g = tsg * TSG
                        for a in range(8):
                            ph.add("sync",
                                   lambda e, a=a, t0g=t0g: e.dma_start(
                                       out=ntb[:, a * TSG:(a + 1) * TSG],
                                       in_=xhatT_dram[:, a * TH + t0g: a * TH + t0g + TSG]),
                                   waits=([(gc, gc.n), st] if a == 0 else []),
                                   incs=[(s_nt, 16)])
                        sd_nt = s_nt.n
                        for kb in range(8):
                            ph.add("gpsimd",
                                   lambda e, e_i=e_i, kb=kb: e.tensor_scalar(
                                       ntb[:, kb * TSG:(kb + 1) * TSG],
                                       ntb[:, kb * TSG:(kb + 1) * TSG],
                                       gamma_sb[:, e_i * 8 + kb:e_i * 8 + kb + 1],
                                       beta_sb[:, e_i * 8 + kb:e_i * 8 + kb + 1],
                                       ALU.mult, ALU.add),
                                   waits=([(s_nt, sd_nt)] if kb == 0 else []),
                                   incs=([(gc, 1)] if kb == 7 else []))
                        gc_nt = gc.n
                        ph.add("sync",
                               lambda e, e_i=e_i, t0g=t0g: e.dma_start(
                                   out=cwrow[:],
                                   in_=cwT_sb[e_i:e_i + 1, t0g:t0g + TSG]),
                               waits=[ss], incs=[(s_cw, 16)])
                        sd_cw = s_cw.n
                        for tc in range(4):
                            ph.add("tensor",
                                   lambda e, tc=tc: e.matmul(
                                       pB[:, tc * 512:(tc + 1) * 512], onesb[0:1, :],
                                       cwrow[0:1, tc * 512:(tc + 1) * 512],
                                       start=True, stop=True),
                                   waits=([(s_cw, sd_cw), ss, sv] if tc == 0 else []),
                                   incs=([(st, 1)] if tc == 3 else []))
                        ph.add("scalar", lambda e: e.copy(cwrep[:], pB[:]),
                               waits=[st, (gc, gc.n)], incs=[(ss, 1)])
                        ss_cw = ss.n
                        hpp_done = {-2: ss_cw, -1: ss_cw}
                        hm_done = {}
                        for fidx in range(16):
                            fb, ft = fidx // 4, fidx % 4
                            fcol = fb * 512 + ft * 128
                            fi = fh * 16 + fb * 4 + ft
                            pp = pA if fidx % 2 == 0 else pB
                            for kb in range(8):
                                for tc4 in range(4):
                                    w = []
                                    if kb == 0 and tc4 == 0:
                                        w = [(gc, gc_nt), (w_w1, gd_w1r[ui]),
                                             (ss, hpp_done[fidx - 2])]
                                    ph.add("tensor",
                                           lambda e, kb=kb, fcol=fcol, tc4=tc4, pp=pp: e.matmul(
                                               pp[:, tc4 * 512:(tc4 + 1) * 512],
                                               w1h[:, kb * 2048 + fcol: kb * 2048 + fcol + 128],
                                               ntb[:, kb * TSG + tc4 * 512: kb * TSG + (tc4 + 1) * 512],
                                               start=(kb == 0), stop=(kb == 7)),
                                           waits=w,
                                           incs=([(st, 1)] if (kb == 7 and tc4 == 3) else []))
                            for tc4 in range(4):
                                ph.add("scalar",
                                       lambda e, fidx=fidx, fi=fi, e_i=e_i, tc4=tc4, pp=pp: e.activation(
                                           hbig[:, fidx * TSG + tc4 * 512:
                                                fidx * TSG + (tc4 + 1) * 512],
                                           pp[:, tc4 * 512:(tc4 + 1) * 512], AF.Relu,
                                           bias=b1_sb[:, e_i * 32 + fi:e_i * 32 + fi + 1]),
                                       waits=([st] if tc4 == 0 else []),
                                       incs=([(ss, 1)] if tc4 == 3 else []))
                            hpp_done[fidx] = ss.n
                            ph.add("gpsimd",
                                   lambda e, fidx=fidx: e.tensor_mul(
                                       hbig[:, fidx * TSG:(fidx + 1) * TSG],
                                       hbig[:, fidx * TSG:(fidx + 1) * TSG],
                                       cwrep[:]),
                                   waits=[(ss, hpp_done[fidx])],
                                   incs=[(gc, 1)])
                            hm_done[fidx] = gc.n
                        if tsg == NTSG - 1 and ui + 1 < len(units):
                            gd_w1r[ui + 1] = load_w1(*units[ui + 1], wts=[st])
                        copy_done = {-2: 0, -1: 0}
                        gadd_done = {-2: 0, -1: 0}
                        for dt in range(8):
                            pp = pB if dt % 2 == 0 else pA
                            yb = ydt[dt % 2]
                            yt = ytmp1
                            wdep = max(yout_sd.get((dt, t0g), seed_yio),
                                       yout_sd.get(("buf", dt % 2), 0))
                            ph.add("sync",
                                   lambda e, dt=dt, t0g=t0g, yb=yb: e.dma_start(
                                       out=yb[:],
                                       in_=ymoe_dram[dt * 128:(dt + 1) * 128, t0g:t0g + TSG]),
                                   waits=[(s_yio, wdep)], incs=[(s_yio, 16)])
                            sd_y = s_yio.n
                            for ftile in range(16):
                                for tc4 in range(4):
                                    w = []
                                    if ftile == 0 and tc4 == 0:
                                        w = [(gc, hm_done[15]),
                                             (ss, copy_done[dt - 2]),
                                             (w_w2, gd_w2r[ui])]
                                    ph.add("tensor",
                                           lambda e, dt=dt, ftile=ftile, tc4=tc4, pp=pp: e.matmul(
                                               pp[:, tc4 * 512:(tc4 + 1) * 512],
                                               w2h[:, ftile * 1024 + dt * 128: ftile * 1024 + dt * 128 + 128],
                                               hbig[:, ftile * TSG + tc4 * 512: ftile * TSG + (tc4 + 1) * 512],
                                               start=(ftile == 0), stop=(ftile == 15)),
                                           waits=w,
                                           incs=([(st, 1)] if (ftile == 15 and tc4 == 3) else []))
                            ph.add("scalar",
                                   lambda e, pp=pp, yt=yt: e.copy(yt[:], pp[:]),
                                   waits=[st, (gc, gadd_done[dt - 1])],
                                   incs=[(ss, 1)])
                            copy_done[dt] = ss.n
                            ph.add("gpsimd",
                                   lambda e, yb=yb, yt=yt: e.tensor_add(yb[:], yb[:], yt[:]),
                                   waits=[(ss, copy_done[dt]), (s_yio, sd_y)],
                                   incs=[(gc, 1)])
                            gadd_done[dt] = gc.n
                            ph.add("sync",
                                   lambda e, dt=dt, t0g=t0g, yb=yb: e.dma_start(
                                       out=ymoe_dram[dt * 128:(dt + 1) * 128, t0g:t0g + TSG],
                                       in_=yb[:]),
                                   waits=[(gc, gadd_done[dt])], incs=[(s_yio, 16)])
                            yout_sd[(dt, t0g)] = s_yio.n
                            yout_sd[("buf", dt % 2)] = s_yio.n
                        if tsg == NTSG - 1 and ui + 1 < len(units):
                            gd_w2r[ui + 1] = load_w2(*units[ui + 1], wts=[st])
                ph.emit(nc)

        esE = ExitStack()
        with esE:
            ydgrp = sb(esE, "ydgrp", [128, 8 * 512], F32)
            ytl = sb(esE, "ytl", [128, D], F32)
            vtl = [sb(esE, f"vtl{i}", [128, D], F32) for i in range(2)]
            x1t = [sb(esE, f"x1t{i}", [128, D], F32) for i in range(2)]
            fyw = sb(esE, "fyw", [128, D], F32)
            sqscr = sb(esE, "sqscrE", [128, D], F32)
            otl = sb(esE, "otl", [128, D], F32)
            ptrE = ps(esE, "ptrE", [128, 1024], F32)

            ph = Phase()
            for tt in range(TH // 128):
                row = tt * 128
                bfi = tt % 2
                if tt % 4 == 0:
                    ph.add("sync",
                           lambda e, row=row: e.dma_start(
                               out=ydgrp[:].rearrange("p (a t) -> p a t", a=8),
                               in_=ymoe_dram.rearrange("(a p) t -> p a t", p=128)
                                   [:, :, row:row + 512]),
                           waits=bw(), incs=[(sd, 16)])
                ph.add("sync",
                       lambda e, row=row, bfi=bfi: e.dma_start(
                           out=vtl[bfi][:], in_=v_dram[row:row + 128, :]),
                       waits=([sv] if tt >= 2 else []), incs=[(sd, 16)])
                ph.add("sync",
                       lambda e, row=row, bfi=bfi: e.dma_start(
                           out=x1t[bfi][:], in_=x1_dram[row:row + 128, :]),
                       incs=[(sd, 16)])
                for a in range(8):
                    ph.add("tensor",
                           lambda e, a=a, tt=tt: e.transpose(
                               ptrE[:, a * 128:(a + 1) * 128],
                               ydgrp[:, a * 512 + (tt % 4) * 128: a * 512 + (tt % 4) * 128 + 128],
                               eyef[:]),
                           waits=([sd, ss] if a == 0 else []),
                           incs=([(st, 1)] if a == 7 else []))
                ph.add("scalar", lambda e: e.copy(ytl[:], ptrE[:]), waits=[st],
                       incs=[(ss, 1)])
                ph.add("vector",
                       lambda e, bfi=bfi: e.tensor_add(fyw[:], vtl[bfi][:], ytl[:]),
                       waits=[sd, ss, ob], incs=[(sv, 1)])
                emit_stats(ph, fyw, sqscr, 0, 4, cv[:, 0:1])
                ph.add("vector",
                       lambda e: e.tensor_scalar(fyw[:], fyw[:], stat[:, 0:1],
                                                 stat[:, 4:5], ALU.add, ALU.mult),
                       waits=[(vt, vt.n), (sce, sce.n)], incs=[(sv, 1)])
                ph.add("vector",
                       lambda e, bfi=bfi: e.tensor_add(otl[:], x1t[bfi][:], fyw[:]),
                       incs=[(sv, 1)])
                ph.add("sync",
                       lambda e, row=row: e.dma_start(out=out_d[row:row + 128, :],
                                                      in_=otl[:]),
                       waits=[sv], incs=[(ob, 16)])
            for (o, t) in dbg_copies:
                ph.add("sync", lambda e, o=o, t=t: e.dma_start(out=o, in_=t),
                       waits=bw(), incs=[(ob, 16)])
            ph.add("sync", lambda e: e.nop(), waits=[(ob, ob.n)])
            ph.emit(nc)

    return nc


_NC_CACHE = {}


def _get_nc(debug=False):
    key = bool(debug)
    if key not in _NC_CACHE:
        _NC_CACHE[key] = build_program(debug=key)
    return _NC_CACHE[key]


def _stage(inputs):
    f32 = np.float32
    x = np.asarray(inputs["L0_attn_x"], f32)
    nw = np.asarray(inputs["L0_attn_norm_w"], f32)
    Wx = np.asarray(inputs["L0_attn_Wx"], f32)
    bx = np.asarray(inputs["L0_attn_bx"], f32)
    Wz = np.asarray(inputs["L0_attn_Wz"], f32)
    bz = np.asarray(inputs["L0_attn_bz"], f32)
    convw = np.asarray(inputs["L0_attn_conv_w"], f32).reshape(DI, 4)
    convb = np.asarray(inputs["L0_attn_conv_b"], f32)
    Wxp = np.asarray(inputs["L0_attn_Wxp"], f32)
    Wdt = np.asarray(inputs["L0_attn_Wdt"], f32)
    bdt = np.asarray(inputs["L0_attn_bdt"], f32)
    Alog = np.asarray(inputs["L0_attn_A_log"], f32)
    Dp = np.asarray(inputs["L0_attn_D"], f32)
    Wout = np.asarray(inputs["L0_attn_Wout"], f32)
    bout = np.asarray(inputs["L0_attn_bout"], f32)
    Wg = np.asarray(inputs["L0_ffn_Wg"], f32)
    gamma = np.asarray(inputs["L0_ffn_gamma"], f32)
    beta = np.asarray(inputs["L0_ffn_beta"], f32)
    W1 = np.ascontiguousarray(np.asarray(inputs["L0_ffn_W1"], f32))
    b1 = np.asarray(inputs["L0_ffn_b1"], f32)
    W2 = np.ascontiguousarray(np.asarray(inputs["L0_ffn_W2"], f32))
    b2 = np.ascontiguousarray(np.asarray(inputs["L0_ffn_b2"], f32))

    eye = np.eye(128, dtype=f32)
    constv = np.ascontiguousarray(
        np.tile(np.array([EPS_A, EPS_F, 1.0, -1.0, -0.5, 1.0 / D, 0.0, 0.0],
                         dtype=f32), (128, 1)))
    wg_s = np.ascontiguousarray(Wg.reshape(8, 128, E).transpose(1, 0, 2).reshape(128, 8 * E))
    gamma_s = np.ascontiguousarray(gamma.reshape(E, 8, 128).transpose(2, 0, 1).reshape(128, E * 8))
    beta_s = np.ascontiguousarray(beta.reshape(E, 8, 128).transpose(2, 0, 1).reshape(128, E * 8))
    b1_s = np.ascontiguousarray(b1.reshape(E, 32, 128).transpose(2, 0, 1).reshape(128, E * 32))

    def chparam(vec, p):
        hh = vec[p * CH:(p + 1) * CH]
        return np.ascontiguousarray(hh.reshape(NJ, 128).T)

    per_par = {}
    for p in range(2):
        chs = slice(p * CH, (p + 1) * CH)
        per_par[p] = {
            "wxt_w": np.ascontiguousarray(Wx[chs, :].T),
            "wzt_w": np.ascontiguousarray(Wz[chs, :].T),
            "bx": chparam(bx, p),
            "bz": chparam(bz, p),
            "convw": np.ascontiguousarray(
                convw[chs].reshape(NJ, 128, 4).transpose(1, 0, 2).reshape(128, NJ * 4)),
            "convb": chparam(convb, p),
            "wxpt_w": np.ascontiguousarray(Wxp[:, chs].T),
            "wdtt_w": np.ascontiguousarray(Wdt[chs, :].T),
            "bdt": chparam(bdt, p),
            "alog": np.ascontiguousarray(
                Alog[chs].reshape(NJ, 128, N16).transpose(1, 0, 2).reshape(128, NJ * N16)),
            "dp": chparam(Dp, p),
            "woutt_w": np.ascontiguousarray(Wout[:, chs].T),
        }

    in_maps = []
    for c in range(8):
        b, p = c // 2, c % 2
        im = {
            "x_full": np.ascontiguousarray(x[b]),
            "x_half": np.ascontiguousarray(x[b, p * TH:(p + 1) * TH]),
            "eye_f": eye,
            "constv": constv,
            "norm_w": nw,
            "bout": bout,
            "wg": wg_s,
            "gamma": gamma_s,
            "beta": beta_s,
            "w1": W1,
            "b1": b1_s,
            "w2": W2,
            "b2": b2,
        }
        im.update(per_par[p])
        in_maps.append(im)
    return in_maps


def kernel(**inputs):
    nc = _get_nc(debug=False)
    in_maps = _stage(inputs)
    res = run_bass_kernel_spmd(nc, in_maps, core_ids=list(range(8)))
    out = np.empty((Bq, L, D), np.float32)
    for c in range(8):
        b, p = c // 2, c % 2
        out[b, p * TH:(p + 1) * TH] = res.results[c]["out"]
    return out


# version 15
# speedup vs baseline: 1.0861x; 1.0074x over previous
import numpy as np
from contextlib import ExitStack

import concourse.bass_utils as _bu

if not getattr(_bu, "_ldw_opt_patched", False):
    _orig_run_command = _bu.run_command

    def _run_command_ldw(argv, **kwargs):
        argv = ["--enable-ldw-opt=true" if a == "--enable-ldw-opt=false" else a
                for a in argv]
        return _orig_run_command(argv, **kwargs)

    _bu.run_command = _run_command_ldw
    _bu._ldw_opt_patched = True

import concourse.bass as bass
import concourse.mybir as mybir
from concourse.bass_utils import run_bass_kernel_spmd

F32 = mybir.dt.float32
BF16 = mybir.dt.bfloat16
AF = mybir.ActivationFunctionType
ALU = mybir.AluOpType
AX = mybir.AxisListType

Bq, L, D = 4, 8192, 1024
DI = 2048
CH = DI // 2
NJ = CH // 128
N16 = 16
R = 64
RR = 96
TH = L // 2
E, F = 8, 4096
EPS_A = 1e-5
EPS_F = 1e-6

CA = 2048
NCA = L // CA
QB = 512
NCB = L // QB
TS = 256
NTS = TH // TS
NFB = 4

PAIRS = [[0, 1], [2, 3], [4, 5], [6, 7]]
ENGINES = ("sync", "gpsimd", "vector", "scalar", "tensor")


class Sem:
    def __init__(self, nc, es, name):
        self.h = es.enter_context(nc.semaphore(name))
        self.n = 0


class Phase:
    def __init__(self):
        self.ops = {e: [] for e in ENGINES}

    def add(self, eng, fn, waits=(), incs=()):
        w = []
        for x in waits:
            if isinstance(x, tuple):
                w.append((x[0].h, x[1]))
            else:
                w.append((x.h, x.n))
        ii = []
        assert len(list(incs)) <= 1, "one sync update per instruction"
        for (s, k) in incs:
            if k is None:
                s.n += 1
                ii.append((s.h, None))
            else:
                s.n += k
                ii.append((s.h, k))
        self.ops[eng].append((fn, w, ii))

    def emit(self, nc):
        with nc.Block() as blk:
            for eng in ENGINES:
                ops = self.ops[eng]
                if not ops:
                    continue

                def mk(ops):
                    def body(e):
                        for (fn, w, ii) in ops:
                            for (h, c) in w:
                                if c > 0:
                                    e.wait_ge(h, c)
                            instr = fn(e)
                            for (h, k) in ii:
                                if k is None:
                                    instr.then_inc(h)
                                else:
                                    instr.then_inc(h, k)
                    return body

                getattr(blk, eng)(mk(ops))


def build_program(debug=False):
    nc = bass.Bass("TRN2", target_bir_lowering=False)

    def din(name, shape, dt=F32):
        return nc.dram_tensor(name, shape, dt, kind="ExternalInput").ap()

    x_full = din("x_full", [L, D])
    x_half = din("x_half", [TH, D])
    eye_f = din("eye_f", [128, 128])
    constv_d = din("constv", [128, 8])
    norm_w = din("norm_w", [D])
    wxt_d = din("wxt_w", [D, CH])
    wzt_d = din("wzt_w", [D, CH])
    bx_d = din("bx", [128, NJ])
    bz_d = din("bz", [128, NJ])
    convw_d = din("convw", [128, NJ * 4])
    convb_d = din("convb", [128, NJ])
    wxpt_d = din("wxpt_w", [CH, RR])
    wdtt_d = din("wdtt_w", [R, CH])
    bdt_d = din("bdt", [128, NJ])
    alog_d = din("alog", [128, NJ * N16])
    dp_d = din("dp", [128, NJ])
    woutt_d = din("woutt_w", [CH, D])
    bout_d = din("bout", [D])
    wg_d = din("wg", [128, 8 * E])
    gamma_d = din("gamma", [128, E * 8])
    beta_d = din("beta", [128, E * 8])
    w1_d = din("w1", [E, D, F])
    b1_d = din("b1", [128, E * (F // 128)])
    w2_d = din("w2", [E, F, D])
    b2_d = din("b2", [E, D])

    out_d = nc.dram_tensor("out", [TH, D], F32, kind="ExternalOutput").ap()

    dbg_copies = []

    def scr(name, shape, dt=F32, dbg=False):
        t = nc.dram_tensor(name, shape, dt, kind="Internal").ap()
        if debug and dbg:
            o = nc.dram_tensor("dbg_" + name, shape, dt, kind="ExternalOutput").ap()
            dbg_copies.append((o, t))
        return t

    u_dram = scr("u_dram", [L, D], dbg=True)
    xb2_dram = scr("xb2_dram", [CH, L], BF16, dbg=True)
    zs_dram = scr("zs_dram", [CH, L], BF16)
    xdbl_in = scr("xdbl_in", [RR, L])
    xdbl_out = scr("xdbl_out", [RR, L], dbg=True)
    rs_in = scr("rs_in", [L, D])
    rs_out = scr("rs_out", [TH, D], dbg=True)
    x1_dram = scr("x1_dram", [TH, D], dbg=True)
    v_dram = scr("v_dram", [TH, D])
    ymoe_dram = scr("ymoe_dram", [D, TH], dbg=True)
    xhatT_dram = scr("xhatT_dram", [128, 8 * TH], BF16)
    nwrep_dram = scr("nwrep_dram", [128, D])
    boutrep_dram = scr("boutrep_dram", [128, D])
    w1b_dram = scr("w1b_dram", [E, D, F], BF16)
    w2b_dram = scr("w2b_dram", [E, F, D], BF16)

    es = ExitStack()
    with es:
        sd = Sem(nc, es, "sd")
        gd = Sem(nc, es, "gd")
        sv = Sem(nc, es, "sv")
        ss = Sem(nc, es, "ss")
        st = Sem(nc, es, "st")
        c1 = Sem(nc, es, "c1")
        c2 = Sem(nc, es, "c2")
        ob = Sem(nc, es, "ob")
        vt = Sem(nc, es, "vt")
        gc = Sem(nc, es, "gc")
        s_pc = Sem(nc, es, "s_pc")
        g_pc = Sem(nc, es, "g_pc")
        s_usp = Sem(nc, es, "s_usp")
        s_zsp = Sem(nc, es, "s_zsp")
        s_xsp = Sem(nc, es, "s_xsp")
        s_xt = Sem(nc, es, "s_xt")
        g_xb = Sem(nc, es, "g_xb")
        s_bl = Sem(nc, es, "s_bl")
        s_nt = Sem(nc, es, "s_nt")
        s_cw = Sem(nc, es, "s_cw")
        s_yio = Sem(nc, es, "s_yio")
        w_w1 = Sem(nc, es, "w_w1")
        w_w2 = Sem(nc, es, "w_w2")
        s_ue = Sem(nc, es, "s_ue")
        s_xh = Sem(nc, es, "s_xh")
        sce = Sem(nc, es, "sce")

        ALLS = (sd, gd, sv, ss, st)

        def bw():
            return [sd, gd, sv, ss, st, s_pc, g_pc, s_usp, s_zsp, s_xsp, s_xt,
                    g_xb, s_bl, s_nt, s_cw, s_yio, w_w1, w_w2, s_ue, s_xh]

        def sb(stk, name, shape, dt=F32):
            return stk.enter_context(nc.sbuf_tensor(name, shape, dt))

        def ps(stk, name, shape, dt=F32):
            return stk.enter_context(nc.psum_tensor(name, shape, dt))

        eyeb = sb(es, "eyeb", [128, 128], BF16)
        eyef = sb(es, "eyef", [128, 128], F32)
        onesb = sb(es, "onesb", [33, 128], BF16)
        onesf = sb(es, "onesf", [1, 128], F32)
        bx_sb = sb(es, "bx_sb", [128, NJ], F32)
        bz_sb = sb(es, "bz_sb", [128, NJ], F32)
        convb_sb = sb(es, "convb_sb", [128, NJ], F32)
        bdt_sb = sb(es, "bdt_sb", [128, NJ], F32)
        dp_sb = sb(es, "dp_sb", [128, NJ], F32)
        convw_sb = sb(es, "convw_sb", [128, NJ * 4], F32)
        a_sb = sb(es, "a_sb", [128, NJ * N16], F32)
        hstate = sb(es, "hstate", [128, NJ * N16], F32)
        cv = sb(es, "cv", [128, 8], F32)
        stat = sb(es, "stat", [128, 16], F32)

        def emit_stats(ph, srcbuf, sq, mcol, vcol, eps_ap, rms=False):
            if not rms:
                ph.add("vector",
                       lambda e: e.reduce_sum(stat[:, mcol:mcol + 1], srcbuf[:],
                                              axis=AX.X, negate=True),
                       incs=[(vt, 1)])
            ph.add("scalar",
                   lambda e: e.activation(sq[:], srcbuf[:], AF.Square,
                                          accum_out=stat[:, vcol:vcol + 1]),
                   waits=[(vt, vt.n)] if not rms else [sv], incs=[(sce, 1)])
            if rms:
                ph.add("scalar",
                       lambda e: e.activation(stat[:, vcol:vcol + 1],
                                              stat[:, vcol:vcol + 1], AF.Ln,
                                              scale=cv[:, 5:6], bias=eps_ap),
                       waits=[(sce, sce.n)], incs=[(sce, 1)])
                ph.add("scalar",
                       lambda e: e.activation(stat[:, vcol:vcol + 1],
                                              stat[:, vcol:vcol + 1], AF.Exp,
                                              scale=cv[:, 4:5]),
                       waits=[(sce, sce.n)], incs=[(sce, 1)])
                return
            ph.add("vector",
                   lambda e: e.tensor_scalar_mul(stat[:, mcol:mcol + 1],
                                                 stat[:, mcol:mcol + 1], 1.0 / D),
                   waits=[(vt, vt.n)], incs=[(vt, 1)])
            ph.add("vector",
                   lambda e: e.tensor_scalar_mul(stat[:, 14:15],
                                                 stat[:, vcol:vcol + 1], 1.0 / D),
                   waits=[(sce, sce.n)], incs=[(vt, 1)])
            ph.add("vector",
                   lambda e: e.tensor_mul(stat[:, 15:16], stat[:, mcol:mcol + 1],
                                          stat[:, mcol:mcol + 1]),
                   waits=[(vt, vt.n)], incs=[(vt, 1)])
            ph.add("vector",
                   lambda e: e.tensor_sub(stat[:, vcol:vcol + 1], stat[:, 14:15],
                                          stat[:, 15:16]),
                   waits=[(vt, vt.n)], incs=[(vt, 1)])
            ph.add("scalar",
                   lambda e: e.activation(stat[:, vcol:vcol + 1],
                                          stat[:, vcol:vcol + 1], AF.Ln, bias=eps_ap),
                   waits=[(vt, vt.n)], incs=[(sce, 1)])
            ph.add("scalar",
                   lambda e: e.activation(stat[:, vcol:vcol + 1],
                                          stat[:, vcol:vcol + 1], AF.Exp,
                                          scale=cv[:, 4:5]),
                   waits=[(sce, sce.n)], incs=[(sce, 1)])

        es0 = ExitStack()
        nwrow = sb(es0, "nwrow", [1, D], F32)
        boutrow = sb(es0, "boutrow", [1, D], F32)
        pbc = ps(es0, "pbc", [128, 1024], F32)
        ph = Phase()

        def ld(dst, src):
            ph.add("sync", lambda e, d=dst, s=src: e.dma_start(out=d, in_=s),
                   incs=[(sd, 16)])

        ld(eyef[:], eye_f)
        ld(cv[:], constv_d)
        ld(nwrow[:], norm_w.rearrange("(a d) -> a d", a=1))
        ld(boutrow[:], bout_d.rearrange("(a d) -> a d", a=1))
        nw_rep = sb(es0, "nw_rep", [128, D], F32)
        bout_rep_s = sb(es0, "bout_rep_s", [128, D], F32)
        ld(bx_sb[:], bx_d)
        ld(bz_sb[:], bz_d)
        ld(convb_sb[:], convb_d)
        ld(bdt_sb[:], bdt_d)
        ld(dp_sb[:], dp_d)
        ld(convw_sb[:], convw_d)
        ld(a_sb[:], alog_d)
        ph.add("gpsimd", lambda e: e.dma_start(out=eyeb[:], in_=eye_f),
               incs=[(gd, 16)])
        ph.add("gpsimd", lambda e: e.memset(onesb[:], 1.0), incs=[(gd, 1)])
        ph.add("gpsimd", lambda e: e.memset(onesf[:], 1.0), incs=[(gd, 1)])
        ph.add("scalar", lambda e: e.activation(a_sb[:], a_sb[:], AF.Exp),
               waits=[sd], incs=[(ss, 1)])
        ph.add("vector", lambda e: e.tensor_scalar_mul(a_sb[:], a_sb[:], -1.0),
               waits=[ss], incs=[(sv, 1)])
        ph.add("vector", lambda e: e.memset(hstate[:], 0.0), incs=[(sv, 1)])
        for h in range(2):
            ph.add("tensor",
                   lambda e, h=h: e.matmul(pbc[:, h * 512:(h + 1) * 512], onesf[:],
                                           nwrow[0:1, h * 512:(h + 1) * 512],
                                           start=True, stop=True),
                   waits=([sd, gd] if h == 0 else []), incs=[(st, 1)])
        ph.add("scalar", lambda e: e.copy(nw_rep[:], pbc[:]), waits=[st],
               incs=[(ss, 1)])
        for h in range(2):
            ph.add("tensor",
                   lambda e, h=h: e.matmul(pbc[:, h * 512:(h + 1) * 512], onesf[:],
                                           boutrow[0:1, h * 512:(h + 1) * 512],
                                           start=True, stop=True),
                   waits=([ss] if h == 0 else []), incs=[(st, 1)])
        ph.add("scalar", lambda e: e.copy(bout_rep_s[:], pbc[:]), waits=[st],
               incs=[(ss, 1)])
        ph.add("sync", lambda e: e.dma_start(out=nwrep_dram, in_=nw_rep[:]),
               waits=[ss], incs=[(sd, 16)])
        ph.add("sync", lambda e: e.dma_start(out=boutrep_dram, in_=bout_rep_s[:]),
               waits=[ss], incs=[(sd, 16)])
        ph.emit(nc)
        es0.close()

        esA = ExitStack()
        with esA:
            wxt = sb(esA, "wxt", [128, 8 * CH], BF16)
            wzt = sb(esA, "wzt", [128, 8 * CH], BF16)
            wxpt = sb(esA, "wxpt", [128, NJ * RR], BF16)
            xnT = sb(esA, "xnT", [128, 8 * CA], BF16)
            xb2k = sb(esA, "xb2k", [128, NJ * CA], BF16)
            zstile = sb(esA, "zstile", [128, CA], BF16)
            xbw = sb(esA, "xbw", [128, CA + 3], F32)
            xcw = sb(esA, "xcw", [128, CA], F32)
            xbtail = sb(esA, "xbtail", [128, NJ * 3], F32)
            xdblp = sb(esA, "xdblp", [96, L], F32)
            xtile = [sb(esA, f"xtile{i}", [128, D], F32) for i in range(2)]
            sqscr = sb(esA, "sqscrA", [128, D], F32)
            nw_rep = sb(esA, "nw_repA", [128, D], F32)
            utile = sb(esA, "utile", [128, D], F32)
            xntile = sb(esA, "xntile", [128, D], BF16)
            ptr = ps(esA, "ptr", [128, 1024], BF16)
            pxb2 = [ps(esA, f"pxb{i}", [128, 512], F32) for i in range(2)]
            pz2 = [ps(esA, f"pz{i}", [128, 512], F32) for i in range(2)]
            pxd = ps(esA, "pxd", [96, 512], F32)

            wcast = [sb(esA, f"wcast{i}", [128, 4096], BF16) for i in range(2)]
            gd_pc = {}
            sd_pc = {}
            pc_steps = []
            for e_i in range(E):
                for s8 in range(8):
                    pc_steps.append(("w1", e_i, s8 * 128))
                for s8 in range(8):
                    pc_steps.append(("w2", e_i, s8 * 512))

            ph = Phase()

            def emit_precast(step_i):
                tag, e_i, r0 = pc_steps[step_i]
                wc = wcast[step_i % 2]
                if tag == "w1":
                    src_ap = w1_d[e_i, r0:r0 + 128, :]
                    dst_ap = w1b_dram[e_i, r0:r0 + 128, :]
                    wview_i = wc[:]
                    wview_o = wc[:]
                else:
                    src_ap = w2_d[e_i, r0:r0 + 512, :].rearrange(
                        "(a p) q -> p a q", p=128)
                    dst_ap = w2b_dram[e_i, r0:r0 + 512, :].rearrange(
                        "(a p) q -> p a q", p=128)
                    wview_i = wc[:].rearrange("p (a q) -> p a q", a=4)
                    wview_o = wc[:].rearrange("p (a q) -> p a q", a=4)
                ph.add("gpsimd",
                       lambda e, s=src_ap, w=wview_i: e.dma_start(out=w, in_=s),
                       waits=[(s_pc, sd_pc.get(step_i - 2, 0))], incs=[(g_pc, 16)])
                gd_pc[step_i] = g_pc.n
                ph.add("sync",
                       lambda e, d=dst_ap, w=wview_o: e.dma_start(out=d, in_=w),
                       waits=[(g_pc, gd_pc[step_i])], incs=[(s_pc, 16)])
                sd_pc[step_i] = s_pc.n

            ph.add("sync", lambda e: e.dma_start(out=nw_rep[:], in_=nwrep_dram),
                   waits=[sd], incs=[(sd, 16)])
            for a in range(8):
                ph.add("gpsimd",
                       lambda e, a=a: e.dma_start(
                           out=wxt[:, a * CH:(a + 1) * CH],
                           in_=wxt_d[a * 128:(a + 1) * 128, :]),
                       incs=[(gd, 16)])
                ph.add("gpsimd",
                       lambda e, a=a: e.dma_start(
                           out=wzt[:, a * CH:(a + 1) * CH],
                           in_=wzt_d[a * 128:(a + 1) * 128, :]),
                       incs=[(gd, 16)])
            for a in range(NJ):
                ph.add("gpsimd",
                       lambda e, a=a: e.dma_start(
                           out=wxpt[:, a * RR:(a + 1) * RR],
                           in_=wxpt_d[a * 128:(a + 1) * 128, :]),
                       incs=[(gd, 16)])
            gd_wA = gd.n

            usp = {}
            zsp = {}
            xsp = {}
            for ch in range(NCA):
                t0 = ch * CA
                for _pc in range(32):
                    si = ch * 32 + _pc
                    if si < len(pc_steps):
                        emit_precast(si)
                for tt in range(16):
                    xt = xtile[tt % 2]
                    row = t0 + tt * 128
                    ph.add("sync",
                           lambda e, xt=xt, row=row: e.dma_start(
                               out=xt[:], in_=x_full[row:row + 128, :]),
                           waits=[sv, ss], incs=[(s_xt, 16)])
                    sd_xt = s_xt.n
                    ph.add("vector", lambda e: e.engine_nop(), waits=[(s_xt, sd_xt)])
                    emit_stats(ph, xt, sqscr, 0, 4, cv[:, 0:1])
                    ph.add("vector",
                           lambda e, xt=xt: e.tensor_scalar(utile[:], xt[:], stat[:, 0:1],
                                                            stat[:, 4:5], ALU.add, ALU.mult),
                           waits=[(s_usp, usp.get(0, 0)), (vt, vt.n), (sce, sce.n)],
                           incs=[(sv, 1)])
                    ph.add("sync",
                           lambda e, row=row: e.dma_start(out=u_dram[row:row + 128, :],
                                                          in_=utile[:]),
                           waits=[sv], incs=[(s_usp, 16)])
                    usp[0] = s_usp.n
                    emit_stats(ph, utile, sqscr, 0, 6, cv[:, 0:1], rms=True)
                    ph.add("vector",
                           lambda e: e.scalar_tensor_tensor(xntile[:], utile[:], stat[:, 6:7],
                                                            nw_rep[:], ALU.mult, ALU.mult),
                           waits=[st, (sce, sce.n)], incs=[(sv, 1)])
                    for a in range(8):
                        ph.add("tensor",
                               lambda e, a=a: e.transpose(ptr[:, a * 128:(a + 1) * 128],
                                                          xntile[:, a * 128:(a + 1) * 128],
                                                          eyeb[:]),
                               waits=([sv, ss] if a == 0 else []),
                               incs=([(st, 1)] if a == 7 else []))
                    ph.add("scalar",
                           lambda e, tt=tt: e.copy(
                               xnT[:].rearrange("p (a t) -> p a t", a=8)[:, :, tt * 128:(tt + 1) * 128],
                               ptr[:].rearrange("p (a t) -> p a t", a=8)),
                           waits=[st], incs=[(ss, 1)])

                sv_xnt = sv.n
                ss_xnt = ss.n
                xbev = {0: ss_xnt, 1: ss_xnt}
                zev = {0: ss_xnt, 1: ss_xnt}
                tail_vt = {}
                for j in range(NJ):
                    if ch == 0:
                        ph.add("vector", lambda e: e.memset(xbw[:, 0:3], 0.0),
                               waits=[ss, st], incs=[(vt, 1)])
                        tail_vt[j] = vt.n
                    else:
                        ph.add("vector",
                               lambda e, j=j: e.tensor_copy(xbw[:, 0:3],
                                                            xbtail[:, j * 3:(j + 1) * 3]),
                               waits=[ss, st], incs=[(vt, 1)])
                    tail_vt[j] = vt.n
                    for ts4 in range(4):
                        par = ts4 % 2
                        for a in range(8):
                            ph.add("tensor",
                                   lambda e, a=a, j=j, ts4=ts4, par=par: e.matmul(
                                       pxb2[par][:],
                                       wxt[:, a * CH + j * 128: a * CH + (j + 1) * 128],
                                       xnT[:, a * CA + ts4 * 512: a * CA + (ts4 + 1) * 512],
                                       start=(a == 0), stop=(a == 7)),
                                   waits=([(ss, xbev.get(par, 0)), (gd, gd_wA),
                                           (sv, sv_xnt)] if a == 0 else []),
                                   incs=([(st, 1)] if a == 7 else []))
                        xbmm = st.n
                        for a in range(8):
                            ph.add("tensor",
                                   lambda e, a=a, j=j, ts4=ts4, par=par: e.matmul(
                                       pz2[par][:],
                                       wzt[:, a * CH + j * 128: a * CH + (j + 1) * 128],
                                       xnT[:, a * CA + ts4 * 512: a * CA + (ts4 + 1) * 512],
                                       start=(a == 0), stop=(a == 7)),
                                   waits=([(ss, zev.get(par, 0))] if a == 0 else []),
                                   incs=([(st, 1)] if a == 7 else []))
                        zmm = st.n
                        ph.add("scalar",
                               lambda e, j=j, ts4=ts4, par=par: e.activation(
                                   xbw[:, 3 + ts4 * 512: 3 + (ts4 + 1) * 512],
                                   pxb2[par][:],
                                   AF.Identity, bias=bx_sb[:, j:j + 1]),
                               waits=[(st, xbmm), (vt, tail_vt.get(j, 0))],
                               incs=[(ss, 1)])
                        xbev[par] = ss.n
                        ph.add("scalar",
                               lambda e, ts4=ts4, j=j, par=par: e.activation(
                                   zstile[:, ts4 * 512:(ts4 + 1) * 512], pz2[par][:],
                                   AF.Silu,
                                   bias=bz_sb[:, j:j + 1]),
                               waits=[(st, zmm), (s_zsp, zsp.get(0, 0))],
                               incs=[(ss, 1)])
                        zev[par] = ss.n
                    ph.add("sync",
                           lambda e, j=j, t0=t0: e.dma_start(
                               out=zs_dram[j * 128:(j + 1) * 128, t0:t0 + CA],
                               in_=zstile[:]),
                           waits=[ss], incs=[(s_zsp, 16)])
                    zsp[0] = s_zsp.n
                    ph.add("vector",
                           lambda e, j=j: e.tensor_copy(xbtail[:, j * 3:(j + 1) * 3],
                                                        xbw[:, CA:CA + 3]),
                           waits=[ss], incs=[(sv, 1)])
                    ph.add("vector",
                           lambda e, j=j: e.tensor_scalar_mul(
                               xcw[:], xbw[:, 0:CA],
                               convw_sb[:, j * 4:j * 4 + 1]),
                           waits=[(vt, vt.n)], incs=[(sv, 1)])
                    for k in range(1, 4):
                        ph.add("vector",
                               lambda e, j=j, k=k: e.scalar_tensor_tensor(
                                   xcw[:], xbw[:, k:k + CA],
                                   convw_sb[:, j * 4 + k:j * 4 + k + 1],
                                   xcw[:], ALU.mult, ALU.add),
                               incs=([(sv, 1)] if k == 3 else []))
                    ph.add("scalar",
                           lambda e, j=j: e.activation(
                               xb2k[:, j * CA:(j + 1) * CA], xcw[:], AF.Silu,
                               bias=convb_sb[:, j:j + 1]),
                           waits=[sv, st, (s_xsp, xsp.get(j, 0))], incs=[(ss, 1)])
                    ph.add("sync",
                           lambda e, j=j, t0=t0: e.dma_start(
                               out=xb2_dram[j * 128:(j + 1) * 128, t0:t0 + CA],
                               in_=xb2k[:, j * CA:(j + 1) * CA]),
                           waits=[ss], incs=[(s_xsp, 16)])
                    xsp[j] = s_xsp.n

                for ts4 in range(4):
                    for j in range(NJ):
                        ph.add("tensor",
                               lambda e, j=j, ts4=ts4: e.matmul(
                                   pxd[:],
                                   wxpt[:, j * RR:(j + 1) * RR],
                                   xb2k[:, j * CA + ts4 * 512: j * CA + (ts4 + 1) * 512],
                                   start=(j == 0), stop=(j == NJ - 1)),
                               waits=([ss, sv] if j == 0 else []),
                               incs=([(st, 1)] if j == NJ - 1 else []))
                    ph.add("scalar",
                           lambda e, ts4=ts4, t0=t0: e.copy(
                               xdblp[0:96, t0 + ts4 * 512: t0 + (ts4 + 1) * 512], pxd[:]),
                           waits=[st], incs=[(ss, 1)])

            ph.add("sync", lambda e: e.dma_start(out=xdbl_in, in_=xdblp[0:96, :]),
                   waits=[ss], incs=[(sd, 16)])
            ph.emit(nc)

        ph = Phase()
        ph.add("gpsimd",
               lambda e: e.collective_compute(
                   "AllReduce", ALU.add, replica_groups=PAIRS,
                   ins=[xdbl_in.opt()], outs=[xdbl_out.opt()]),
               waits=bw(), incs=[(c1, None)])
        ph.emit(nc)

        esB = ExitStack()
        with esB:
            wdtt = sb(esB, "wdtt", [64, CH], BF16)
            woutt = sb(esB, "woutt", [128, NJ * D], BF16)
            dlowb = sb(esB, "dlowb", [64, QB], BF16)
            bc1 = sb(esB, "bc1", [33, 16 * QB], BF16)
            dt_sb = sb(esB, "dt_sb", [128, NJ * QB], F32)
            u_sb = sb(esB, "u_sb", [128, NJ * QB], F32)
            xb2c = sb(esB, "xb2c", [128, NJ * QB], BF16)
            zsc = sb(esB, "zsc", [128, NJ * QB], BF16)
            dA = sb(esB, "dA", [128, NJ * QB], F32)
            dA_b = sb(esB, "dA_b", [128, NJ * QB], F32)
            dBx = sb(esB, "dBx", [128, NJ * QB], F32)
            dBx_b = sb(esB, "dBx_b", [128, NJ * QB], F32)
            bbsb = sb(esB, "bbsb", [128, 2 * QB], F32)
            h_n = sb(esB, "h_n", [128, NJ * QB], F32)
            y_sb = sb(esB, "y_sb", [128, NJ * QB], F32)
            yz_bf = sb(esB, "yz_bf", [128, NJ * QB], BF16)
            uev0 = sb(esB, "uev0", [128, D], F32)
            uev = [uev0, uev0]
            outp = [sb(esB, f"outp{i}", [128, D], F32) for i in range(2)]
            tabs = h_n[:, 0:QB]
            tln = h_n[:, QB:2 * QB]
            pdt = ps(esB, "pdt", [128, 512], F32)
            pb_b = ps(esB, "pb_b", [128, 512], F32)
            pb_c = ps(esB, "pb_c", [128, 512], F32)
            pb_b2 = ps(esB, "pb_b2", [128, 512], F32)
            pb_c2 = ps(esB, "pb_c2", [128, 512], F32)
            po = ps(esB, "po", [128, 1024], F32)

            ph = Phase()
            for a in range(NJ):
                ph.add("gpsimd",
                       lambda e, a=a: e.dma_start(
                           out=woutt[:, a * D:(a + 1) * D],
                           in_=woutt_d[a * 128:(a + 1) * 128, :]),
                       incs=[(gd, 16)])
            ph.add("gpsimd", lambda e: e.dma_start(out=wdtt[:], in_=wdtt_d),
                   incs=[(gd, 16)])

            dA2 = [dA, dA_b]
            pbb = [pb_b, pb_b2]
            pbcc = [pb_c, pb_c2]
            yadd_done = {-2: 0, -1: 0}
            bcast_ok = {-2: 0, -1: 0}
            for cb in range(NCB):
                t0 = cb * QB
                ph.add("gpsimd",
                       lambda e, t0=t0: e.dma_start(out=dlowb[:],
                                                    in_=xdbl_out[0:64, t0:t0 + QB]),
                       waits=[(c1, 1), st, ss], incs=[(g_xb, 16)])
                ph.add("gpsimd",
                       lambda e, t0=t0: e.dma_start(
                           out=bc1[0:1, :].rearrange("q (n t) -> q n t", n=16),
                           in_=xdbl_out[64:80, t0:t0 + QB]),
                       waits=[(c1, 1), st], incs=[(g_xb, 16)])
                ph.add("gpsimd",
                       lambda e, t0=t0: e.dma_start(
                           out=bc1[32:33, :].rearrange("q (n t) -> q n t", n=16),
                           in_=xdbl_out[80:96, t0:t0 + QB]),
                       waits=[(c1, 1), st], incs=[(g_xb, 16)])
                for j in range(NJ):
                    ph.add("sync",
                           lambda e, t0=t0, j=j: e.dma_start(
                               out=xb2c[:, j * QB:(j + 1) * QB],
                               in_=xb2_dram[j * 128:(j + 1) * 128, t0:t0 + QB]),
                           waits=([sv, (s_xsp, s_xsp.n), (s_zsp, s_zsp.n)]
                                  if j == 0 else []),
                           incs=[(s_bl, 16)])
                    ph.add("sync",
                           lambda e, t0=t0, j=j: e.dma_start(
                               out=zsc[:, j * QB:(j + 1) * QB],
                               in_=zs_dram[j * 128:(j + 1) * 128, t0:t0 + QB]),
                           incs=[(s_bl, 16)])

                gd_cb = g_xb.n
                for j in range(NJ):
                    ph.add("tensor",
                           lambda e, j=j: e.matmul(pdt[:, 0:QB],
                                                   wdtt[:, j * 128:(j + 1) * 128],
                                                   dlowb[:], start=True, stop=True),
                           waits=([(g_xb, gd_cb), ss, sv] if j == 0 else [ss]),
                           incs=[(st, 1)])
                    ph.add("scalar",
                           lambda e, j=j: e.activation(tabs, pdt[:, 0:QB], AF.Abs,
                                                       bias=bdt_sb[:, j:j + 1]),
                           waits=[st])
                    ph.add("scalar",
                           lambda e: e.activation(tabs, tabs, AF.Exp, scale=cv[:, 3:4]))
                    ph.add("scalar",
                           lambda e: e.activation(tln, tabs, AF.Ln, bias=cv[:, 2:3]))
                    ph.add("scalar",
                           lambda e, j=j: e.activation(tabs, pdt[:, 0:QB], AF.Relu,
                                                       bias=bdt_sb[:, j:j + 1]),
                           incs=[(ss, 1)])
                    ph.add("vector",
                           lambda e, j=j: e.tensor_add(dt_sb[:, j * QB:(j + 1) * QB],
                                                       tabs, tln),
                           waits=[ss], incs=[(sv, 1)])
                    ph.add("vector",
                           lambda e, j=j: e.tensor_mul(u_sb[:, j * QB:(j + 1) * QB],
                                                       dt_sb[:, j * QB:(j + 1) * QB],
                                                       xb2c[:, j * QB:(j + 1) * QB]),
                           waits=[(s_bl, s_bl.n)], incs=[(sv, 1)])
                sv_u = sv.n
                sv_u_vt = vt.n

                dBx2 = [dBx, dBx_b]
                scan_done = {-2: sv_u_vt, -1: sv_u_vt}
                cm_done = {-2: sv_u, -1: sv_u}
                yadd_done = {-2: sv_u, -1: sv_u}
                for n in range(N16):
                    par = n % 2
                    ph.add("tensor",
                           lambda e, n=n, par=par: e.matmul(
                               pbb[par][:, 0:QB], onesb[0:1, :],
                               bc1[0:1, n * QB:(n + 1) * QB],
                               start=True, stop=True),
                           waits=[(sv, yadd_done[n - 2])], incs=[(st, 1)])
                    ph.add("tensor",
                           lambda e, n=n, par=par: e.matmul(
                               pbcc[par][:, 0:QB], onesb[32:33, :],
                               bc1[32:33, n * QB:(n + 1) * QB],
                               start=True, stop=True),
                           incs=[(st, 1)])
                    bcast_ok[n] = st.n
                    for j in range(NJ):
                        ph.add("scalar",
                               lambda e, j=j, n=n, par=par: e.activation(
                                   dA2[par][:, j * QB:(j + 1) * QB],
                                   dt_sb[:, j * QB:(j + 1) * QB], AF.Exp,
                                   scale=a_sb[:, j * N16 + n: j * N16 + n + 1]),
                               waits=([(sv, yadd_done[n - 2])] if j == 0 else []),
                               incs=([(ss, 1)] if j == NJ - 1 else []))
                    ss_dA = ss.n
                    for j in range(NJ):
                        ph.add("vector",
                               lambda e, j=j, par=par: e.tensor_mul(
                                   dBx2[par][:, j * QB:(j + 1) * QB],
                                   u_sb[:, j * QB:(j + 1) * QB],
                                   pbb[par][:, 0:QB]),
                               waits=([(st, bcast_ok[n]), (ss, ss_dA)] if j == 0 else []))
                    for j in range(NJ):
                        ph.add("vector",
                               lambda e, j=j, n=n, par=par: e.scalar_tensor_tensor(
                                   dBx2[par][:, j * QB:j * QB + 1],
                                   dA2[par][:, j * QB:j * QB + 1],
                                   hstate[:, j * N16 + n:j * N16 + n + 1],
                                   dBx2[par][:, j * QB:j * QB + 1], ALU.mult, ALU.add),
                               incs=([(vt, 1)] if j == NJ - 1 else []))
                    for j in range(NJ):
                        ph.add("vector",
                               lambda e, j=j, par=par: e.tensor_tensor_scan(
                                   h_n[:, j * QB:(j + 1) * QB],
                                   dA2[par][:, j * QB:(j + 1) * QB],
                                   dBx2[par][:, j * QB:(j + 1) * QB],
                                   0.0, ALU.mult, ALU.add),
                               waits=([(vt, vt.n)] if j == 0 else []),
                               incs=([(vt, 1)] if j == NJ - 1 else []))
                    scan_done[n] = vt.n
                    for j in range(NJ):
                        ph.add("vector",
                               lambda e, j=j, n=n: e.tensor_copy(
                                   hstate[:, j * N16 + n:j * N16 + n + 1],
                                   h_n[:, (j + 1) * QB - 1:(j + 1) * QB]),
                               waits=([(vt, vt.n)] if j == 0 else []))
                    for j in range(NJ):
                        ph.add("vector",
                               lambda e, j=j, par=par: e.tensor_mul(
                                   dA2[par][:, j * QB:(j + 1) * QB],
                                   h_n[:, j * QB:(j + 1) * QB],
                                   pbcc[par][:, 0:QB]))
                        if n == 0:
                            ph.add("vector",
                                   lambda e, j=j, par=par: e.tensor_copy(
                                       y_sb[:, j * QB:(j + 1) * QB],
                                       dA2[par][:, j * QB:(j + 1) * QB]),
                                   incs=([(sv, 1)] if j == NJ - 1 else []))
                        else:
                            ph.add("vector",
                                   lambda e, j=j, par=par: e.tensor_add(
                                       y_sb[:, j * QB:(j + 1) * QB],
                                       y_sb[:, j * QB:(j + 1) * QB],
                                       dA2[par][:, j * QB:(j + 1) * QB]),
                                   incs=([(sv, 1)] if j == NJ - 1 else []))
                    yadd_done[n] = sv.n
                    cm_done[n] = sv.n

                for j in range(NJ):
                    ph.add("vector",
                           lambda e, j=j: e.scalar_tensor_tensor(
                               y_sb[:, j * QB:(j + 1) * QB],
                               xb2c[:, j * QB:(j + 1) * QB],
                               dp_sb[:, j:j + 1],
                               y_sb[:, j * QB:(j + 1) * QB], ALU.mult, ALU.add))
                    ph.add("vector",
                           lambda e, j=j: e.tensor_mul(yz_bf[:, j * QB:(j + 1) * QB],
                                                       y_sb[:, j * QB:(j + 1) * QB],
                                                       zsc[:, j * QB:(j + 1) * QB]),
                           incs=[(sv, 1)])

                for tsub in range(QB // 128):
                    row = t0 + tsub * 128
                    bfi = tsub % 2
                    ph.add("sync",
                           lambda e, row=row, bfi=bfi: e.dma_start(
                               out=uev[bfi][:], in_=u_dram[row:row + 128, :]),
                           waits=[sv, (s_usp, s_usp.n)], incs=[(s_ue, 16)])
                    for j in range(NJ):
                        for dh in range(2):
                            ph.add("tensor",
                                   lambda e, j=j, dh=dh, tsub=tsub: e.matmul(
                                       po[:, dh * 512:(dh + 1) * 512],
                                       yz_bf[:, j * QB + tsub * 128: j * QB + (tsub + 1) * 128],
                                       woutt[:, j * D + dh * 512: j * D + (dh + 1) * 512],
                                       start=(j == 0), stop=(j == NJ - 1)),
                                   waits=([sv] if (dh == 0 and j == 0) else []),
                                   incs=([(st, 1)] if (dh == 1 and j == NJ - 1) else []))
                    ph.add("vector",
                           lambda e, bfi=bfi: e.scalar_tensor_tensor(
                               outp[bfi][:, 0:512], uev[bfi][:, 0:512], 0.5,
                               po[:, 0:512], ALU.mult, ALU.add),
                           waits=[st, (s_ue, s_ue.n)], incs=[(sv, 1)])
                    ph.add("vector",
                           lambda e, bfi=bfi: e.scalar_tensor_tensor(
                               outp[bfi][:, 512:1024], uev[bfi][:, 512:1024], 0.5,
                               po[:, 512:1024], ALU.mult, ALU.add),
                           incs=[(sv, 1)])
                    ph.add("sync",
                           lambda e, row=row, bfi=bfi: e.dma_start(
                               out=rs_in[row:row + 128, :], in_=outp[bfi][:]),
                           waits=[sv], incs=[(sd, 16)])
            ph.emit(nc)

        ph = Phase()
        ph.add("gpsimd",
               lambda e: e.collective_compute(
                   "ReduceScatter", ALU.add, replica_groups=PAIRS,
                   ins=[rs_in.opt()], outs=[rs_out.opt()]),
               waits=bw(), incs=[(c2, None)])
        ph.emit(nc)

        esFD = ExitStack()
        with esFD:
            gamma_sb = sb(esFD, "gamma_sb", [128, E * 8], F32)
            beta_sb = sb(esFD, "beta_sb", [128, E * 8], F32)
            b1_sb = sb(esFD, "b1_sb", [128, E * 32], F32)
            cwT_sb = sb(esFD, "cwT_sb", [8, TH], BF16)

            esC = ExitStack()
            with esC:
                xhatT = sb(esC, "xhatT", [128, 8 * TH], BF16)
                wg_sb = sb(esC, "wg_sb", [128, 8 * E], F32)
                b2_sb = sb(esC, "b2_sb", [8, D], BF16)
                cw_sb = sb(esC, "cw_sb", [128, (TH // 128) * E], F32)
                sqscr = sb(esC, "sqscrC", [128, D], F32)
                rst = [sb(esC, f"rst{i}", [128, D], F32) for i in range(2)]
                xht = [sb(esC, f"xht{i}", [128, D], F32) for i in range(2)]
                aow = sb(esC, "aow", [128, D], F32)
                x1w = sb(esC, "x1w", [128, D], F32)
                vw = sb(esC, "vw", [128, D], F32)
                xhbf = sb(esC, "xhbf", [128, D], BF16)
                vT = sb(esC, "vT", [128, 8 * 128], F32)
                sct = sb(esC, "sct", [128, 8], F32)
                cwt = sb(esC, "cwt", [128, 8], F32)
                seedw = sb(esC, "seedw", [128, TH], F32)
                bout_rep = sb(esC, "bout_repC", [128, D], F32)
                ptrv = ps(esC, "ptrv", [128, 1024], F32)
                ptrb = ps(esC, "ptrb", [128, 1024], BF16)
                psc = ps(esC, "psc", [128, 512], F32)
                pb2 = ps(esC, "pb2", [128, 1024], F32)

                ph = Phase()
                for dst, srcd in ((wg_sb, wg_d), (gamma_sb, gamma_d),
                                  (beta_sb, beta_d), (b1_sb, b1_d)):
                    ph.add("sync",
                           lambda e, d=dst, s=srcd: e.dma_start(out=d[:], in_=s),
                           incs=[(sd, 16)])
                ph.add("gpsimd", lambda e: e.dma_start(out=b2_sb[:], in_=b2_d),
                       incs=[(gd, 16)])
                gd_b2 = gd.n
                ph.add("sync", lambda e: e.dma_start(out=bout_rep[:], in_=boutrep_dram),
                       incs=[(sd, 16)])

                for tt in range(TH // 128):
                    row = tt * 128
                    bfi = tt % 2
                    ph.add("sync",
                           lambda e, row=row, bfi=bfi: e.dma_start(
                               out=rst[bfi][:], in_=rs_out[row:row + 128, :]),
                           waits=[(c2, 1), sv, ss], incs=[(sd, 16)])
                    ph.add("sync",
                           lambda e, row=row, bfi=bfi: e.dma_start(
                               out=xht[bfi][:], in_=x_half[row:row + 128, :]),
                           incs=[(sd, 16)])
                    ph.add("vector",
                           lambda e, bfi=bfi: e.tensor_add(aow[:], rst[bfi][:], bout_rep[:]),
                           waits=[sd, ss], incs=[(sv, 1)])
                    emit_stats(ph, aow, sqscr, 0, 1, cv[:, 0:1])
                    ph.add("vector",
                           lambda e: e.tensor_scalar(aow[:], aow[:], stat[:, 0:1],
                                                     stat[:, 1:2], ALU.add, ALU.mult),
                           waits=[(vt, vt.n), (sce, sce.n)], incs=[(sv, 1)])
                    ph.add("vector",
                           lambda e, bfi=bfi: e.tensor_add(x1w[:], xht[bfi][:], aow[:]),
                           incs=[(sv, 1)])
                    ph.add("sync",
                           lambda e, row=row: e.dma_start(out=x1_dram[row:row + 128, :],
                                                          in_=x1w[:]),
                           waits=[sv], incs=[(sd, 16)])
                    emit_stats(ph, x1w, sqscr, 2, 3, cv[:, 0:1])
                    ph.add("vector",
                           lambda e: e.tensor_scalar(vw[:], x1w[:], stat[:, 2:3],
                                                     stat[:, 3:4], ALU.add, ALU.mult),
                           waits=[ss, st, sd, (vt, vt.n), (sce, sce.n)], incs=[(sv, 1)])
                    ph.add("sync",
                           lambda e, row=row: e.dma_start(out=v_dram[row:row + 128, :],
                                                          in_=vw[:]),
                           waits=[sv], incs=[(sd, 16)])
                    for a in range(8):
                        ph.add("tensor",
                               lambda e, a=a: e.transpose(ptrv[:, a * 128:(a + 1) * 128],
                                                          vw[:, a * 128:(a + 1) * 128],
                                                          eyef[:]),
                               waits=([sv, ss] if a == 0 else []),
                               incs=([(st, 1)] if a == 7 else []))
                    ph.add("scalar", lambda e: e.copy(vT[:], ptrv[:]), waits=[st],
                           incs=[(ss, 1)])
                    for a in range(8):
                        ph.add("tensor",
                               lambda e, a=a: e.matmul(psc[:, 0:8],
                                                       vT[:, a * 128:(a + 1) * 128],
                                                       wg_sb[:, a * E:(a + 1) * E],
                                                       start=(a == 0), stop=(a == 7)),
                               waits=([ss] if a == 0 else []),
                               incs=([(st, 1)] if a == 7 else []))
                    ph.add("scalar", lambda e: e.copy(sct[:], psc[:, 0:8]), waits=[st],
                           incs=[(ss, 1)])

                    def tv(fn, extra=()):
                        ph.add("vector", fn,
                               waits=list(extra) + [(vt, vt.n)],
                               incs=[(vt, 1)])
                    tv(lambda e: e.reduce_max(stat[:, 8:9], sct[:], axis=AX.X,
                                              negate=True), extra=[ss])
                    tv(lambda e: e.tensor_single_scalar(cwt[:], sct[:], stat[:, 8:9],
                                                        ALU.add))
                    tv(lambda e: e.tensor_single_scalar(cwt[:], cwt[:], 0.0, ALU.is_ge))
                    tv(lambda e: e.scalar_tensor_tensor(cwt[:], cwt[:], -1e30, sct[:],
                                                        ALU.mult, ALU.add))
                    tv(lambda e: e.reduce_max(stat[:, 9:10], cwt[:], axis=AX.X))
                    tv(lambda e: e.tensor_scalar_mul(stat[:, 10:11], stat[:, 9:10], -1.0))
                    tv(lambda e: e.tensor_single_scalar(cwt[:], sct[:], stat[:, 10:11],
                                                        ALU.add))
                    tv(lambda e: e.tensor_single_scalar(cwt[:], cwt[:], 0.0, ALU.is_ge))
                    ph.add("scalar",
                           lambda e: e.activation(sct[:], sct[:], AF.Exp, bias=stat[:, 8:9]),
                           waits=[(vt, vt.n), (sce, sce.n)], incs=[(sce, 1)])
                    tv(lambda e: e.tensor_mul(cwt[:], cwt[:], sct[:]),
                       extra=[(sce, sce.n)])
                    tv(lambda e: e.reduce_sum(stat[:, 11:12], cwt[:], axis=AX.X))
                    tv(lambda e: e.reciprocal(stat[:, 12:13], stat[:, 11:12]))
                    tv(lambda e, tt=tt: e.tensor_scalar_mul(
                           cw_sb[:, tt * E:(tt + 1) * E], cwt[:], stat[:, 12:13]))
                    ph.add("tensor",
                           lambda e, tt=tt: e.transpose(psc[0:8, 128:256],
                                                        cw_sb[:, tt * E:(tt + 1) * E],
                                                        eyef[:]),
                           waits=[(vt, vt.n)], incs=[(st, 1)])
                    ph.add("scalar",
                           lambda e, tt=tt: e.copy(cwT_sb[0:8, tt * 128:(tt + 1) * 128],
                                                   psc[0:8, 128:256]),
                           waits=[st], incs=[(ss, 1)])
                    emit_stats(ph, vw, sqscr, 4, 5, cv[:, 1:2])
                    ph.add("vector",
                           lambda e: e.tensor_scalar(xhbf[:], vw[:], stat[:, 4:5],
                                                     stat[:, 5:6], ALU.add, ALU.mult),
                           waits=[ss, st, (vt, vt.n), (sce, sce.n)], incs=[(sv, 1)])
                    for a in range(8):
                        ph.add("tensor",
                               lambda e, a=a: e.transpose(ptrb[:, a * 128:(a + 1) * 128],
                                                          xhbf[:, a * 128:(a + 1) * 128],
                                                          eyeb[:]),
                               waits=([sv, ss] if a == 0 else []),
                               incs=([(st, 1)] if a == 7 else []))
                    ph.add("scalar",
                           lambda e, tt=tt: e.copy(
                               xhatT[:].rearrange("p (a t) -> p a t", a=8)[:, :, tt * 128:(tt + 1) * 128],
                               ptrb[:].rearrange("p (a t) -> p a t", a=8)),
                           waits=[st], incs=[(ss, 1)])
                    if tt == 15 or tt == TH // 128 - 1:
                        hoff = 0 if tt == 15 else 2048
                        for a in range(8):
                            ph.add("sync",
                                   lambda e, a=a, hoff=hoff: e.dma_start(
                                       out=xhatT_dram[:, a * TH + hoff: a * TH + hoff + 2048],
                                       in_=xhatT[:, a * TH + hoff: a * TH + hoff + 2048]),
                                   waits=([ss] if a == 0 else []), incs=[(s_xh, 16)])
                for dtile in range(8):
                    for tc in range(8):
                        half = tc % 2
                        ph.add("tensor",
                               lambda e, dtile=dtile, tc=tc, half=half: e.matmul(
                                   pb2[:, half * 512:(half + 1) * 512],
                                   b2_sb[0:8, dtile * 128:(dtile + 1) * 128],
                                   cwT_sb[0:8, tc * 512:(tc + 1) * 512],
                                   start=True, stop=True),
                               waits=[ss, (gd, gd_b2)], incs=[(st, 1)])
                        ph.add("scalar",
                               lambda e, tc=tc, half=half: e.copy(
                                   seedw[:, tc * 512:(tc + 1) * 512],
                                   pb2[:, half * 512:(half + 1) * 512]),
                               waits=[st], incs=[(ss, 1)])
                    ph.add("sync",
                           lambda e, dtile=dtile: e.dma_start(
                               out=ymoe_dram[dtile * 128:(dtile + 1) * 128, :],
                               in_=seedw[:]),
                           waits=[ss], incs=[(s_yio, 16)])
                ph.emit(nc)

            TSG = 2048
            NTSG = TH // TSG
            esD = ExitStack()
            with esD:
                w1h = sb(esD, "w1h", [128, 8 * 2048], BF16)
                w2h = sb(esD, "w2h", [128, 16 * 1024], BF16)
                ntb = sb(esD, "ntb", [128, 8 * TSG], BF16)
                hbig = sb(esD, "hbig", [128, 16 * TSG], BF16)
                cwrow = sb(esD, "cwrow", [1, TSG], BF16)
                cwrep = sb(esD, "cwrep", [128, TSG], BF16)
                ydt = [sb(esD, f"ydt{i}", [128, TSG], F32) for i in range(2)]
                ytmp1 = sb(esD, "ytmp1", [128, TSG], F32)
                pA = ps(esD, "pA", [128, 2048], F32)
                pB = ps(esD, "pB", [128, 2048], F32)

                ph = Phase()
                units = [(e_i, fh) for e_i in range(E) for fh in range(2)]

                def load_w1(e_i, fh, wts):
                    for kb in range(8):
                        ph.add("scalar",
                               lambda e, e_i=e_i, fh=fh, kb=kb: e.dma_start(
                                   out=w1h[:, kb * 2048:(kb + 1) * 2048],
                                   in_=w1b_dram[e_i, kb * 128:(kb + 1) * 128,
                                                fh * 2048:(fh + 1) * 2048]),
                               waits=(wts if kb == 0 else []), incs=[(w_w1, 16)])
                    return w_w1.n

                def load_w2(e_i, fh, wts):
                    for a in range(16):
                        ph.add("scalar",
                               lambda e, e_i=e_i, fh=fh, a=a: e.dma_start(
                                   out=w2h[:, a * 1024:(a + 1) * 1024],
                                   in_=w2b_dram[e_i, fh * 2048 + a * 128:
                                                fh * 2048 + (a + 1) * 128, :]),
                               waits=(wts if a == 0 else []), incs=[(w_w2, 16)])
                    return w_w2.n

                seed_yio = s_yio.n
                yout_sd = {}
                gd_w1r = {0: load_w1(*units[0], wts=[(s_pc, sd_pc[len(pc_steps) - 1])])}
                gd_w2r = {0: load_w2(*units[0], wts=[])}
                for ui, (e_i, fh) in enumerate(units):
                    for tsg in range(NTSG):
                        t0g = tsg * TSG
                        for a in range(8):
                            ph.add("sync",
                                   lambda e, a=a, t0g=t0g: e.dma_start(
                                       out=ntb[:, a * TSG:(a + 1) * TSG],
                                       in_=xhatT_dram[:, a * TH + t0g: a * TH + t0g + TSG]),
                                   waits=([(gc, gc.n), st,
                                           (s_xh, 128 if t0g == 0 else 256)]
                                          if a == 0 else []),
                                   incs=[(s_nt, 16)])
                        sd_nt = s_nt.n
                        for kb in range(8):
                            ph.add("gpsimd",
                                   lambda e, e_i=e_i, kb=kb: e.tensor_scalar(
                                       ntb[:, kb * TSG:(kb + 1) * TSG],
                                       ntb[:, kb * TSG:(kb + 1) * TSG],
                                       gamma_sb[:, e_i * 8 + kb:e_i * 8 + kb + 1],
                                       beta_sb[:, e_i * 8 + kb:e_i * 8 + kb + 1],
                                       ALU.mult, ALU.add),
                                   waits=([(s_nt, sd_nt)] if kb == 0 else []),
                                   incs=([(gc, 1)] if kb == 7 else []))
                        gc_nt = gc.n
                        ph.add("sync",
                               lambda e, e_i=e_i, t0g=t0g: e.dma_start(
                                   out=cwrow[:],
                                   in_=cwT_sb[e_i:e_i + 1, t0g:t0g + TSG]),
                               waits=[ss], incs=[(s_cw, 16)])
                        sd_cw = s_cw.n
                        for tc in range(4):
                            ph.add("tensor",
                                   lambda e, tc=tc: e.matmul(
                                       pB[:, tc * 512:(tc + 1) * 512], onesb[0:1, :],
                                       cwrow[0:1, tc * 512:(tc + 1) * 512],
                                       start=True, stop=True),
                                   waits=([(s_cw, sd_cw), ss, sv] if tc == 0 else []),
                                   incs=([(st, 1)] if tc == 3 else []))
                        ph.add("scalar", lambda e: e.copy(cwrep[:], pB[:]),
                               waits=[st, (gc, gc.n)], incs=[(ss, 1)])
                        ss_cw = ss.n
                        hpp_done = {-2: ss_cw, -1: ss_cw}
                        hm_done = {}
                        for fidx in range(16):
                            fb, ft = fidx // 4, fidx % 4
                            fcol = fb * 512 + ft * 128
                            fi = fh * 16 + fb * 4 + ft
                            pp = pA if fidx % 2 == 0 else pB
                            for kb in range(8):
                                for tc4 in range(4):
                                    w = []
                                    if kb == 0 and tc4 == 0:
                                        w = [(gc, gc_nt), (w_w1, gd_w1r[ui]),
                                             (ss, hpp_done[fidx - 2])]
                                    ph.add("tensor",
                                           lambda e, kb=kb, fcol=fcol, tc4=tc4, pp=pp: e.matmul(
                                               pp[:, tc4 * 512:(tc4 + 1) * 512],
                                               w1h[:, kb * 2048 + fcol: kb * 2048 + fcol + 128],
                                               ntb[:, kb * TSG + tc4 * 512: kb * TSG + (tc4 + 1) * 512],
                                               start=(kb == 0), stop=(kb == 7)),
                                           waits=w,
                                           incs=([(st, 1)] if (kb == 7 and tc4 == 3) else []))
                            for tc4 in range(4):
                                ph.add("scalar",
                                       lambda e, fidx=fidx, fi=fi, e_i=e_i, tc4=tc4, pp=pp: e.activation(
                                           hbig[:, fidx * TSG + tc4 * 512:
                                                fidx * TSG + (tc4 + 1) * 512],
                                           pp[:, tc4 * 512:(tc4 + 1) * 512], AF.Relu,
                                           bias=b1_sb[:, e_i * 32 + fi:e_i * 32 + fi + 1]),
                                       waits=([st] if tc4 == 0 else []),
                                       incs=([(ss, 1)] if tc4 == 3 else []))
                            hpp_done[fidx] = ss.n
                            ph.add("gpsimd",
                                   lambda e, fidx=fidx: e.tensor_mul(
                                       hbig[:, fidx * TSG:(fidx + 1) * TSG],
                                       hbig[:, fidx * TSG:(fidx + 1) * TSG],
                                       cwrep[:]),
                                   waits=[(ss, hpp_done[fidx])],
                                   incs=[(gc, 1)])
                            hm_done[fidx] = gc.n
                        if tsg == NTSG - 1 and ui + 1 < len(units):
                            gd_w1r[ui + 1] = load_w1(*units[ui + 1], wts=[st])
                        copy_done = {-2: 0, -1: 0}
                        gadd_done = {-2: 0, -1: 0}
                        for dt in range(8):
                            pp = pB if dt % 2 == 0 else pA
                            yb = ydt[dt % 2]
                            yt = ytmp1
                            wdep = max(yout_sd.get((dt, t0g), seed_yio),
                                       yout_sd.get(("buf", dt % 2), 0))
                            ph.add("sync",
                                   lambda e, dt=dt, t0g=t0g, yb=yb: e.dma_start(
                                       out=yb[:],
                                       in_=ymoe_dram[dt * 128:(dt + 1) * 128, t0g:t0g + TSG]),
                                   waits=[(s_yio, wdep)], incs=[(s_yio, 16)])
                            sd_y = s_yio.n
                            for ftile in range(16):
                                for tc4 in range(4):
                                    w = []
                                    if ftile == 0 and tc4 == 0:
                                        w = [(gc, hm_done[15]),
                                             (ss, copy_done[dt - 2]),
                                             (w_w2, gd_w2r[ui])]
                                    ph.add("tensor",
                                           lambda e, dt=dt, ftile=ftile, tc4=tc4, pp=pp: e.matmul(
                                               pp[:, tc4 * 512:(tc4 + 1) * 512],
                                               w2h[:, ftile * 1024 + dt * 128: ftile * 1024 + dt * 128 + 128],
                                               hbig[:, ftile * TSG + tc4 * 512: ftile * TSG + (tc4 + 1) * 512],
                                               start=(ftile == 0), stop=(ftile == 15)),
                                           waits=w,
                                           incs=([(st, 1)] if (ftile == 15 and tc4 == 3) else []))
                            ph.add("scalar",
                                   lambda e, pp=pp, yt=yt: e.copy(yt[:], pp[:]),
                                   waits=[st, (gc, gadd_done[dt - 1])],
                                   incs=[(ss, 1)])
                            copy_done[dt] = ss.n
                            ph.add("gpsimd",
                                   lambda e, yb=yb, yt=yt: e.tensor_add(yb[:], yb[:], yt[:]),
                                   waits=[(ss, copy_done[dt]), (s_yio, sd_y)],
                                   incs=[(gc, 1)])
                            gadd_done[dt] = gc.n
                            ph.add("sync",
                                   lambda e, dt=dt, t0g=t0g, yb=yb: e.dma_start(
                                       out=ymoe_dram[dt * 128:(dt + 1) * 128, t0g:t0g + TSG],
                                       in_=yb[:]),
                                   waits=[(gc, gadd_done[dt])], incs=[(s_yio, 16)])
                            yout_sd[(dt, t0g)] = s_yio.n
                            yout_sd[("buf", dt % 2)] = s_yio.n
                        if tsg == NTSG - 1 and ui + 1 < len(units):
                            gd_w2r[ui + 1] = load_w2(*units[ui + 1], wts=[st])
                ph.emit(nc)

        esE = ExitStack()
        with esE:
            ydgrp = sb(esE, "ydgrp", [128, 8 * 512], F32)
            ytl = sb(esE, "ytl", [128, D], F32)
            vtl = [sb(esE, f"vtl{i}", [128, D], F32) for i in range(2)]
            x1t = [sb(esE, f"x1t{i}", [128, D], F32) for i in range(2)]
            fyw = sb(esE, "fyw", [128, D], F32)
            sqscr = sb(esE, "sqscrE", [128, D], F32)
            otl = sb(esE, "otl", [128, D], F32)
            ptrE = ps(esE, "ptrE", [128, 1024], F32)

            ph = Phase()
            for tt in range(TH // 128):
                row = tt * 128
                bfi = tt % 2
                if tt % 4 == 0:
                    ph.add("sync",
                           lambda e, row=row: e.dma_start(
                               out=ydgrp[:].rearrange("p (a t) -> p a t", a=8),
                               in_=ymoe_dram.rearrange("(a p) t -> p a t", p=128)
                                   [:, :, row:row + 512]),
                           waits=bw(), incs=[(sd, 16)])
                ph.add("sync",
                       lambda e, row=row, bfi=bfi: e.dma_start(
                           out=vtl[bfi][:], in_=v_dram[row:row + 128, :]),
                       waits=([sv] if tt >= 2 else []), incs=[(sd, 16)])
                ph.add("sync",
                       lambda e, row=row, bfi=bfi: e.dma_start(
                           out=x1t[bfi][:], in_=x1_dram[row:row + 128, :]),
                       incs=[(sd, 16)])
                for a in range(8):
                    ph.add("tensor",
                           lambda e, a=a, tt=tt: e.transpose(
                               ptrE[:, a * 128:(a + 1) * 128],
                               ydgrp[:, a * 512 + (tt % 4) * 128: a * 512 + (tt % 4) * 128 + 128],
                               eyef[:]),
                           waits=([sd, ss] if a == 0 else []),
                           incs=([(st, 1)] if a == 7 else []))
                ph.add("scalar", lambda e: e.copy(ytl[:], ptrE[:]), waits=[st],
                       incs=[(ss, 1)])
                ph.add("vector",
                       lambda e, bfi=bfi: e.tensor_add(fyw[:], vtl[bfi][:], ytl[:]),
                       waits=[sd, ss, ob], incs=[(sv, 1)])
                emit_stats(ph, fyw, sqscr, 0, 4, cv[:, 0:1])
                ph.add("vector",
                       lambda e: e.tensor_scalar(fyw[:], fyw[:], stat[:, 0:1],
                                                 stat[:, 4:5], ALU.add, ALU.mult),
                       waits=[(vt, vt.n), (sce, sce.n)], incs=[(sv, 1)])
                ph.add("vector",
                       lambda e, bfi=bfi: e.tensor_add(otl[:], x1t[bfi][:], fyw[:]),
                       incs=[(sv, 1)])
                ph.add("sync",
                       lambda e, row=row: e.dma_start(out=out_d[row:row + 128, :],
                                                      in_=otl[:]),
                       waits=[sv], incs=[(ob, 16)])
            for (o, t) in dbg_copies:
                ph.add("sync", lambda e, o=o, t=t: e.dma_start(out=o, in_=t),
                       waits=bw(), incs=[(ob, 16)])
            ph.add("sync", lambda e: e.nop(), waits=[(ob, ob.n)])
            ph.emit(nc)

    return nc


_NC_CACHE = {}


def _get_nc(debug=False):
    key = bool(debug)
    if key not in _NC_CACHE:
        _NC_CACHE[key] = build_program(debug=key)
    return _NC_CACHE[key]


def _stage(inputs):
    f32 = np.float32
    x = np.asarray(inputs["L0_attn_x"], f32)
    nw = np.asarray(inputs["L0_attn_norm_w"], f32)
    Wx = np.asarray(inputs["L0_attn_Wx"], f32)
    bx = np.asarray(inputs["L0_attn_bx"], f32)
    Wz = np.asarray(inputs["L0_attn_Wz"], f32)
    bz = np.asarray(inputs["L0_attn_bz"], f32)
    convw = np.asarray(inputs["L0_attn_conv_w"], f32).reshape(DI, 4)
    convb = np.asarray(inputs["L0_attn_conv_b"], f32)
    Wxp = np.asarray(inputs["L0_attn_Wxp"], f32)
    Wdt = np.asarray(inputs["L0_attn_Wdt"], f32)
    bdt = np.asarray(inputs["L0_attn_bdt"], f32)
    Alog = np.asarray(inputs["L0_attn_A_log"], f32)
    Dp = np.asarray(inputs["L0_attn_D"], f32)
    Wout = np.asarray(inputs["L0_attn_Wout"], f32)
    bout = np.asarray(inputs["L0_attn_bout"], f32)
    Wg = np.asarray(inputs["L0_ffn_Wg"], f32)
    gamma = np.asarray(inputs["L0_ffn_gamma"], f32)
    beta = np.asarray(inputs["L0_ffn_beta"], f32)
    W1 = np.ascontiguousarray(np.asarray(inputs["L0_ffn_W1"], f32))
    b1 = np.asarray(inputs["L0_ffn_b1"], f32)
    W2 = np.ascontiguousarray(np.asarray(inputs["L0_ffn_W2"], f32))
    b2 = np.ascontiguousarray(np.asarray(inputs["L0_ffn_b2"], f32))

    eye = np.eye(128, dtype=f32)
    constv = np.ascontiguousarray(
        np.tile(np.array([EPS_A, EPS_F, 1.0, -1.0, -0.5, 1.0 / D, 0.0, 0.0],
                         dtype=f32), (128, 1)))
    wg_s = np.ascontiguousarray(Wg.reshape(8, 128, E).transpose(1, 0, 2).reshape(128, 8 * E))
    gamma_s = np.ascontiguousarray(gamma.reshape(E, 8, 128).transpose(2, 0, 1).reshape(128, E * 8))
    beta_s = np.ascontiguousarray(beta.reshape(E, 8, 128).transpose(2, 0, 1).reshape(128, E * 8))
    b1_s = np.ascontiguousarray(b1.reshape(E, 32, 128).transpose(2, 0, 1).reshape(128, E * 32))

    def chparam(vec, p):
        hh = vec[p * CH:(p + 1) * CH]
        return np.ascontiguousarray(hh.reshape(NJ, 128).T)

    per_par = {}
    for p in range(2):
        chs = slice(p * CH, (p + 1) * CH)
        per_par[p] = {
            "wxt_w": np.ascontiguousarray(Wx[chs, :].T),
            "wzt_w": np.ascontiguousarray(Wz[chs, :].T),
            "bx": chparam(bx, p),
            "bz": chparam(bz, p),
            "convw": np.ascontiguousarray(
                convw[chs].reshape(NJ, 128, 4).transpose(1, 0, 2).reshape(128, NJ * 4)),
            "convb": chparam(convb, p),
            "wxpt_w": np.ascontiguousarray(Wxp[:, chs].T),
            "wdtt_w": np.ascontiguousarray(Wdt[chs, :].T),
            "bdt": chparam(bdt, p),
            "alog": np.ascontiguousarray(
                Alog[chs].reshape(NJ, 128, N16).transpose(1, 0, 2).reshape(128, NJ * N16)),
            "dp": chparam(Dp, p),
            "woutt_w": np.ascontiguousarray(Wout[:, chs].T),
        }

    in_maps = []
    for c in range(8):
        b, p = c // 2, c % 2
        im = {
            "x_full": np.ascontiguousarray(x[b]),
            "x_half": np.ascontiguousarray(x[b, p * TH:(p + 1) * TH]),
            "eye_f": eye,
            "constv": constv,
            "norm_w": nw,
            "bout": bout,
            "wg": wg_s,
            "gamma": gamma_s,
            "beta": beta_s,
            "w1": W1,
            "b1": b1_s,
            "w2": W2,
            "b2": b2,
        }
        im.update(per_par[p])
        in_maps.append(im)
    return in_maps


def kernel(**inputs):
    nc = _get_nc(debug=False)
    in_maps = _stage(inputs)
    res = run_bass_kernel_spmd(nc, in_maps, core_ids=list(range(8)))
    out = np.empty((Bq, L, D), np.float32)
    for c in range(8):
        b, p = c // 2, c % 2
        out[b, p * TH:(p + 1) * TH] = res.results[c]["out"]
    return out


# version 16
# speedup vs baseline: 1.1347x; 1.0447x over previous
import numpy as np
from contextlib import ExitStack

import concourse.bass_utils as _bu

if not getattr(_bu, "_ldw_opt_patched", False):
    _orig_run_command = _bu.run_command

    def _run_command_ldw(argv, **kwargs):
        argv = ["--enable-ldw-opt=true" if a == "--enable-ldw-opt=false" else a
                for a in argv]
        return _orig_run_command(argv, **kwargs)

    _bu.run_command = _run_command_ldw
    _bu._ldw_opt_patched = True

import concourse.bass as bass
import concourse.mybir as mybir
from concourse.bass_utils import run_bass_kernel_spmd

F32 = mybir.dt.float32
BF16 = mybir.dt.bfloat16
AF = mybir.ActivationFunctionType
ALU = mybir.AluOpType
AX = mybir.AxisListType

Bq, L, D = 4, 8192, 1024
DI = 2048
CH = DI // 2
NJ = CH // 128
N16 = 16
R = 64
RR = 96
TH = L // 2
E, F = 8, 4096
EPS_A = 1e-5
EPS_F = 1e-6

CA = 2048
NCA = L // CA
QB = 512
NCB = L // QB
TS = 256
NTS = TH // TS
NFB = 4

PAIRS = [[0, 1], [2, 3], [4, 5], [6, 7]]
ENGINES = ("sync", "gpsimd", "vector", "scalar", "tensor")


class Sem:
    def __init__(self, nc, es, name):
        self.h = es.enter_context(nc.semaphore(name))
        self.n = 0


class Phase:
    def __init__(self):
        self.ops = {e: [] for e in ENGINES}

    def add(self, eng, fn, waits=(), incs=()):
        w = []
        for x in waits:
            if isinstance(x, tuple):
                w.append((x[0].h, x[1]))
            else:
                w.append((x.h, x.n))
        ii = []
        assert len(list(incs)) <= 1, "one sync update per instruction"
        for (s, k) in incs:
            if k is None:
                s.n += 1
                ii.append((s.h, None))
            else:
                s.n += k
                ii.append((s.h, k))
        self.ops[eng].append((fn, w, ii))

    def emit(self, nc):
        with nc.Block() as blk:
            for eng in ENGINES:
                ops = self.ops[eng]
                if not ops:
                    continue

                def mk(ops):
                    def body(e):
                        for (fn, w, ii) in ops:
                            for (h, c) in w:
                                if c > 0:
                                    e.wait_ge(h, c)
                            instr = fn(e)
                            for (h, k) in ii:
                                if k is None:
                                    instr.then_inc(h)
                                else:
                                    instr.then_inc(h, k)
                    return body

                getattr(blk, eng)(mk(ops))


def build_program(debug=False):
    nc = bass.Bass("TRN2", target_bir_lowering=False)

    def din(name, shape, dt=F32):
        return nc.dram_tensor(name, shape, dt, kind="ExternalInput").ap()

    x_full = din("x_full", [L, D])
    x_half = din("x_half", [TH, D])
    eye_f = din("eye_f", [128, 128])
    constv_d = din("constv", [128, 8])
    norm_w = din("norm_w", [D])
    wxt_d = din("wxt_w", [D, CH])
    wzt_d = din("wzt_w", [D, CH])
    bx_d = din("bx", [128, NJ])
    bz_d = din("bz", [128, NJ])
    convw_d = din("convw", [128, NJ * 4])
    convb_d = din("convb", [128, NJ])
    wxpt_d = din("wxpt_w", [CH, RR])
    wdtt_d = din("wdtt_w", [R, CH])
    bdt_d = din("bdt", [128, NJ])
    alog_d = din("alog", [128, NJ * N16])
    dp_d = din("dp", [128, NJ])
    woutt_d = din("woutt_w", [CH, D])
    bout_d = din("bout", [D])
    wg_d = din("wg", [128, 8 * E])
    gamma_d = din("gamma", [128, E * 8])
    beta_d = din("beta", [128, E * 8])
    w1_d = din("w1", [E, D, F])
    b1_d = din("b1", [128, E * (F // 128)])
    w2_d = din("w2", [E, F, D])
    b2_d = din("b2", [E, D])

    out_d = nc.dram_tensor("out", [TH, D], F32, kind="ExternalOutput").ap()

    dbg_copies = []

    def scr(name, shape, dt=F32, dbg=False):
        t = nc.dram_tensor(name, shape, dt, kind="Internal").ap()
        if debug and dbg:
            o = nc.dram_tensor("dbg_" + name, shape, dt, kind="ExternalOutput").ap()
            dbg_copies.append((o, t))
        return t

    u_dram = scr("u_dram", [L, D], dbg=True)
    xb2_dram = scr("xb2_dram", [CH, L], BF16, dbg=True)
    zs_dram = scr("zs_dram", [CH, L], BF16)
    xdbl_in = scr("xdbl_in", [RR, L])
    xdbl_out = scr("xdbl_out", [RR, L], dbg=True)
    rs_in = scr("rs_in", [L, D])
    rs_out = scr("rs_out", [TH, D], dbg=True)
    x1_dram = scr("x1_dram", [TH, D], dbg=True)
    v_dram = scr("v_dram", [TH, D])
    ymoe_dram = scr("ymoe_dram", [D, TH], dbg=True)
    xhatT_dram = scr("xhatT_dram", [128, 8 * TH], BF16)
    nwrep_dram = scr("nwrep_dram", [128, D])
    boutrep_dram = scr("boutrep_dram", [128, D])
    w1b_dram = scr("w1b_dram", [E, D, F], BF16)
    w2b_dram = scr("w2b_dram", [E, F, D], BF16)

    es = ExitStack()
    with es:
        sd = Sem(nc, es, "sd")
        gd = Sem(nc, es, "gd")
        sv = Sem(nc, es, "sv")
        ss = Sem(nc, es, "ss")
        st = Sem(nc, es, "st")
        c1 = Sem(nc, es, "c1")
        c2 = Sem(nc, es, "c2")
        ob = Sem(nc, es, "ob")
        vt = Sem(nc, es, "vt")
        gc = Sem(nc, es, "gc")
        s_pc = Sem(nc, es, "s_pc")
        g_pc = Sem(nc, es, "g_pc")
        s_usp = Sem(nc, es, "s_usp")
        s_zsp = Sem(nc, es, "s_zsp")
        s_xsp = Sem(nc, es, "s_xsp")
        s_xt = Sem(nc, es, "s_xt")
        g_xb = Sem(nc, es, "g_xb")
        s_bl = Sem(nc, es, "s_bl")
        s_nt = Sem(nc, es, "s_nt")
        s_cw = Sem(nc, es, "s_cw")
        s_yio = Sem(nc, es, "s_yio")
        w_w1 = Sem(nc, es, "w_w1")
        w_w2 = Sem(nc, es, "w_w2")
        s_ue = Sem(nc, es, "s_ue")
        s_xh = Sem(nc, es, "s_xh")
        sce = Sem(nc, es, "sce")

        ALLS = (sd, gd, sv, ss, st)

        def bw():
            return [sd, gd, sv, ss, st, s_pc, g_pc, s_usp, s_zsp, s_xsp, s_xt,
                    g_xb, s_bl, s_nt, s_cw, s_yio, w_w1, w_w2, s_ue, s_xh]

        def sb(stk, name, shape, dt=F32):
            return stk.enter_context(nc.sbuf_tensor(name, shape, dt))

        def ps(stk, name, shape, dt=F32):
            return stk.enter_context(nc.psum_tensor(name, shape, dt))

        eyeb = sb(es, "eyeb", [128, 128], BF16)
        eyef = sb(es, "eyef", [128, 128], F32)
        onesb = sb(es, "onesb", [33, 128], BF16)
        onesf = sb(es, "onesf", [1, 128], F32)
        bx_sb = sb(es, "bx_sb", [128, NJ], F32)
        bz_sb = sb(es, "bz_sb", [128, NJ], F32)
        convb_sb = sb(es, "convb_sb", [128, NJ], F32)
        bdt_sb = sb(es, "bdt_sb", [128, NJ], F32)
        dp_sb = sb(es, "dp_sb", [128, NJ], F32)
        convw_sb = sb(es, "convw_sb", [128, NJ * 4], F32)
        a_sb = sb(es, "a_sb", [128, NJ * N16], F32)
        hstate = sb(es, "hstate", [128, NJ * N16], F32)
        cv = sb(es, "cv", [128, 8], F32)
        stat = sb(es, "stat", [128, 16], F32)

        def emit_stats(ph, srcbuf, sq, mcol, vcol, eps_ap, rms=False):
            if not rms:
                ph.add("vector",
                       lambda e: e.reduce_sum(stat[:, mcol:mcol + 1], srcbuf[:],
                                              axis=AX.X, negate=True),
                       incs=[(vt, 1)])
            ph.add("scalar",
                   lambda e: e.activation(sq[:], srcbuf[:], AF.Square,
                                          accum_out=stat[:, vcol:vcol + 1]),
                   waits=[(vt, vt.n)] if not rms else [sv], incs=[(sce, 1)])
            if rms:
                ph.add("scalar",
                       lambda e: e.activation(stat[:, vcol:vcol + 1],
                                              stat[:, vcol:vcol + 1], AF.Ln,
                                              scale=cv[:, 5:6], bias=eps_ap),
                       waits=[(sce, sce.n)], incs=[(sce, 1)])
                ph.add("scalar",
                       lambda e: e.activation(stat[:, vcol:vcol + 1],
                                              stat[:, vcol:vcol + 1], AF.Exp,
                                              scale=cv[:, 4:5]),
                       waits=[(sce, sce.n)], incs=[(sce, 1)])
                return
            ph.add("vector",
                   lambda e: e.tensor_scalar_mul(stat[:, mcol:mcol + 1],
                                                 stat[:, mcol:mcol + 1], 1.0 / D),
                   waits=[(vt, vt.n)], incs=[(vt, 1)])
            ph.add("vector",
                   lambda e: e.tensor_scalar_mul(stat[:, 14:15],
                                                 stat[:, vcol:vcol + 1], 1.0 / D),
                   waits=[(sce, sce.n)], incs=[(vt, 1)])
            ph.add("vector",
                   lambda e: e.tensor_mul(stat[:, 15:16], stat[:, mcol:mcol + 1],
                                          stat[:, mcol:mcol + 1]),
                   waits=[(vt, vt.n)], incs=[(vt, 1)])
            ph.add("vector",
                   lambda e: e.tensor_sub(stat[:, vcol:vcol + 1], stat[:, 14:15],
                                          stat[:, 15:16]),
                   waits=[(vt, vt.n)], incs=[(vt, 1)])
            ph.add("scalar",
                   lambda e: e.activation(stat[:, vcol:vcol + 1],
                                          stat[:, vcol:vcol + 1], AF.Ln, bias=eps_ap),
                   waits=[(vt, vt.n)], incs=[(sce, 1)])
            ph.add("scalar",
                   lambda e: e.activation(stat[:, vcol:vcol + 1],
                                          stat[:, vcol:vcol + 1], AF.Exp,
                                          scale=cv[:, 4:5]),
                   waits=[(sce, sce.n)], incs=[(sce, 1)])

        es0 = ExitStack()
        nwrow = sb(es0, "nwrow", [1, D], F32)
        boutrow = sb(es0, "boutrow", [1, D], F32)
        pbc = ps(es0, "pbc", [128, 1024], F32)
        ph = Phase()

        def ld(dst, src):
            ph.add("sync", lambda e, d=dst, s=src: e.dma_start(out=d, in_=s),
                   incs=[(sd, 16)])

        ld(eyef[:], eye_f)
        ld(cv[:], constv_d)
        ld(nwrow[:], norm_w.rearrange("(a d) -> a d", a=1))
        ld(boutrow[:], bout_d.rearrange("(a d) -> a d", a=1))
        nw_rep = sb(es0, "nw_rep", [128, D], F32)
        bout_rep_s = sb(es0, "bout_rep_s", [128, D], F32)
        ld(bx_sb[:], bx_d)
        ld(bz_sb[:], bz_d)
        ld(convb_sb[:], convb_d)
        ld(bdt_sb[:], bdt_d)
        ld(dp_sb[:], dp_d)
        ld(convw_sb[:], convw_d)
        ld(a_sb[:], alog_d)
        ph.add("gpsimd", lambda e: e.dma_start(out=eyeb[:], in_=eye_f),
               incs=[(gd, 16)])
        ph.add("gpsimd", lambda e: e.memset(onesb[:], 1.0), incs=[(gd, 1)])
        ph.add("gpsimd", lambda e: e.memset(onesf[:], 1.0), incs=[(gd, 1)])
        ph.add("scalar", lambda e: e.activation(a_sb[:], a_sb[:], AF.Exp),
               waits=[sd], incs=[(ss, 1)])
        ph.add("vector", lambda e: e.tensor_scalar_mul(a_sb[:], a_sb[:], -1.0),
               waits=[ss], incs=[(sv, 1)])
        ph.add("vector", lambda e: e.memset(hstate[:], 0.0), incs=[(sv, 1)])
        for h in range(2):
            ph.add("tensor",
                   lambda e, h=h: e.matmul(pbc[:, h * 512:(h + 1) * 512], onesf[:],
                                           nwrow[0:1, h * 512:(h + 1) * 512],
                                           start=True, stop=True),
                   waits=([sd, gd] if h == 0 else []), incs=[(st, 1)])
        ph.add("scalar", lambda e: e.copy(nw_rep[:], pbc[:]), waits=[st],
               incs=[(ss, 1)])
        for h in range(2):
            ph.add("tensor",
                   lambda e, h=h: e.matmul(pbc[:, h * 512:(h + 1) * 512], onesf[:],
                                           boutrow[0:1, h * 512:(h + 1) * 512],
                                           start=True, stop=True),
                   waits=([ss] if h == 0 else []), incs=[(st, 1)])
        ph.add("scalar", lambda e: e.copy(bout_rep_s[:], pbc[:]), waits=[st],
               incs=[(ss, 1)])
        ph.add("sync", lambda e: e.dma_start(out=nwrep_dram, in_=nw_rep[:]),
               waits=[ss], incs=[(sd, 16)])
        ph.add("sync", lambda e: e.dma_start(out=boutrep_dram, in_=bout_rep_s[:]),
               waits=[ss], incs=[(sd, 16)])
        ph.emit(nc)
        es0.close()

        esA = ExitStack()
        with esA:
            wxt = sb(esA, "wxt", [128, 8 * CH], BF16)
            wzt = sb(esA, "wzt", [128, 8 * CH], BF16)
            wxpt = sb(esA, "wxpt", [128, NJ * RR], BF16)
            xnT = sb(esA, "xnT", [128, 8 * CA], BF16)
            xb2k = sb(esA, "xb2k", [128, NJ * CA], BF16)
            zstile = sb(esA, "zstile", [128, CA], BF16)
            xbw = sb(esA, "xbw", [128, CA + 3], F32)
            xcw = sb(esA, "xcw", [128, CA], F32)
            xbtail = sb(esA, "xbtail", [128, NJ * 3], F32)
            xdblp = sb(esA, "xdblp", [96, L], F32)
            xtile = [sb(esA, f"xtile{i}", [128, D], F32) for i in range(2)]
            sqscr = sb(esA, "sqscrA", [128, D], F32)
            nw_rep = sb(esA, "nw_repA", [128, D], F32)
            utile = sb(esA, "utile", [128, D], F32)
            xntile = sb(esA, "xntile", [128, D], BF16)
            ptr = ps(esA, "ptr", [128, 1024], BF16)
            pxb2 = [ps(esA, f"pxb{i}", [128, 512], F32) for i in range(2)]
            pz2 = [ps(esA, f"pz{i}", [128, 512], F32) for i in range(2)]
            pxd = ps(esA, "pxd", [96, 512], F32)

            wcast = [sb(esA, f"wcast{i}", [128, 4096], BF16) for i in range(2)]
            gd_pc = {}
            sd_pc = {}
            pc_steps = []
            for e_i in range(E):
                for s8 in range(8):
                    pc_steps.append(("w1", e_i, s8 * 128))
                for s8 in range(8):
                    pc_steps.append(("w2", e_i, s8 * 512))

            ph = Phase()

            def emit_precast(step_i):
                tag, e_i, r0 = pc_steps[step_i]
                wc = wcast[step_i % 2]
                if tag == "w1":
                    src_ap = w1_d[e_i, r0:r0 + 128, :]
                    dst_ap = w1b_dram[e_i, r0:r0 + 128, :]
                    wview_i = wc[:]
                    wview_o = wc[:]
                else:
                    src_ap = w2_d[e_i, r0:r0 + 512, :].rearrange(
                        "(a p) q -> p a q", p=128)
                    dst_ap = w2b_dram[e_i, r0:r0 + 512, :].rearrange(
                        "(a p) q -> p a q", p=128)
                    wview_i = wc[:].rearrange("p (a q) -> p a q", a=4)
                    wview_o = wc[:].rearrange("p (a q) -> p a q", a=4)
                ph.add("gpsimd",
                       lambda e, s=src_ap, w=wview_i: e.dma_start(out=w, in_=s),
                       waits=[(s_pc, sd_pc.get(step_i - 2, 0))], incs=[(g_pc, 16)])
                gd_pc[step_i] = g_pc.n
                ph.add("sync",
                       lambda e, d=dst_ap, w=wview_o: e.dma_start(out=d, in_=w),
                       waits=[(g_pc, gd_pc[step_i])], incs=[(s_pc, 16)])
                sd_pc[step_i] = s_pc.n

            ph.add("sync", lambda e: e.dma_start(out=nw_rep[:], in_=nwrep_dram),
                   waits=[sd], incs=[(sd, 16)])
            for a in range(8):
                ph.add("gpsimd",
                       lambda e, a=a: e.dma_start(
                           out=wxt[:, a * CH:(a + 1) * CH],
                           in_=wxt_d[a * 128:(a + 1) * 128, :]),
                       incs=[(gd, 16)])
                ph.add("gpsimd",
                       lambda e, a=a: e.dma_start(
                           out=wzt[:, a * CH:(a + 1) * CH],
                           in_=wzt_d[a * 128:(a + 1) * 128, :]),
                       incs=[(gd, 16)])
            for a in range(NJ):
                ph.add("gpsimd",
                       lambda e, a=a: e.dma_start(
                           out=wxpt[:, a * RR:(a + 1) * RR],
                           in_=wxpt_d[a * 128:(a + 1) * 128, :]),
                       incs=[(gd, 16)])
            gd_wA = gd.n

            usp = {}
            zsp = {}
            xsp = {}
            for ch in range(NCA):
                t0 = ch * CA
                for _pc in range(32):
                    si = ch * 32 + _pc
                    if si < len(pc_steps):
                        emit_precast(si)
                for tt in range(16):
                    xt = xtile[tt % 2]
                    row = t0 + tt * 128
                    ph.add("sync",
                           lambda e, xt=xt, row=row: e.dma_start(
                               out=xt[:], in_=x_full[row:row + 128, :]),
                           waits=[sv, ss], incs=[(s_xt, 16)])
                    sd_xt = s_xt.n
                    ph.add("vector", lambda e: e.engine_nop(), waits=[(s_xt, sd_xt)])
                    emit_stats(ph, xt, sqscr, 0, 4, cv[:, 0:1])
                    ph.add("vector",
                           lambda e, xt=xt: e.tensor_scalar(utile[:], xt[:], stat[:, 0:1],
                                                            stat[:, 4:5], ALU.add, ALU.mult),
                           waits=[(s_usp, usp.get(0, 0)), (vt, vt.n), (sce, sce.n)],
                           incs=[(sv, 1)])
                    ph.add("sync",
                           lambda e, row=row: e.dma_start(out=u_dram[row:row + 128, :],
                                                          in_=utile[:]),
                           waits=[sv], incs=[(s_usp, 16)])
                    usp[0] = s_usp.n
                    emit_stats(ph, utile, sqscr, 0, 6, cv[:, 0:1], rms=True)
                    ph.add("vector",
                           lambda e: e.scalar_tensor_tensor(xntile[:], utile[:], stat[:, 6:7],
                                                            nw_rep[:], ALU.mult, ALU.mult),
                           waits=[st, (sce, sce.n)], incs=[(sv, 1)])
                    for a in range(8):
                        ph.add("tensor",
                               lambda e, a=a: e.transpose(ptr[:, a * 128:(a + 1) * 128],
                                                          xntile[:, a * 128:(a + 1) * 128],
                                                          eyeb[:]),
                               waits=([sv, ss] if a == 0 else []),
                               incs=([(st, 1)] if a == 7 else []))
                    ph.add("scalar",
                           lambda e, tt=tt: e.copy(
                               xnT[:].rearrange("p (a t) -> p a t", a=8)[:, :, tt * 128:(tt + 1) * 128],
                               ptr[:].rearrange("p (a t) -> p a t", a=8)),
                           waits=[st], incs=[(ss, 1)])

                sv_xnt = sv.n
                ss_xnt = ss.n
                xbev = {0: ss_xnt, 1: ss_xnt}
                zev = {0: ss_xnt, 1: ss_xnt}
                tail_vt = {}
                for j in range(NJ):
                    if ch == 0:
                        ph.add("vector", lambda e: e.memset(xbw[:, 0:3], 0.0),
                               waits=[ss, st], incs=[(vt, 1)])
                        tail_vt[j] = vt.n
                    else:
                        ph.add("vector",
                               lambda e, j=j: e.tensor_copy(xbw[:, 0:3],
                                                            xbtail[:, j * 3:(j + 1) * 3]),
                               waits=[ss, st], incs=[(vt, 1)])
                    tail_vt[j] = vt.n
                    for ts4 in range(4):
                        par = ts4 % 2
                        for a in range(8):
                            ph.add("tensor",
                                   lambda e, a=a, j=j, ts4=ts4, par=par: e.matmul(
                                       pxb2[par][:],
                                       wxt[:, a * CH + j * 128: a * CH + (j + 1) * 128],
                                       xnT[:, a * CA + ts4 * 512: a * CA + (ts4 + 1) * 512],
                                       start=(a == 0), stop=(a == 7)),
                                   waits=([(ss, xbev.get(par, 0)), (gd, gd_wA),
                                           (sv, sv_xnt)] if a == 0 else []),
                                   incs=([(st, 1)] if a == 7 else []))
                        xbmm = st.n
                        for a in range(8):
                            ph.add("tensor",
                                   lambda e, a=a, j=j, ts4=ts4, par=par: e.matmul(
                                       pz2[par][:],
                                       wzt[:, a * CH + j * 128: a * CH + (j + 1) * 128],
                                       xnT[:, a * CA + ts4 * 512: a * CA + (ts4 + 1) * 512],
                                       start=(a == 0), stop=(a == 7)),
                                   waits=([(ss, zev.get(par, 0))] if a == 0 else []),
                                   incs=([(st, 1)] if a == 7 else []))
                        zmm = st.n
                        ph.add("scalar",
                               lambda e, j=j, ts4=ts4, par=par: e.activation(
                                   xbw[:, 3 + ts4 * 512: 3 + (ts4 + 1) * 512],
                                   pxb2[par][:],
                                   AF.Identity, bias=bx_sb[:, j:j + 1]),
                               waits=[(st, xbmm), (vt, tail_vt.get(j, 0))],
                               incs=[(ss, 1)])
                        xbev[par] = ss.n
                        ph.add("scalar",
                               lambda e, ts4=ts4, j=j, par=par: e.activation(
                                   zstile[:, ts4 * 512:(ts4 + 1) * 512], pz2[par][:],
                                   AF.Silu,
                                   bias=bz_sb[:, j:j + 1]),
                               waits=[(st, zmm), (s_zsp, zsp.get(0, 0))],
                               incs=[(ss, 1)])
                        zev[par] = ss.n
                    ph.add("sync",
                           lambda e, j=j, t0=t0: e.dma_start(
                               out=zs_dram[j * 128:(j + 1) * 128, t0:t0 + CA],
                               in_=zstile[:]),
                           waits=[ss], incs=[(s_zsp, 16)])
                    zsp[0] = s_zsp.n
                    ph.add("vector",
                           lambda e, j=j: e.tensor_copy(xbtail[:, j * 3:(j + 1) * 3],
                                                        xbw[:, CA:CA + 3]),
                           waits=[ss], incs=[(sv, 1)])
                    ph.add("vector",
                           lambda e, j=j: e.tensor_scalar_mul(
                               xcw[:], xbw[:, 0:CA],
                               convw_sb[:, j * 4:j * 4 + 1]),
                           waits=[(vt, vt.n)], incs=[(sv, 1)])
                    for k in range(1, 4):
                        ph.add("vector",
                               lambda e, j=j, k=k: e.scalar_tensor_tensor(
                                   xcw[:], xbw[:, k:k + CA],
                                   convw_sb[:, j * 4 + k:j * 4 + k + 1],
                                   xcw[:], ALU.mult, ALU.add),
                               incs=([(sv, 1)] if k == 3 else []))
                    ph.add("scalar",
                           lambda e, j=j: e.activation(
                               xb2k[:, j * CA:(j + 1) * CA], xcw[:], AF.Silu,
                               bias=convb_sb[:, j:j + 1]),
                           waits=[sv, st, (s_xsp, xsp.get(j, 0))], incs=[(ss, 1)])
                    ph.add("sync",
                           lambda e, j=j, t0=t0: e.dma_start(
                               out=xb2_dram[j * 128:(j + 1) * 128, t0:t0 + CA],
                               in_=xb2k[:, j * CA:(j + 1) * CA]),
                           waits=[ss], incs=[(s_xsp, 16)])
                    xsp[j] = s_xsp.n

                for ts4 in range(4):
                    for j in range(NJ):
                        ph.add("tensor",
                               lambda e, j=j, ts4=ts4: e.matmul(
                                   pxd[:],
                                   wxpt[:, j * RR:(j + 1) * RR],
                                   xb2k[:, j * CA + ts4 * 512: j * CA + (ts4 + 1) * 512],
                                   start=(j == 0), stop=(j == NJ - 1)),
                               waits=([ss, sv] if j == 0 else []),
                               incs=([(st, 1)] if j == NJ - 1 else []))
                    ph.add("scalar",
                           lambda e, ts4=ts4, t0=t0: e.copy(
                               xdblp[0:96, t0 + ts4 * 512: t0 + (ts4 + 1) * 512], pxd[:]),
                           waits=[st], incs=[(ss, 1)])

            ph.add("sync", lambda e: e.dma_start(out=xdbl_in, in_=xdblp[0:96, :]),
                   waits=[ss], incs=[(sd, 16)])
            ph.emit(nc)

        ph = Phase()
        ph.add("gpsimd",
               lambda e: e.collective_compute(
                   "AllReduce", ALU.add, replica_groups=PAIRS,
                   ins=[xdbl_in.opt()], outs=[xdbl_out.opt()]),
               waits=bw(), incs=[(c1, None)])
        ph.emit(nc)

        esB = ExitStack()
        with esB:
            wdtt = sb(esB, "wdtt", [64, CH], BF16)
            woutt = sb(esB, "woutt", [128, NJ * D], BF16)
            dlowb = sb(esB, "dlowb", [64, QB], BF16)
            bc1 = sb(esB, "bc1", [33, 16 * QB], BF16)
            dt_sb = sb(esB, "dt_sb", [128, NJ * QB], F32)
            u_sb = sb(esB, "u_sb", [128, NJ * QB], F32)
            xb2c = sb(esB, "xb2c", [128, NJ * QB], BF16)
            zsc = sb(esB, "zsc", [128, NJ * QB], BF16)
            dA = sb(esB, "dA", [128, NJ * QB], F32)
            dA_b = sb(esB, "dA_b", [128, NJ * QB], F32)
            dBx = sb(esB, "dBx", [128, NJ * QB], F32)
            dBx_b = sb(esB, "dBx_b", [128, NJ * QB], F32)
            bbsb = sb(esB, "bbsb", [128, 2 * QB], F32)
            h_n = sb(esB, "h_n", [128, NJ * QB], F32)
            y_sb = sb(esB, "y_sb", [128, NJ * QB], F32)
            yz_bf = sb(esB, "yz_bf", [128, NJ * QB], BF16)
            uev0 = sb(esB, "uev0", [128, D], F32)
            uev = [uev0, uev0]
            outp = [sb(esB, f"outp{i}", [128, D], F32) for i in range(2)]
            tabs = h_n[:, 0:QB]
            tln = h_n[:, QB:2 * QB]
            pdt = ps(esB, "pdt", [128, 512], F32)
            pb_b = ps(esB, "pb_b", [128, 512], F32)
            pb_c = ps(esB, "pb_c", [128, 512], F32)
            pb_b2 = ps(esB, "pb_b2", [128, 512], F32)
            pb_c2 = ps(esB, "pb_c2", [128, 512], F32)
            po = ps(esB, "po", [128, 1024], F32)

            ph = Phase()
            for a in range(NJ):
                ph.add("gpsimd",
                       lambda e, a=a: e.dma_start(
                           out=woutt[:, a * D:(a + 1) * D],
                           in_=woutt_d[a * 128:(a + 1) * 128, :]),
                       incs=[(gd, 16)])
            ph.add("gpsimd", lambda e: e.dma_start(out=wdtt[:], in_=wdtt_d),
                   incs=[(gd, 16)])

            dA2 = [dA, dA_b]
            pbb = [pb_b, pb_b2]
            pbcc = [pb_c, pb_c2]
            yadd_done = {-2: 0, -1: 0}
            bcast_ok = {-2: 0, -1: 0}
            for cb in range(NCB):
                t0 = cb * QB
                ph.add("gpsimd",
                       lambda e, t0=t0: e.dma_start(out=dlowb[:],
                                                    in_=xdbl_out[0:64, t0:t0 + QB]),
                       waits=[(c1, 1), st, ss], incs=[(g_xb, 16)])
                ph.add("gpsimd",
                       lambda e, t0=t0: e.dma_start(
                           out=bc1[0:1, :].rearrange("q (n t) -> q n t", n=16),
                           in_=xdbl_out[64:80, t0:t0 + QB]),
                       waits=[(c1, 1), st], incs=[(g_xb, 16)])
                ph.add("gpsimd",
                       lambda e, t0=t0: e.dma_start(
                           out=bc1[32:33, :].rearrange("q (n t) -> q n t", n=16),
                           in_=xdbl_out[80:96, t0:t0 + QB]),
                       waits=[(c1, 1), st], incs=[(g_xb, 16)])
                for j in range(NJ):
                    ph.add("sync",
                           lambda e, t0=t0, j=j: e.dma_start(
                               out=xb2c[:, j * QB:(j + 1) * QB],
                               in_=xb2_dram[j * 128:(j + 1) * 128, t0:t0 + QB]),
                           waits=([sv, (s_xsp, s_xsp.n), (s_zsp, s_zsp.n)]
                                  if j == 0 else []),
                           incs=[(s_bl, 16)])
                    ph.add("sync",
                           lambda e, t0=t0, j=j: e.dma_start(
                               out=zsc[:, j * QB:(j + 1) * QB],
                               in_=zs_dram[j * 128:(j + 1) * 128, t0:t0 + QB]),
                           incs=[(s_bl, 16)])

                gd_cb = g_xb.n
                for j in range(NJ):
                    ph.add("tensor",
                           lambda e, j=j: e.matmul(pdt[:, 0:QB],
                                                   wdtt[:, j * 128:(j + 1) * 128],
                                                   dlowb[:], start=True, stop=True),
                           waits=([(g_xb, gd_cb), ss, sv] if j == 0 else [ss]),
                           incs=[(st, 1)])
                    ph.add("scalar",
                           lambda e, j=j: e.activation(tabs, pdt[:, 0:QB], AF.Abs,
                                                       bias=bdt_sb[:, j:j + 1]),
                           waits=[st])
                    ph.add("scalar",
                           lambda e: e.activation(tabs, tabs, AF.Exp, scale=cv[:, 3:4]))
                    ph.add("scalar",
                           lambda e: e.activation(tln, tabs, AF.Ln, bias=cv[:, 2:3]))
                    ph.add("scalar",
                           lambda e, j=j: e.activation(tabs, pdt[:, 0:QB], AF.Relu,
                                                       bias=bdt_sb[:, j:j + 1]),
                           incs=[(ss, 1)])
                    ph.add("vector",
                           lambda e, j=j: e.tensor_add(dt_sb[:, j * QB:(j + 1) * QB],
                                                       tabs, tln),
                           waits=[ss], incs=[(sv, 1)])
                    ph.add("vector",
                           lambda e, j=j: e.tensor_mul(u_sb[:, j * QB:(j + 1) * QB],
                                                       dt_sb[:, j * QB:(j + 1) * QB],
                                                       xb2c[:, j * QB:(j + 1) * QB]),
                           waits=[(s_bl, s_bl.n)], incs=[(sv, 1)])
                sv_u = sv.n
                sv_u_vt = vt.n

                dBx2 = [dBx, dBx_b]
                scan_done = {-2: sv_u_vt, -1: sv_u_vt}
                cm_done = {-2: sv_u, -1: sv_u}
                yadd_done = {-2: sv_u, -1: sv_u}
                for n in range(N16):
                    par = n % 2
                    ph.add("tensor",
                           lambda e, n=n, par=par: e.matmul(
                               pbb[par][:, 0:QB], onesb[0:1, :],
                               bc1[0:1, n * QB:(n + 1) * QB],
                               start=True, stop=True),
                           waits=[(sv, yadd_done[n - 2])], incs=[(st, 1)])
                    ph.add("tensor",
                           lambda e, n=n, par=par: e.matmul(
                               pbcc[par][:, 0:QB], onesb[32:33, :],
                               bc1[32:33, n * QB:(n + 1) * QB],
                               start=True, stop=True),
                           incs=[(st, 1)])
                    bcast_ok[n] = st.n
                    for j in range(NJ):
                        ph.add("scalar",
                               lambda e, j=j, n=n, par=par: e.activation(
                                   dA2[par][:, j * QB:(j + 1) * QB],
                                   dt_sb[:, j * QB:(j + 1) * QB], AF.Exp,
                                   scale=a_sb[:, j * N16 + n: j * N16 + n + 1]),
                               waits=([(sv, yadd_done[n - 2])] if j == 0 else []),
                               incs=([(ss, 1)] if j == NJ - 1 else []))
                    ss_dA = ss.n
                    for j in range(NJ):
                        ph.add("vector",
                               lambda e, j=j, par=par: e.tensor_mul(
                                   dBx2[par][:, j * QB:(j + 1) * QB],
                                   u_sb[:, j * QB:(j + 1) * QB],
                                   pbb[par][:, 0:QB]),
                               waits=([(st, bcast_ok[n]), (ss, ss_dA)] if j == 0 else []))
                    for j in range(NJ):
                        ph.add("vector",
                               lambda e, j=j, n=n, par=par: e.scalar_tensor_tensor(
                                   dBx2[par][:, j * QB:j * QB + 1],
                                   dA2[par][:, j * QB:j * QB + 1],
                                   hstate[:, j * N16 + n:j * N16 + n + 1],
                                   dBx2[par][:, j * QB:j * QB + 1], ALU.mult, ALU.add),
                               incs=([(vt, 1)] if j == NJ - 1 else []))
                    for j in range(NJ):
                        ph.add("vector",
                               lambda e, j=j, par=par: e.tensor_tensor_scan(
                                   h_n[:, j * QB:(j + 1) * QB],
                                   dA2[par][:, j * QB:(j + 1) * QB],
                                   dBx2[par][:, j * QB:(j + 1) * QB],
                                   0.0, ALU.mult, ALU.add),
                               waits=([(vt, vt.n)] if j == 0 else []),
                               incs=([(vt, 1)] if j == NJ - 1 else []))
                    scan_done[n] = vt.n
                    for j in range(NJ):
                        ph.add("vector",
                               lambda e, j=j, n=n: e.tensor_copy(
                                   hstate[:, j * N16 + n:j * N16 + n + 1],
                                   h_n[:, (j + 1) * QB - 1:(j + 1) * QB]),
                               waits=([(vt, vt.n)] if j == 0 else []))
                    for j in range(NJ):
                        ph.add("vector",
                               lambda e, j=j, par=par: e.tensor_mul(
                                   dA2[par][:, j * QB:(j + 1) * QB],
                                   h_n[:, j * QB:(j + 1) * QB],
                                   pbcc[par][:, 0:QB]))
                        if n == 0:
                            ph.add("vector",
                                   lambda e, j=j, par=par: e.tensor_copy(
                                       y_sb[:, j * QB:(j + 1) * QB],
                                       dA2[par][:, j * QB:(j + 1) * QB]),
                                   incs=([(sv, 1)] if j == NJ - 1 else []))
                        else:
                            ph.add("vector",
                                   lambda e, j=j, par=par: e.tensor_add(
                                       y_sb[:, j * QB:(j + 1) * QB],
                                       y_sb[:, j * QB:(j + 1) * QB],
                                       dA2[par][:, j * QB:(j + 1) * QB]),
                                   incs=([(sv, 1)] if j == NJ - 1 else []))
                    yadd_done[n] = sv.n
                    cm_done[n] = sv.n

                for j in range(NJ):
                    ph.add("vector",
                           lambda e, j=j: e.scalar_tensor_tensor(
                               y_sb[:, j * QB:(j + 1) * QB],
                               xb2c[:, j * QB:(j + 1) * QB],
                               dp_sb[:, j:j + 1],
                               y_sb[:, j * QB:(j + 1) * QB], ALU.mult, ALU.add))
                    ph.add("vector",
                           lambda e, j=j: e.tensor_mul(yz_bf[:, j * QB:(j + 1) * QB],
                                                       y_sb[:, j * QB:(j + 1) * QB],
                                                       zsc[:, j * QB:(j + 1) * QB]),
                           incs=[(sv, 1)])

                for tsub in range(QB // 128):
                    row = t0 + tsub * 128
                    bfi = tsub % 2
                    ph.add("sync",
                           lambda e, row=row, bfi=bfi: e.dma_start(
                               out=uev[bfi][:], in_=u_dram[row:row + 128, :]),
                           waits=[sv, (s_usp, s_usp.n)], incs=[(s_ue, 16)])
                    for j in range(NJ):
                        for dh in range(2):
                            ph.add("tensor",
                                   lambda e, j=j, dh=dh, tsub=tsub: e.matmul(
                                       po[:, dh * 512:(dh + 1) * 512],
                                       yz_bf[:, j * QB + tsub * 128: j * QB + (tsub + 1) * 128],
                                       woutt[:, j * D + dh * 512: j * D + (dh + 1) * 512],
                                       start=(j == 0), stop=(j == NJ - 1)),
                                   waits=([sv] if (dh == 0 and j == 0) else []),
                                   incs=([(st, 1)] if (dh == 1 and j == NJ - 1) else []))
                    ph.add("vector",
                           lambda e, bfi=bfi: e.scalar_tensor_tensor(
                               outp[bfi][:, 0:512], uev[bfi][:, 0:512], 0.5,
                               po[:, 0:512], ALU.mult, ALU.add),
                           waits=[st, (s_ue, s_ue.n)], incs=[(sv, 1)])
                    ph.add("vector",
                           lambda e, bfi=bfi: e.scalar_tensor_tensor(
                               outp[bfi][:, 512:1024], uev[bfi][:, 512:1024], 0.5,
                               po[:, 512:1024], ALU.mult, ALU.add),
                           incs=[(sv, 1)])
                    ph.add("sync",
                           lambda e, row=row, bfi=bfi: e.dma_start(
                               out=rs_in[row:row + 128, :], in_=outp[bfi][:]),
                           waits=[sv], incs=[(sd, 16)])
            ph.emit(nc)

        ph = Phase()
        ph.add("gpsimd",
               lambda e: e.collective_compute(
                   "ReduceScatter", ALU.add, replica_groups=PAIRS,
                   ins=[rs_in.opt()], outs=[rs_out.opt()]),
               waits=bw(), incs=[(c2, None)])
        ph.emit(nc)

        esFD = ExitStack()
        with esFD:
            gamma_sb = sb(esFD, "gamma_sb", [128, E * 8], F32)
            beta_sb = sb(esFD, "beta_sb", [128, E * 8], F32)
            b1_sb = sb(esFD, "b1_sb", [128, E * 32], F32)
            cwT_sb = sb(esFD, "cwT_sb", [8, TH], BF16)

            esC = ExitStack()
            with esC:
                xhatT = sb(esC, "xhatT", [128, 8 * TH], BF16)
                wg_sb = sb(esC, "wg_sb", [128, 8 * E], F32)
                b2_sb = sb(esC, "b2_sb", [8, D], BF16)
                cw_sb = sb(esC, "cw_sb", [128, (TH // 128) * E], F32)
                sqscr = sb(esC, "sqscrC", [128, D], F32)
                rst = [sb(esC, f"rst{i}", [128, D], F32) for i in range(2)]
                xht = [sb(esC, f"xht{i}", [128, D], F32) for i in range(2)]
                aow = sb(esC, "aow", [128, D], F32)
                x1w = sb(esC, "x1w", [128, D], F32)
                vw = sb(esC, "vw", [128, D], F32)
                xhbf = sb(esC, "xhbf", [128, D], BF16)
                vT = sb(esC, "vT", [128, 8 * 128], F32)
                sct = sb(esC, "sct", [128, 8], F32)
                cwt = sb(esC, "cwt", [128, 8], F32)
                seedw = sb(esC, "seedw", [128, TH], F32)
                bout_rep = sb(esC, "bout_repC", [128, D], F32)
                ptrv = ps(esC, "ptrv", [128, 1024], F32)
                ptrb = ps(esC, "ptrb", [128, 1024], BF16)
                psc = ps(esC, "psc", [128, 512], F32)
                pb2 = ps(esC, "pb2", [128, 1024], F32)

                ph = Phase()
                for dst, srcd in ((wg_sb, wg_d), (gamma_sb, gamma_d),
                                  (beta_sb, beta_d), (b1_sb, b1_d)):
                    ph.add("sync",
                           lambda e, d=dst, s=srcd: e.dma_start(out=d[:], in_=s),
                           incs=[(sd, 16)])
                ph.add("gpsimd", lambda e: e.dma_start(out=b2_sb[:], in_=b2_d),
                       incs=[(gd, 16)])
                gd_b2 = gd.n
                ph.add("sync", lambda e: e.dma_start(out=bout_rep[:], in_=boutrep_dram),
                       incs=[(sd, 16)])

                for tt in range(TH // 128):
                    row = tt * 128
                    bfi = tt % 2
                    ph.add("sync",
                           lambda e, row=row, bfi=bfi: e.dma_start(
                               out=rst[bfi][:], in_=rs_out[row:row + 128, :]),
                           waits=[(c2, 1), sv, ss], incs=[(sd, 16)])
                    ph.add("sync",
                           lambda e, row=row, bfi=bfi: e.dma_start(
                               out=xht[bfi][:], in_=x_half[row:row + 128, :]),
                           incs=[(sd, 16)])
                    ph.add("vector",
                           lambda e, bfi=bfi: e.tensor_add(aow[:], rst[bfi][:], bout_rep[:]),
                           waits=[sd, ss], incs=[(sv, 1)])
                    emit_stats(ph, aow, sqscr, 0, 1, cv[:, 0:1])
                    ph.add("vector",
                           lambda e: e.tensor_scalar(aow[:], aow[:], stat[:, 0:1],
                                                     stat[:, 1:2], ALU.add, ALU.mult),
                           waits=[(vt, vt.n), (sce, sce.n)], incs=[(sv, 1)])
                    ph.add("vector",
                           lambda e, bfi=bfi: e.tensor_add(x1w[:], xht[bfi][:], aow[:]),
                           incs=[(sv, 1)])
                    ph.add("sync",
                           lambda e, row=row: e.dma_start(out=x1_dram[row:row + 128, :],
                                                          in_=x1w[:]),
                           waits=[sv], incs=[(sd, 16)])
                    emit_stats(ph, x1w, sqscr, 2, 3, cv[:, 0:1])
                    ph.add("vector",
                           lambda e: e.tensor_scalar(vw[:], x1w[:], stat[:, 2:3],
                                                     stat[:, 3:4], ALU.add, ALU.mult),
                           waits=[ss, st, sd, (vt, vt.n), (sce, sce.n)], incs=[(sv, 1)])
                    ph.add("sync",
                           lambda e, row=row: e.dma_start(out=v_dram[row:row + 128, :],
                                                          in_=vw[:]),
                           waits=[sv], incs=[(sd, 16)])
                    for a in range(8):
                        ph.add("tensor",
                               lambda e, a=a: e.transpose(ptrv[:, a * 128:(a + 1) * 128],
                                                          vw[:, a * 128:(a + 1) * 128],
                                                          eyef[:]),
                               waits=([sv, ss] if a == 0 else []),
                               incs=([(st, 1)] if a == 7 else []))
                    ph.add("scalar", lambda e: e.copy(vT[:], ptrv[:]), waits=[st],
                           incs=[(ss, 1)])
                    for a in range(8):
                        ph.add("tensor",
                               lambda e, a=a: e.matmul(psc[:, 0:8],
                                                       vT[:, a * 128:(a + 1) * 128],
                                                       wg_sb[:, a * E:(a + 1) * E],
                                                       start=(a == 0), stop=(a == 7)),
                               waits=([ss] if a == 0 else []),
                               incs=([(st, 1)] if a == 7 else []))
                    ph.add("scalar", lambda e: e.copy(sct[:], psc[:, 0:8]), waits=[st],
                           incs=[(ss, 1)])

                    def tv(fn, extra=()):
                        ph.add("vector", fn,
                               waits=list(extra) + [(vt, vt.n)],
                               incs=[(vt, 1)])
                    tv(lambda e: e.reduce_max(stat[:, 8:9], sct[:], axis=AX.X,
                                              negate=True), extra=[ss])
                    tv(lambda e: e.tensor_single_scalar(cwt[:], sct[:], stat[:, 8:9],
                                                        ALU.add))
                    tv(lambda e: e.tensor_single_scalar(cwt[:], cwt[:], 0.0, ALU.is_ge))
                    tv(lambda e: e.scalar_tensor_tensor(cwt[:], cwt[:], -1e30, sct[:],
                                                        ALU.mult, ALU.add))
                    tv(lambda e: e.reduce_max(stat[:, 9:10], cwt[:], axis=AX.X))
                    tv(lambda e: e.tensor_scalar_mul(stat[:, 10:11], stat[:, 9:10], -1.0))
                    tv(lambda e: e.tensor_single_scalar(cwt[:], sct[:], stat[:, 10:11],
                                                        ALU.add))
                    tv(lambda e: e.tensor_single_scalar(cwt[:], cwt[:], 0.0, ALU.is_ge))
                    ph.add("scalar",
                           lambda e: e.activation(sct[:], sct[:], AF.Exp, bias=stat[:, 8:9]),
                           waits=[(vt, vt.n), (sce, sce.n)], incs=[(sce, 1)])
                    tv(lambda e: e.tensor_mul(cwt[:], cwt[:], sct[:]),
                       extra=[(sce, sce.n)])
                    tv(lambda e: e.reduce_sum(stat[:, 11:12], cwt[:], axis=AX.X))
                    tv(lambda e: e.reciprocal(stat[:, 12:13], stat[:, 11:12]))
                    tv(lambda e, tt=tt: e.tensor_scalar_mul(
                           cw_sb[:, tt * E:(tt + 1) * E], cwt[:], stat[:, 12:13]))
                    ph.add("tensor",
                           lambda e, tt=tt: e.transpose(psc[0:8, 128:256],
                                                        cw_sb[:, tt * E:(tt + 1) * E],
                                                        eyef[:]),
                           waits=[(vt, vt.n)], incs=[(st, 1)])
                    ph.add("scalar",
                           lambda e, tt=tt: e.copy(cwT_sb[0:8, tt * 128:(tt + 1) * 128],
                                                   psc[0:8, 128:256]),
                           waits=[st], incs=[(ss, 1)])
                    emit_stats(ph, vw, sqscr, 4, 5, cv[:, 1:2])
                    ph.add("vector",
                           lambda e: e.tensor_scalar(xhbf[:], vw[:], stat[:, 4:5],
                                                     stat[:, 5:6], ALU.add, ALU.mult),
                           waits=[ss, st, (vt, vt.n), (sce, sce.n)], incs=[(sv, 1)])
                    for a in range(8):
                        ph.add("tensor",
                               lambda e, a=a: e.transpose(ptrb[:, a * 128:(a + 1) * 128],
                                                          xhbf[:, a * 128:(a + 1) * 128],
                                                          eyeb[:]),
                               waits=([sv, ss] if a == 0 else []),
                               incs=([(st, 1)] if a == 7 else []))
                    ph.add("scalar",
                           lambda e, tt=tt: e.copy(
                               xhatT[:].rearrange("p (a t) -> p a t", a=8)[:, :, tt * 128:(tt + 1) * 128],
                               ptrb[:].rearrange("p (a t) -> p a t", a=8)),
                           waits=[st], incs=[(ss, 1)])
                    if tt == 15 or tt == TH // 128 - 1:
                        hoff = 0 if tt == 15 else 2048
                        for a in range(8):
                            ph.add("sync",
                                   lambda e, a=a, hoff=hoff: e.dma_start(
                                       out=xhatT_dram[:, a * TH + hoff: a * TH + hoff + 2048],
                                       in_=xhatT[:, a * TH + hoff: a * TH + hoff + 2048]),
                                   waits=([ss] if a == 0 else []), incs=[(s_xh, 16)])
                for dtile in range(8):
                    for tc in range(8):
                        half = tc % 2
                        ph.add("tensor",
                               lambda e, dtile=dtile, tc=tc, half=half: e.matmul(
                                   pb2[:, half * 512:(half + 1) * 512],
                                   b2_sb[0:8, dtile * 128:(dtile + 1) * 128],
                                   cwT_sb[0:8, tc * 512:(tc + 1) * 512],
                                   start=True, stop=True),
                               waits=[ss, (gd, gd_b2)], incs=[(st, 1)])
                        ph.add("scalar",
                               lambda e, tc=tc, half=half: e.copy(
                                   seedw[:, tc * 512:(tc + 1) * 512],
                                   pb2[:, half * 512:(half + 1) * 512]),
                               waits=[st], incs=[(ss, 1)])
                    ph.add("sync",
                           lambda e, dtile=dtile: e.dma_start(
                               out=ymoe_dram[dtile * 128:(dtile + 1) * 128, :],
                               in_=seedw[:]),
                           waits=[ss], incs=[(s_yio, 16)])
                ph.emit(nc)

            TSG = 2048
            NTSG = TH // TSG
            esD = ExitStack()
            with esD:
                w1h = sb(esD, "w1h", [128, 8 * 2048], BF16)
                w2h = sb(esD, "w2h", [128, 16 * 1024], BF16)
                ntb = sb(esD, "ntb", [128, 8 * TSG], BF16)
                hbig = sb(esD, "hbig", [128, 16 * TSG], BF16)
                cwrow = sb(esD, "cwrow", [1, TSG], BF16)
                cwrep = sb(esD, "cwrep", [128, TSG], BF16)
                ydt = [sb(esD, f"ydt{i}", [128, TSG], F32) for i in range(2)]
                ytmp1 = sb(esD, "ytmp1", [128, TSG], F32)
                pA = ps(esD, "pA", [128, 2048], F32)
                pB = ps(esD, "pB", [128, 2048], F32)

                ph = Phase()
                units = [(e_i, fh) for e_i in range(E) for fh in range(2)]

                def load_w1(e_i, fh, wts, eng="scalar"):
                    for kb in range(8):
                        ph.add(eng,
                               lambda e, e_i=e_i, fh=fh, kb=kb: e.dma_start(
                                   out=w1h[:, kb * 2048:(kb + 1) * 2048],
                                   in_=w1b_dram[e_i, kb * 128:(kb + 1) * 128,
                                                fh * 2048:(fh + 1) * 2048]),
                               waits=(wts if kb == 0 else []), incs=[(w_w1, 16)])
                    return w_w1.n

                def load_w2(e_i, fh, wts, eng="scalar"):
                    for a in range(16):
                        ph.add(eng,
                               lambda e, e_i=e_i, fh=fh, a=a: e.dma_start(
                                   out=w2h[:, a * 1024:(a + 1) * 1024],
                                   in_=w2b_dram[e_i, fh * 2048 + a * 128:
                                                fh * 2048 + (a + 1) * 128, :]),
                               waits=(wts if a == 0 else []), incs=[(w_w2, 16)])
                    return w_w2.n

                seed_yio = s_yio.n
                yout_sd = {}
                gd_w1r = {0: load_w1(*units[0], wts=[(s_pc, sd_pc[len(pc_steps) - 1])],
                                      eng="gpsimd")}
                gd_w2r = {0: load_w2(*units[0], wts=[], eng="gpsimd")}
                for ui, (e_i, fh) in enumerate(units):
                    for tsg in range(NTSG):
                        t0g = tsg * TSG
                        for a in range(8):
                            ph.add("sync",
                                   lambda e, a=a, t0g=t0g: e.dma_start(
                                       out=ntb[:, a * TSG:(a + 1) * TSG],
                                       in_=xhatT_dram[:, a * TH + t0g: a * TH + t0g + TSG]),
                                   waits=([(gc, gc.n), st,
                                           (s_xh, 128 if t0g == 0 else 256)]
                                          if a == 0 else []),
                                   incs=[(s_nt, 16)])
                        sd_nt = s_nt.n
                        for kb in range(8):
                            ph.add("gpsimd",
                                   lambda e, e_i=e_i, kb=kb: e.tensor_scalar(
                                       ntb[:, kb * TSG:(kb + 1) * TSG],
                                       ntb[:, kb * TSG:(kb + 1) * TSG],
                                       gamma_sb[:, e_i * 8 + kb:e_i * 8 + kb + 1],
                                       beta_sb[:, e_i * 8 + kb:e_i * 8 + kb + 1],
                                       ALU.mult, ALU.add),
                                   waits=([(s_nt, sd_nt)] if kb == 0 else []),
                                   incs=([(gc, 1)] if kb == 7 else []))
                        gc_nt = gc.n
                        ph.add("sync",
                               lambda e, e_i=e_i, t0g=t0g: e.dma_start(
                                   out=cwrow[:],
                                   in_=cwT_sb[e_i:e_i + 1, t0g:t0g + TSG]),
                               waits=[ss], incs=[(s_cw, 16)])
                        sd_cw = s_cw.n
                        for tc in range(4):
                            ph.add("tensor",
                                   lambda e, tc=tc: e.matmul(
                                       pB[:, tc * 512:(tc + 1) * 512], onesb[0:1, :],
                                       cwrow[0:1, tc * 512:(tc + 1) * 512],
                                       start=True, stop=True),
                                   waits=([(s_cw, sd_cw), ss, sv] if tc == 0 else []),
                                   incs=([(st, 1)] if tc == 3 else []))
                        ph.add("scalar", lambda e: e.copy(cwrep[:], pB[:]),
                               waits=[st, (gc, gc.n)], incs=[(ss, 1)])
                        ss_cw = ss.n
                        hpp_done = {-2: ss_cw, -1: ss_cw}
                        hm_done = {}
                        for fidx in range(16):
                            fb, ft = fidx // 4, fidx % 4
                            fcol = fb * 512 + ft * 128
                            fi = fh * 16 + fb * 4 + ft
                            pp = pA if fidx % 2 == 0 else pB
                            for kb in range(8):
                                for tc4 in range(4):
                                    w = []
                                    if kb == 0 and tc4 == 0:
                                        w = [(gc, gc_nt), (w_w1, gd_w1r[ui]),
                                             (ss, hpp_done[fidx - 2])]
                                    ph.add("tensor",
                                           lambda e, kb=kb, fcol=fcol, tc4=tc4, pp=pp: e.matmul(
                                               pp[:, tc4 * 512:(tc4 + 1) * 512],
                                               w1h[:, kb * 2048 + fcol: kb * 2048 + fcol + 128],
                                               ntb[:, kb * TSG + tc4 * 512: kb * TSG + (tc4 + 1) * 512],
                                               start=(kb == 0), stop=(kb == 7)),
                                           waits=w,
                                           incs=([(st, 1)] if (kb == 7 and tc4 == 3) else []))
                            for tc4 in range(4):
                                ph.add("scalar",
                                       lambda e, fidx=fidx, fi=fi, e_i=e_i, tc4=tc4, pp=pp: e.activation(
                                           hbig[:, fidx * TSG + tc4 * 512:
                                                fidx * TSG + (tc4 + 1) * 512],
                                           pp[:, tc4 * 512:(tc4 + 1) * 512], AF.Relu,
                                           bias=b1_sb[:, e_i * 32 + fi:e_i * 32 + fi + 1]),
                                       waits=([st] if tc4 == 0 else []),
                                       incs=([(ss, 1)] if tc4 == 3 else []))
                            hpp_done[fidx] = ss.n
                            ph.add("gpsimd",
                                   lambda e, fidx=fidx: e.tensor_mul(
                                       hbig[:, fidx * TSG:(fidx + 1) * TSG],
                                       hbig[:, fidx * TSG:(fidx + 1) * TSG],
                                       cwrep[:]),
                                   waits=[(ss, hpp_done[fidx])],
                                   incs=[(gc, 1)])
                            hm_done[fidx] = gc.n
                        if tsg == NTSG - 1 and ui + 1 < len(units):
                            gd_w1r[ui + 1] = load_w1(*units[ui + 1], wts=[st])
                        copy_done = {-2: 0, -1: 0}
                        gadd_done = {-2: 0, -1: 0}
                        for dt in range(8):
                            pp = pB if dt % 2 == 0 else pA
                            yb = ydt[dt % 2]
                            yt = ytmp1
                            wdep = max(yout_sd.get((dt, t0g), seed_yio),
                                       yout_sd.get(("buf", dt % 2), 0))
                            ph.add("sync",
                                   lambda e, dt=dt, t0g=t0g, yb=yb: e.dma_start(
                                       out=yb[:],
                                       in_=ymoe_dram[dt * 128:(dt + 1) * 128, t0g:t0g + TSG]),
                                   waits=[(s_yio, wdep)], incs=[(s_yio, 16)])
                            sd_y = s_yio.n
                            for ftile in range(16):
                                for tc4 in range(4):
                                    w = []
                                    if ftile == 0 and tc4 == 0:
                                        w = [(gc, hm_done[15]),
                                             (ss, copy_done[dt - 2]),
                                             (w_w2, gd_w2r[ui])]
                                    ph.add("tensor",
                                           lambda e, dt=dt, ftile=ftile, tc4=tc4, pp=pp: e.matmul(
                                               pp[:, tc4 * 512:(tc4 + 1) * 512],
                                               w2h[:, ftile * 1024 + dt * 128: ftile * 1024 + dt * 128 + 128],
                                               hbig[:, ftile * TSG + tc4 * 512: ftile * TSG + (tc4 + 1) * 512],
                                               start=(ftile == 0), stop=(ftile == 15)),
                                           waits=w,
                                           incs=([(st, 1)] if (ftile == 15 and tc4 == 3) else []))
                            ph.add("scalar",
                                   lambda e, pp=pp, yt=yt: e.copy(yt[:], pp[:]),
                                   waits=[st, (gc, gadd_done[dt - 1])],
                                   incs=[(ss, 1)])
                            copy_done[dt] = ss.n
                            ph.add("gpsimd",
                                   lambda e, yb=yb, yt=yt: e.tensor_add(yb[:], yb[:], yt[:]),
                                   waits=[(ss, copy_done[dt]), (s_yio, sd_y)],
                                   incs=[(gc, 1)])
                            gadd_done[dt] = gc.n
                            ph.add("sync",
                                   lambda e, dt=dt, t0g=t0g, yb=yb: e.dma_start(
                                       out=ymoe_dram[dt * 128:(dt + 1) * 128, t0g:t0g + TSG],
                                       in_=yb[:]),
                                   waits=[(gc, gadd_done[dt])], incs=[(s_yio, 16)])
                            yout_sd[(dt, t0g)] = s_yio.n
                            yout_sd[("buf", dt % 2)] = s_yio.n
                        if tsg == NTSG - 1 and ui + 1 < len(units):
                            gd_w2r[ui + 1] = load_w2(*units[ui + 1], wts=[st])
                ph.emit(nc)

        esE = ExitStack()
        with esE:
            ydgrp = sb(esE, "ydgrp", [128, 8 * 512], F32)
            ytl = sb(esE, "ytl", [128, D], F32)
            vtl = [sb(esE, f"vtl{i}", [128, D], F32) for i in range(2)]
            x1t = [sb(esE, f"x1t{i}", [128, D], F32) for i in range(2)]
            fyw = sb(esE, "fyw", [128, D], F32)
            sqscr = sb(esE, "sqscrE", [128, D], F32)
            otl = sb(esE, "otl", [128, D], F32)
            ptrE = ps(esE, "ptrE", [128, 1024], F32)

            ph = Phase()
            for tt in range(TH // 128):
                row = tt * 128
                bfi = tt % 2
                if tt % 4 == 0:
                    ph.add("sync",
                           lambda e, row=row: e.dma_start(
                               out=ydgrp[:].rearrange("p (a t) -> p a t", a=8),
                               in_=ymoe_dram.rearrange("(a p) t -> p a t", p=128)
                                   [:, :, row:row + 512]),
                           waits=bw(), incs=[(sd, 16)])
                ph.add("sync",
                       lambda e, row=row, bfi=bfi: e.dma_start(
                           out=vtl[bfi][:], in_=v_dram[row:row + 128, :]),
                       waits=([sv] if tt >= 2 else []), incs=[(sd, 16)])
                ph.add("sync",
                       lambda e, row=row, bfi=bfi: e.dma_start(
                           out=x1t[bfi][:], in_=x1_dram[row:row + 128, :]),
                       incs=[(sd, 16)])
                for a in range(8):
                    ph.add("tensor",
                           lambda e, a=a, tt=tt: e.transpose(
                               ptrE[:, a * 128:(a + 1) * 128],
                               ydgrp[:, a * 512 + (tt % 4) * 128: a * 512 + (tt % 4) * 128 + 128],
                               eyef[:]),
                           waits=([sd, ss] if a == 0 else []),
                           incs=([(st, 1)] if a == 7 else []))
                ph.add("scalar", lambda e: e.copy(ytl[:], ptrE[:]), waits=[st],
                       incs=[(ss, 1)])
                ph.add("vector",
                       lambda e, bfi=bfi: e.tensor_add(fyw[:], vtl[bfi][:], ytl[:]),
                       waits=[sd, ss, ob], incs=[(sv, 1)])
                emit_stats(ph, fyw, sqscr, 0, 4, cv[:, 0:1])
                ph.add("vector",
                       lambda e: e.tensor_scalar(fyw[:], fyw[:], stat[:, 0:1],
                                                 stat[:, 4:5], ALU.add, ALU.mult),
                       waits=[(vt, vt.n), (sce, sce.n)], incs=[(sv, 1)])
                ph.add("vector",
                       lambda e, bfi=bfi: e.tensor_add(otl[:], x1t[bfi][:], fyw[:]),
                       incs=[(sv, 1)])
                ph.add("sync",
                       lambda e, row=row: e.dma_start(out=out_d[row:row + 128, :],
                                                      in_=otl[:]),
                       waits=[sv], incs=[(ob, 16)])
            for (o, t) in dbg_copies:
                ph.add("sync", lambda e, o=o, t=t: e.dma_start(out=o, in_=t),
                       waits=bw(), incs=[(ob, 16)])
            ph.add("sync", lambda e: e.nop(), waits=[(ob, ob.n)])
            ph.emit(nc)

    return nc


_NC_CACHE = {}


def _get_nc(debug=False):
    key = bool(debug)
    if key not in _NC_CACHE:
        _NC_CACHE[key] = build_program(debug=key)
    return _NC_CACHE[key]


def _stage(inputs):
    f32 = np.float32
    x = np.asarray(inputs["L0_attn_x"], f32)
    nw = np.asarray(inputs["L0_attn_norm_w"], f32)
    Wx = np.asarray(inputs["L0_attn_Wx"], f32)
    bx = np.asarray(inputs["L0_attn_bx"], f32)
    Wz = np.asarray(inputs["L0_attn_Wz"], f32)
    bz = np.asarray(inputs["L0_attn_bz"], f32)
    convw = np.asarray(inputs["L0_attn_conv_w"], f32).reshape(DI, 4)
    convb = np.asarray(inputs["L0_attn_conv_b"], f32)
    Wxp = np.asarray(inputs["L0_attn_Wxp"], f32)
    Wdt = np.asarray(inputs["L0_attn_Wdt"], f32)
    bdt = np.asarray(inputs["L0_attn_bdt"], f32)
    Alog = np.asarray(inputs["L0_attn_A_log"], f32)
    Dp = np.asarray(inputs["L0_attn_D"], f32)
    Wout = np.asarray(inputs["L0_attn_Wout"], f32)
    bout = np.asarray(inputs["L0_attn_bout"], f32)
    Wg = np.asarray(inputs["L0_ffn_Wg"], f32)
    gamma = np.asarray(inputs["L0_ffn_gamma"], f32)
    beta = np.asarray(inputs["L0_ffn_beta"], f32)
    W1 = np.ascontiguousarray(np.asarray(inputs["L0_ffn_W1"], f32))
    b1 = np.asarray(inputs["L0_ffn_b1"], f32)
    W2 = np.ascontiguousarray(np.asarray(inputs["L0_ffn_W2"], f32))
    b2 = np.ascontiguousarray(np.asarray(inputs["L0_ffn_b2"], f32))

    eye = np.eye(128, dtype=f32)
    constv = np.ascontiguousarray(
        np.tile(np.array([EPS_A, EPS_F, 1.0, -1.0, -0.5, 1.0 / D, 0.0, 0.0],
                         dtype=f32), (128, 1)))
    wg_s = np.ascontiguousarray(Wg.reshape(8, 128, E).transpose(1, 0, 2).reshape(128, 8 * E))
    gamma_s = np.ascontiguousarray(gamma.reshape(E, 8, 128).transpose(2, 0, 1).reshape(128, E * 8))
    beta_s = np.ascontiguousarray(beta.reshape(E, 8, 128).transpose(2, 0, 1).reshape(128, E * 8))
    b1_s = np.ascontiguousarray(b1.reshape(E, 32, 128).transpose(2, 0, 1).reshape(128, E * 32))

    def chparam(vec, p):
        hh = vec[p * CH:(p + 1) * CH]
        return np.ascontiguousarray(hh.reshape(NJ, 128).T)

    per_par = {}
    for p in range(2):
        chs = slice(p * CH, (p + 1) * CH)
        per_par[p] = {
            "wxt_w": np.ascontiguousarray(Wx[chs, :].T),
            "wzt_w": np.ascontiguousarray(Wz[chs, :].T),
            "bx": chparam(bx, p),
            "bz": chparam(bz, p),
            "convw": np.ascontiguousarray(
                convw[chs].reshape(NJ, 128, 4).transpose(1, 0, 2).reshape(128, NJ * 4)),
            "convb": chparam(convb, p),
            "wxpt_w": np.ascontiguousarray(Wxp[:, chs].T),
            "wdtt_w": np.ascontiguousarray(Wdt[chs, :].T),
            "bdt": chparam(bdt, p),
            "alog": np.ascontiguousarray(
                Alog[chs].reshape(NJ, 128, N16).transpose(1, 0, 2).reshape(128, NJ * N16)),
            "dp": chparam(Dp, p),
            "woutt_w": np.ascontiguousarray(Wout[:, chs].T),
        }

    in_maps = []
    for c in range(8):
        b, p = c // 2, c % 2
        im = {
            "x_full": np.ascontiguousarray(x[b]),
            "x_half": np.ascontiguousarray(x[b, p * TH:(p + 1) * TH]),
            "eye_f": eye,
            "constv": constv,
            "norm_w": nw,
            "bout": bout,
            "wg": wg_s,
            "gamma": gamma_s,
            "beta": beta_s,
            "w1": W1,
            "b1": b1_s,
            "w2": W2,
            "b2": b2,
        }
        im.update(per_par[p])
        in_maps.append(im)
    return in_maps


def kernel(**inputs):
    nc = _get_nc(debug=False)
    in_maps = _stage(inputs)
    res = run_bass_kernel_spmd(nc, in_maps, core_ids=list(range(8)))
    out = np.empty((Bq, L, D), np.float32)
    for c in range(8):
        b, p = c // 2, c % 2
        out[b, p * TH:(p + 1) * TH] = res.results[c]["out"]
    return out
